# version 11
# baseline (speedup 1.0000x reference)
import math
from contextlib import ExitStack

import numpy as np
import concourse.bass as bass
import concourse.mybir as mybir
from concourse.bass_utils import run_bass_kernel_spmd

F32 = mybir.dt.float32
BF16 = mybir.dt.bfloat16
AF = mybir.ActivationFunctionType
ALU = mybir.AluOpType
AX = mybir.AxisListType

NCORE = 8
D = 1024
TOK = 16384
TPC = TOK // NCORE

ENG = ("sync", "tensor", "vector", "scalar", "gpsimd")


class P:

    SERIALIZE = False
    SER_ENG = None
    FORCE_SLOW = False

    def __init__(self, nc, stack):
        self.nc = nc
        self.stack = stack
        self.sems = {e: stack.enter_context(nc.semaphore(f"prog_{e}")) for e in ENG}
        self.items = {e: [] for e in ENG}
        self.cum = {e: 0 for e in ENG}

    def _chan(self, ch):
        key = ("ch", ch)
        if key not in self.sems:
            self.sems[key] = self.stack.enter_context(self.nc.semaphore(f"ch_{ch}"))
            self.cum[key] = 0
        return key

    SER_SCOPE = None

    def emit(self, eng, fn, deps=(), dma=False, ch=None, slow=None):
        deps = list(deps)
        if P.SERIALIZE and getattr(self, "_last", None):
            deps.append(self._last)
        if P.SER_SCOPE and getattr(self, "scope", None) == P.SER_SCOPE and getattr(self, "_last", None):
            deps.append(self._last)
        if P.SER_ENG and getattr(self, "_last", None):
            if eng == P.SER_ENG or (self._last and self._last[0] == P.SER_ENG):
                deps.append(self._last)
        m = {}
        for d in deps:
            if d is None:
                continue
            k2, c2, was_dma = d[0], d[1], d[2]
            fast = d[3] if len(d) > 3 else False
            if k2 == eng and not was_dma and fast and not P.FORCE_SLOW:
                continue
            if m.get(k2, 0) < c2:
                m[k2] = c2
        if dma:
            assert ch is not None, "DMA emits need a channel"
            key = self._chan(ch)
            self.cum[key] += 16
            tok = (key, self.cum[key], True, False)
            self.items[eng].append((fn, m, 16, key))
        else:
            self.cum[eng] += 1
            if slow is None:
                slow = eng != "tensor"
            tok = (eng, self.cum[eng], False, not slow)
            self.items[eng].append((fn, m, 1, eng))
        self._last = tok
        return tok

    def realize(self, block):
        for e in ENG:
            items = self.items[e]
            if not items:
                continue

            def body(eng, items=items, e=e):
                water = {}
                for fn, m, inc, semkey in items:
                    for k2, c2 in m.items():
                        if water.get(k2, 0) < c2:
                            eng.wait_ge(self.sems[k2], c2)
                            water[k2] = c2
                    inst = fn(eng)
                    inst.then_inc(self.sems[semkey], inc)

            getattr(block, e)(body)


class Banks:

    def __init__(self):
        self.last = {}

    def war(self, key):
        t = self.last.get(key)
        return [t] if t is not None else []

    def read(self, key, tok):
        self.last[key] = tok


def make_identity(p, tile, sq, val=1.0):
    p.emit("gpsimd", lambda g: g.memset(tile, 0.0))
    return p.emit("gpsimd", lambda g: g.affine_select(
        out=tile, in_=tile, compare_op=ALU.not_equal, fill=val, base=0,
        pattern=[[-1, sq]], channel_multiplier=1))


def make_tri(p, tile, sq, val, kind, deps=()):
    cmp = {"low_incl": ALU.is_ge, "low_strict": ALU.is_gt,
           "up_incl": ALU.is_le, "up_strict": ALU.is_lt}[kind]
    p.emit("gpsimd", lambda g: g.memset(tile, val), deps=deps)
    return p.emit("gpsimd", lambda g: g.affine_select(
        out=tile, in_=tile, compare_op=cmp, fill=0.0, base=0,
        pattern=[[-1, sq]], channel_multiplier=1))


def build_ffn0(dbg=0, nexpert=9):
    SEG = 1024
    NSEG = TPC // SEG
    TS = 512
    NTS = SEG // TS
    FF = 4096
    NE = nexpert
    NKC = D // 128
    NFC = FF // 128
    GRP = 8

    nc = bass.Bass("TRN2", target_bir_lowering=False)
    if dbg:
        xn_h = nc.dram_tensor("dbg_xn", [D, TPC], BF16, kind="ExternalOutput").ap()
        g_h = nc.dram_tensor("dbg_gates", [128, 8 * (TPC // 128)], F32, kind="ExternalOutput").ap()
    if dbg >= 2:
        h_h = nc.dram_tensor("dbg_h", [128, 1024], BF16, kind="ExternalOutput").ap()
    if dbg >= 3:
        st_h = nc.dram_tensor("dbg_stats", [8, 2048], F32, kind="ExternalOutput").ap()
    if dbg:
        sc_h = nc.dram_tensor("dbg_scores", [128, 8 * (TPC // 128)], F32, kind="ExternalOutput").ap()
    if dbg >= 4:
        a1_h = nc.dram_tensor("dbg_acc_ln", [D, TPC], F32, kind="ExternalOutput").ap()
        a2_h = nc.dram_tensor("dbg_acc_exp", [D, TPC], F32, kind="ExternalOutput").ap()
    x_h = nc.dram_tensor("x", [D, TPC], F32, kind="ExternalInput").ap()
    su_h = nc.dram_tensor("shared_up", [D, FF], F32, kind="ExternalInput").ap()
    sd_h = nc.dram_tensor("shared_down", [FF, D], F32, kind="ExternalInput").ap()
    ru_h = nc.dram_tensor("routed_up", [8, D, FF], F32, kind="ExternalInput").ap()
    rd_h = nc.dram_tensor("routed_down", [8, FF, D], F32, kind="ExternalInput").ap()
    rw_h = nc.dram_tensor("router_w", [D, 8], F32, kind="ExternalInput").ap()
    xo_h = nc.dram_tensor("xo", [D, TPC], F32, kind="ExternalOutput").ap()

    with ExitStack() as st:
        p = P(nc, st)
        bk = Banks()
        sb = lambda name, shape, dt=F32: st.enter_context(nc.sbuf_tensor(name, shape, dt))

        xn = [sb(f"xn{k}", [128, SEG], BF16) for k in range(NKC)]
        acc = [sb(f"acc{k}", [128, SEG], F32) for k in range(NKC)]
        w1b = [sb(f"w1b{k}", [128, FF], BF16) for k in range(NKC)]
        w2g = [sb(f"w2g{j}", [128, D], BF16) for j in range(2 * GRP)]
        hgrp = [sb(f"h{g}", [128, SEG], BF16) for g in range(GRP)]
        xt2 = [sb(f"xt{i}", [128, TS], F32) for i in range(2)]
        xt = xt2[0]
        xbt2 = [sb(f"xbt{i}", [128, TS], BF16) for i in range(2)]
        xbt = xbt2[0]
        sqt2 = [sb(f"sqt{i}", [128, TS], BF16) for i in range(2)]
        sqt = sqt2[0]
        tmp = sb("tmpf", [128, TS], F32)
        ones_bf = sb("ones_bf", [128, 1], BF16)
        ones_f = sb("ones_f", [1, 128], F32)
        ones128 = sb("ones128", [128, 128], F32)
        idn128 = sb("idn128", [128, 128], F32)
        mu = sb("mu", [1, TS], F32)
        rstd = sb("rstd", [1, TS], F32)
        murstd = sb("murstd", [1, TS], F32)
        srow = sb("srow", [1, TS], F32)
        srow2 = sb("srow2", [1, TS], F32)
        rws = sb("rws", [128, 64], F32)
        rwsb = sb("rwsb", [128, 64], BF16)
        gates = sb("gates", [128, 8 * (SEG // 128)], F32)
        colbc = sb("colbc", [128, 128], F32)
        gate_bc = sb("gate_bc", [128, SEG], F32)
        prob = sb("prob", [128, 8], F32)
        pmax = sb("pmax", [128, 1], F32)
        pm2 = sb("pm2", [128, 1], F32)

        pt = lambda name, shape, dt=F32: st.enter_context(nc.psum_tensor(name, shape, dt))
        ps_s1 = pt("ps_s1", [1, TS])
        ps_s2 = pt("ps_s2", [1, TS])
        ps_b1 = pt("ps_b1", [128, TS])
        ps_b2 = pt("ps_b2", [128, TS])
        ps_h = [pt(f"ps_h{i}", [128, TS]) for i in range(2)]
        ps_y = [pt(f"ps_y{i}", [128, TS]) for i in range(2)]

        p.emit("gpsimd", lambda g: g.memset(ones_bf[:, :], 1.0))
        p.emit("gpsimd", lambda g: g.memset(ones_f[:, :], 1.0))
        p.emit("gpsimd", lambda g: g.memset(ones128[:, :], 1.0))
        make_identity(p, idn128[:, :], 128)

        rw_toks = []
        for k in range(NKC):
            ld = p.emit("sync", lambda s, k=k: s.dma_start(
                out=rws[:, k * 8:(k + 1) * 8], in_=rw_h[k * 128:(k + 1) * 128, :]), dma=True, ch="rws")
            rw_toks.append(ld)
        rw_cv = p.emit("vector", lambda v: v.tensor_copy(rwsb[:, :], rws[:, :]), deps=rw_toks)

        last_seg_stores = []
        dbg_toks = []
        for seg in range(NSEG):
            p.scope = "ln"
            xn_done = []
            if seg == 0:
                xtoks = []
            for ts in range(NTS):
                p.scope = "lnstats"
                tsl = slice(ts * TS, (ts + 1) * TS)
                gsl = slice(seg * SEG + ts * TS, seg * SEG + ts * TS + TS)
                s1tok = s2tok = None
                for k in range(NKC):
                    sl2 = k % 2
                    ld = p.emit("sync", lambda s, k=k, gsl=gsl, sl2=sl2: s.dma_start(
                        out=xt2[sl2][:, :], in_=x_h[k * 128:(k + 1) * 128, gsl]),
                        deps=(xtoks if k == 0 else []) + bk.war(("xt", sl2)), dma=True, ch=f"xt{sl2}")
                    cb = p.emit("vector", lambda v, sl2=sl2: v.tensor_copy(xbt2[sl2][:, :], xt2[sl2][:, :]),
                                deps=[ld, s1tok] + bk.war(("xbt", sl2)))
                    sq = p.emit("scalar", lambda s, sl2=sl2: s.activation(
                        out=sqt2[sl2][:, :], in_=xt2[sl2][:, :], func=AF.Square),
                        deps=[ld, s2tok] + bk.war(("sqt", sl2)))
                    bk.read(("xt", sl2), cb)
                    bk.read(("xt", sl2), sq)
                    s1tok = p.emit("tensor", lambda t, k=k, sl2=sl2: t.matmul(
                        ps_s1[:, :], ones_bf[:, :], xbt2[sl2][:, :], start=(k == 0), stop=(k == NKC - 1)),
                        deps=[cb, s1tok] + (bk.war("s1") if k == 0 else []))
                    bk.read(("xbt", sl2), s1tok)
                    s2tok = p.emit("tensor", lambda t, k=k, sl2=sl2: t.matmul(
                        ps_s2[:, :], ones_bf[:, :], sqt2[sl2][:, :], start=(k == 0), stop=(k == NKC - 1)),
                        deps=[sq, s2tok] + (bk.war("s2") if k == 0 else []))
                    bk.read(("sqt", sl2), s2tok)
                    xtoks = [cb, sq]
                p.scope = "lnmid"
                t1 = p.emit("vector", lambda v: v.tensor_scalar_mul(mu[:, :], ps_s1[:, :], 1.0 / D), deps=[s1tok])
                bk.read("s1", t1)
                t2 = p.emit("vector", lambda v: v.tensor_scalar_mul(srow[:, :], ps_s2[:, :], 1.0 / D), deps=[s2tok])
                bk.read("s2", t2)
                t3 = p.emit("vector", lambda v: v.tensor_mul(srow2[:, :], mu[:, :], mu[:, :]), deps=[t1])
                t4 = p.emit("vector", lambda v: v.tensor_sub(srow[:, :], srow[:, :], srow2[:, :]), deps=[t2, t3])
                t4b = p.emit("vector", lambda v: v.tensor_scalar_add(srow[:, :], srow[:, :], 1e-5), deps=[t4])
                t5 = p.emit("scalar", lambda s: s.activation(out=srow[:, :], in_=srow[:, :], func=AF.Sqrt), deps=[t4b])
                t6 = p.emit("vector", lambda v: v.reciprocal(rstd[:, :], srow[:, :]), deps=[t5])
                t7 = p.emit("vector", lambda v: v.tensor_mul(murstd[:, :], mu[:, :], rstd[:, :]), slow=True, deps=[t6])
                if dbg >= 3:
                    dbg_toks.append(p.emit("gpsimd", lambda g, seg=seg, ts=ts: g.dma_start(
                        out=st_h[seg * NTS + ts:seg * NTS + ts + 1, 0:TS], in_=mu[:, :]), deps=[t7], dma=True, ch="dbg"))
                    dbg_toks.append(p.emit("gpsimd", lambda g, seg=seg, ts=ts: g.dma_start(
                        out=st_h[4 + seg * NTS + ts:4 + seg * NTS + ts + 1, 0:TS], in_=rstd[:, :]), deps=[t7], dma=True, ch="dbg"))
                b1 = p.emit("tensor", lambda t: t.matmul(ps_b1[:, :], ones_f[:, :], rstd[:, :], start=True, stop=True),
                            deps=[t6] + bk.war("b1"))
                b2 = p.emit("tensor", lambda t: t.matmul(ps_b2[:, :], ones_f[:, :], murstd[:, :], start=True, stop=True),
                            deps=[t7] + bk.war("b2"))
                p.scope = "lnnorm"
                for k in range(NKC):
                    sl2 = k % 2
                    ld = p.emit("sync", lambda s, k=k, gsl=gsl, sl2=sl2: s.dma_start(
                        out=xt2[sl2][:, :], in_=x_h[k * 128:(k + 1) * 128, gsl]),
                        deps=(xtoks if k == 0 else []) + bk.war(("xt", sl2)), dma=True, ch=f"xt{sl2}")
                    m1 = p.emit("vector", lambda v, sl2=sl2: v.tensor_mul(tmp[:, :], xt2[sl2][:, :], ps_b1[:, :]), deps=[ld, b1])
                    bk.read(("xt", sl2), m1)
                    m2 = p.emit("vector", lambda v, k=k, tsl=tsl: v.tensor_sub(acc[k][:, tsl], tmp[:, :], ps_b2[:, :]), deps=[m1, b2])
                    m3 = p.emit("vector", lambda v, k=k, tsl=tsl: v.tensor_copy(xn[k][:, tsl], acc[k][:, tsl]), deps=[m2])
                    xtoks = [m1, m2, m3]
                    xn_done.append(m3)
                bk.read("b1", xtoks[0])
                bk.read("b2", xtoks[1])

            if dbg >= 4:
                for k in range(NKC):
                    dbg_toks.append(p.emit("gpsimd", lambda g, k=k, seg=seg: g.dma_start(
                        out=a1_h[k * 128:(k + 1) * 128, seg * SEG:(seg + 1) * SEG], in_=acc[k][:, :]),
                        deps=xn_done, dma=True, ch="dbg"))
            p.scope = "router"
            gate_done = []
            for tt in range(SEG // 128):
                ttl = slice(ts * 0 + tt * 128, tt * 128 + 128)
                mm = None
                for k in range(NKC):
                    mm = p.emit("tensor", lambda t, k=k, ttl=ttl: t.matmul(
                        ps_b1[0:128, 0:8], xn[k][:, ttl], rwsb[:, k * 8:(k + 1) * 8],
                        start=(k == 0), stop=(k == NKC - 1)),
                        deps=[rw_cv, mm] + xn_done + (bk.war("b1") if k == 0 else []))
                if dbg:
                    sccp = p.emit("vector", lambda v: v.tensor_copy(srow2[:, 0:8].tensor[0:128, 0:8] if False else colbc[:, 0:8], ps_b1[0:128, 0:8]), deps=[mm] + bk.war("colbc"))
                    bk.read("colbc", sccp)
                    drsc = p.emit("vector", lambda v: v.drain(), deps=[sccp])
                    dbg_toks.append(p.emit("gpsimd", lambda g, seg=seg, tt=tt: g.dma_start(
                        out=sc_h[:, (seg * (SEG // 128) + tt) * 8:(seg * (SEG // 128) + tt) * 8 + 8], in_=colbc[:, 0:8]),
                        deps=[drsc], dma=True, ch="dbg"))
                    bk.read("colbc", dbg_toks[-1])
                a0 = p.emit("vector", lambda v: v.tensor_copy(prob[:, :], ps_b1[0:128, 0:8]), deps=[mm])
                bk.read("b1", a0)
                a1 = p.emit("vector", lambda v: v.reduce_max(pmax[:, :], prob[:, :], axis=AX.X), deps=[a0])
                a2 = p.emit("vector", lambda v: v.tensor_scalar(prob[:, :], prob[:, :], pmax[:, 0:1], None, ALU.subtract), slow=True, deps=[a1])
                a3 = p.emit("scalar", lambda s: s.activation(out=prob[:, :], in_=prob[:, :], func=AF.Exp), deps=[a2])
                a4 = p.emit("vector", lambda v: v.reduce_sum(pmax[:, :], prob[:, :], axis=AX.X), deps=[a3])
                a5 = p.emit("vector", lambda v: v.reciprocal(pmax[:, :], pmax[:, :]), slow=True, deps=[a4])
                a6 = p.emit("vector", lambda v: v.tensor_scalar(prob[:, :], prob[:, :], pmax[:, 0:1], None, ALU.mult), deps=[a5])
                a7 = p.emit("vector", lambda v: v.reduce_max(pmax[:, :], prob[:, :], axis=AX.X), deps=[a6])
                a8 = p.emit("vector", lambda v: v.tensor_scalar(tmp[:, 0:8], prob[:, :], pmax[:, 0:1], None, ALU.is_lt), slow=True, deps=[a7])
                a9 = p.emit("vector", lambda v: v.tensor_mul(tmp[:, 0:8], prob[:, :], tmp[:, 0:8]), deps=[a8])
                a10 = p.emit("vector", lambda v: v.reduce_max(pm2[:, :], tmp[:, 0:8], axis=AX.X), deps=[a9])
                a11 = p.emit("vector", lambda v: v.tensor_scalar(tmp[:, 0:8], prob[:, :], pm2[:, 0:1], None, ALU.is_ge), slow=True, deps=[a10])
                a12 = p.emit("vector", lambda v, tt=tt: v.tensor_tensor(
                    gates[:, tt * 8:(tt + 1) * 8], prob[:, :], tmp[:, 0:8], ALU.mult), deps=[a11], slow=True)
                gate_done.append(a12)
            if dbg:
                for k in range(NKC):
                    dbg_toks.append(p.emit("gpsimd", lambda g, k=k, seg=seg: g.dma_start(
                        out=xn_h[k * 128:(k + 1) * 128, seg * SEG:(seg + 1) * SEG], in_=xn[k][:, :]),
                        deps=xn_done, dma=True, ch="dbg"))
                drg = p.emit("vector", lambda v: v.drain(), deps=gate_done)
                dbg_toks.append(p.emit("gpsimd", lambda g, seg=seg: g.dma_start(
                    out=g_h[:, seg * (8 * SEG // 128):(seg + 1) * (8 * SEG // 128)], in_=gates[:, :]),
                    deps=[drg], dma=True, ch="dbg"))

            p.scope = "wload"
            expert_tail = list(gate_done)
            w1_ready = {}
            up_done_prev = list(gate_done)
            for e in range(NE):
                w1src = su_h if e == 0 else ru_h[e - 1]
                w2src = sd_h if e == 0 else rd_h[e - 1]
                if (seg, e) in w1_ready:
                    wtoks = w1_ready.pop((seg, e))
                else:
                    wtoks = []
                    for k in range(NKC):
                        wtoks.append(p.emit("gpsimd", lambda g, k=k, w1src=w1src: g.dma_start(
                            out=w1b[k][:, :], in_=w1src[k * 128:(k + 1) * 128, :]),
                            deps=up_done_prev + bk.war(("w1b", k)), dma=True, ch=f"w1c{k}"))
                gb_done = []
                if e > 0:
                    c2 = None
                    for tt in range(SEG // 128):
                        c1 = p.emit("vector", lambda v, tt=tt, e=e: v.tensor_scalar(
                            colbc[:, :], ones128[:, :], gates[:, tt * 8 + (e - 1):tt * 8 + e], None, ALU.mult),
                            deps=gate_done + bk.war("colbc"))
                        c2 = p.emit("tensor", lambda t: t.matmul(
                            ps_b2[:, 0:128], colbc[:, :], idn128[:, :], start=True, stop=True),
                            deps=[c1] + bk.war("b2"))
                        bk.read("colbc", c2)
                        c3 = p.emit("vector", lambda v, tt=tt: v.tensor_copy(
                            gate_bc[:, tt * 128:(tt + 1) * 128], ps_b2[:, 0:128]), deps=[c2])
                        bk.read("b2", c3)
                        gb_done.append(c3)
                new_tail = []
                up_done = []
                for fg in range(NFC // GRP):
                    p.scope = "wload"
                    w2toks = []
                    for j in range(GRP):
                        f = fg * GRP + j
                        slot = (fg % 2) * GRP + j
                        w2toks.append(p.emit("gpsimd", lambda g, f=f, w2src=w2src, slot=slot: g.dma_start(
                            out=w2g[slot][:, :], in_=w2src[f * 128:(f + 1) * 128, :]),
                            deps=bk.war(("w2", slot)), dma=True, ch=f"w2c{slot}"))
                    htoks = {}
                    p.scope = "up"
                    for j in range(GRP):
                        f = fg * GRP + j
                        for ts_ in range(NTS):
                            tsl = slice(ts_ * TS, (ts_ + 1) * TS)
                            hb = ps_h[(j * NTS + ts_) % 2]
                            hkey = ("h", (j * NTS + ts_) % 2)
                            mmt = None
                            for k in range(NKC):
                                mmt = p.emit("tensor", lambda t, k=k, f=f, tsl=tsl, hb=hb: t.matmul(
                                    hb[:, :], w1b[k][:, f * 128:(f + 1) * 128], xn[k][:, tsl],
                                    start=(k == 0), stop=(k == NKC - 1)),
                                    deps=[wtoks[k], mmt] + (bk.war(hkey) if k == 0 else []))
                            for k in range(NKC):
                                bk.read(("w1b", k), mmt)
                            up_done = [mmt]
                            ge = p.emit("scalar", lambda s, j=j, tsl=tsl, hb=hb: s.activation(
                                out=hgrp[j][:, tsl], in_=hb[:, :], func=AF.Gelu),
                                deps=[mmt] + bk.war(("hg", j, ts_)))
                            bk.read(hkey, ge)
                            htoks[(j, ts_)] = ge
                            if dbg >= 2 and seg == 0 and e == 0 and fg == 0 and j == 0:
                                dbg_toks.append(p.emit("gpsimd", lambda g, tsl=tsl: g.dma_start(
                                    out=h_h[:, tsl], in_=hgrp[0][:, tsl]), deps=[ge], dma=True, ch="dbg"))
                    p.scope = "down"
                    for c in range(NKC):
                        for ts_ in range(NTS):
                            tsl = slice(ts_ * TS, (ts_ + 1) * TS)
                            yb = ps_y[(c * NTS + ts_) % 2]
                            ykey = ("y", (c * NTS + ts_) % 2)
                            mmt = None
                            for j in range(GRP):
                                slot = (fg % 2) * GRP + j
                                mmt = p.emit("tensor", lambda t, j=j, c=c, tsl=tsl, yb=yb, slot=slot: t.matmul(
                                    yb[:, :], w2g[slot][:, c * 128:(c + 1) * 128], hgrp[j][:, tsl],
                                    start=(j == 0), stop=(j == GRP - 1)),
                                    deps=[w2toks[j], htoks[(j, ts_)], mmt] + (bk.war(ykey) if j == 0 else []))
                            if e == 0:
                                fl = p.emit("vector", lambda v, c=c, tsl=tsl, yb=yb: v.tensor_add(
                                    acc[c][:, tsl], acc[c][:, tsl], yb[:, :]), deps=[mmt] + xn_done)
                            else:
                                t_ = p.emit("vector", lambda v, tsl=tsl, yb=yb: v.tensor_mul(
                                    tmp[:, :], yb[:, :], gate_bc[:, tsl]), deps=[mmt] + gb_done)
                                fl = p.emit("vector", lambda v, c=c, tsl=tsl: v.tensor_add(
                                    acc[c][:, tsl], acc[c][:, tsl], tmp[:, :]), deps=[t_])
                            bk.read(ykey, fl)
                            for j in range(GRP):
                                bk.read(("w2", (fg % 2) * GRP + j), mmt)
                                bk.read(("hg", j, ts_), mmt)
                            new_tail.append(fl)
                up_done_prev = up_done
                if e + 1 < NE:
                    nsrc = ru_h[e]
                    toks = []
                    for k in range(NKC):
                        toks.append(p.emit("gpsimd", lambda g, k=k, nsrc=nsrc: g.dma_start(
                            out=w1b[k][:, :], in_=nsrc[k * 128:(k + 1) * 128, :]),
                            deps=up_done + bk.war(("w1b", k)), dma=True, ch=f"w1c{k}"))
                    w1_ready[(seg, e + 1)] = toks
                expert_tail = new_tail[-NKC * NTS:]

            if dbg >= 4:
                for k in range(NKC):
                    dbg_toks.append(p.emit("gpsimd", lambda g, k=k, seg=seg: g.dma_start(
                        out=a2_h[k * 128:(k + 1) * 128, seg * SEG:(seg + 1) * SEG], in_=acc[k][:, :]),
                        deps=expert_tail, dma=True, ch="dbg"))
            p.scope = "out"
            stoks = []
            ldprev = []
            for k in range(NKC):
                for ts_ in range(NTS):
                    tsl = slice(ts_ * TS, (ts_ + 1) * TS)
                    gsl = slice(seg * SEG + ts_ * TS, seg * SEG + ts_ * TS + TS)
                    ld = p.emit("sync", lambda s, k=k, gsl=gsl: s.dma_start(
                        out=xt2[0][:, :], in_=x_h[k * 128:(k + 1) * 128, gsl]),
                        deps=expert_tail + ldprev + bk.war(("xt", 0)), dma=True, ch="xt0")
                    f1 = p.emit("vector", lambda v, k=k, tsl=tsl: v.scalar_tensor_tensor(
                        acc[k][:, tsl], acc[k][:, tsl], 0.5, xt2[0][:, :], ALU.mult, ALU.add),
                        deps=[ld] + expert_tail + (dbg_toks[-NKC:] if dbg >= 4 else []))
                    ldprev = [f1]
                    bk.read(("xt", 0), f1)
                    dr = p.emit("vector", lambda v: v.drain(), deps=[f1])
                    stoks.append(p.emit("gpsimd", lambda g, k=k, tsl=tsl, gsl=gsl: g.dma_start(
                        out=xo_h[k * 128:(k + 1) * 128, gsl], in_=acc[k][:, tsl]), deps=[dr], dma=True, ch="st"))
            last_seg_stores = stoks
            xtoks = stoks + [ldprev[0]]

        p.emit("gpsimd", lambda g: g.engine_nop(), deps=last_seg_stores + dbg_toks)

        with nc.Block() as block:
            p.realize(block)
    return nc


def _shard_fm(xT):
    return [np.ascontiguousarray(xT[:, c * TPC:(c + 1) * TPC]) for c in range(NCORE)]


_cache = {}


def _get(name, fn):
    if name not in _cache:
        _cache[name] = fn()
    return _cache[name]


def run_ffn0(xT, I):
    nc = _get("ffn0", build_ffn0)
    shards = _shard_fm(xT)
    base = {
        "shared_up": np.ascontiguousarray(I["L0_ffn_shared_up"][0]),
        "shared_down": np.ascontiguousarray(I["L0_ffn_shared_down"][0]),
        "routed_up": np.ascontiguousarray(I["L0_ffn_routed_up"]),
        "routed_down": np.ascontiguousarray(I["L0_ffn_routed_down"]),
        "router_w": np.ascontiguousarray(I["L0_ffn_router_w"]),
    }
    in_maps = [dict(base, x=s) for s in shards]
    res = run_bass_kernel_spmd(nc, in_maps, core_ids=list(range(NCORE)))
    return np.concatenate([r["xo"] for r in res.results], axis=1)


def kernel(**inputs):
    I = inputs
    _ = (I.get("L0_attn_keys"), I.get("L0_attn_values"), I.get("L0_attn_attn_mask"))
    x = np.asarray(I["L0_attn_queries"], dtype=np.float32).reshape(TOK, D)
    xT = np.ascontiguousarray(x.T)
    xT = run_ffn0(xT, I)
    streams = run_dn_proj(xT, I)
    scan_in = {"w": streams["o_w"], "at": streams["o_at"], "kn": streams["o_kn"],
               "rk": streams["o_rk"], "v": streams["o_v"], "r": streams["o_r"]}
    yT = run_dn_scan(scan_in)
    xT = run_dn_post(xT, yT, {"r": streams["o_r"], "rk": streams["o_rk"],
                              "v": streams["o_v"], "gate": streams["o_gate"]}, I)
    xT = run_ffn0(xT, I)
    xT = run_l1_ffn(xT, I)
    qkvT = run_qkv(xT, I)
    yT = run_attn_core(qkvT)
    xT = run_attn_post(xT, yT, I)
    xT = run_l1_ffn(xT, I)
    return np.ascontiguousarray(xT.T).reshape(2, 8192, D)


def build_dn_proj():
    TS = 512
    SEGW = 1024
    NSEG = TPC // SEGW
    NTS = SEGW // TS
    NTS4 = TPC // TS
    NKC = D // 128
    EP = 1280
    NFC = EP // 128
    LR = 128

    nc = bass.Bass("TRN2", target_bir_lowering=False)
    w1_h = {}
    w2_h = {}
    b1_h = {}
    b2_h = {}
    x_h = nc.dram_tensor("x", [D, TPC], F32, kind="ExternalInput").ap()
    for pth in ("r", "k", "v"):
        w1_h[pth] = nc.dram_tensor(f"{pth}W1", [D, EP], F32, kind="ExternalInput").ap()
        b1_h[pth] = nc.dram_tensor(f"{pth}b1", [NFC, 128, 1], F32, kind="ExternalInput").ap()
        w2_h[pth] = nc.dram_tensor(f"{pth}W2", [EP, D], F32, kind="ExternalInput").ap()
        b2_h[pth] = nc.dram_tensor(f"{pth}b2", [NKC, 128, 1], F32, kind="ExternalInput").ap()
    lrA_h = {}
    lrB_h = {}
    lrb_h = {}
    for pth in ("g", "a", "d"):
        lrA_h[pth] = nc.dram_tensor(f"{pth}A", [D, LR], F32, kind="ExternalInput").ap()
        lrB_h[pth] = nc.dram_tensor(f"{pth}B", [LR, D], F32, kind="ExternalInput").ap()
        lrb_h[pth] = nc.dram_tensor(f"{pth}b", [NKC, 128, 1], F32, kind="ExternalInput").ap()
    mu_h = nc.dram_tensor("mus", [6, NKC, 128, 1], F32, kind="ExternalInput").ap()
    rem_h = nc.dram_tensor("rem_mult", [NKC, 128, 1], F32, kind="ExternalInput").ap()
    mix_h = nc.dram_tensor("iclr_mix", [NKC, 128, 1], F32, kind="ExternalInput").ap()
    E_h = nc.dram_tensor("Emat", [128, 2], F32, kind="ExternalInput").ap()
    E2_h = nc.dram_tensor("E2mat", [2, 128], F32, kind="ExternalInput").ap()
    outs = {}
    for o in ("o_r", "o_v", "o_kn", "o_at", "o_rk", "o_gate", "o_w"):
        outs[o] = nc.dram_tensor(o, [D, TPC], F32, kind="ExternalOutput").ap()
    zbf_h = nc.dram_tensor("zbf_scr", [D, TPC], BF16, kind="Internal").ap()
    iclr_h = nc.dram_tensor("iclr_scr", [D, TPC], F32, kind="Internal").ap()

    with ExitStack() as st:
        p = P(nc, st)
        bk = Banks()
        sb = lambda name, shape, dt=F32: st.enter_context(nc.sbuf_tensor(name, shape, dt))
        pt = lambda name, shape, dt=F32: st.enter_context(nc.psum_tensor(name, shape, dt))

        zmu = [sb(f"zmu{k}", [128, SEGW], BF16) for k in range(NKC)]
        hx = [sb(f"hx{f}", [128, SEGW], BF16) for f in range(NFC)]
        w1b = [sb(f"w1b{k}", [128, EP], BF16) for k in range(NKC)]
        w2b = [sb(f"w2b{f}", [128, D], BF16) for f in range(NFC)]
        out = [sb(f"out{k}", [128, SEGW], F32) for k in range(NKC)]
        stage = sb("stage", [128, EP], F32)
        xt = sb("xt", [128, TS], F32)
        xbt = sb("xbt", [128, TS], BF16)
        sqt = sb("sqt", [128, TS], BF16)
        tmp = sb("tmpf", [128, TS], F32)
        tmp2 = sb("tmp2", [128, TS], F32)
        tmp3 = sb("tmp3", [128, TS], F32)
        zbt = sb("zbt", [128, TS], BF16)
        itile = sb("itile", [128, TS], F32)
        ones_bf = sb("ones_bf", [128, 1], BF16)
        ones_f = sb("ones_f", [1, 128], F32)
        mu = sb("mu", [1, TS], F32)
        rstd = sb("rstd", [1, TS], F32)
        murstd = sb("murstd", [1, TS], F32)
        srow = sb("srow", [1, TS], F32)
        srow2 = sb("srow2", [1, TS], F32)
        mus = sb("mus_t", [128, 48], F32)
        remt = sb("remt", [128, 8], F32)
        mixt = sb("mixt", [128, 8], F32)
        b1t = sb("b1t", [128, NFC], F32)
        b2t = sb("b2t", [128, NKC], F32)
        Et = sb("Et", [128, 2], F32)
        E2t = sb("E2t", [2, 128], F32)
        nrow = sb("nrow", [2, TS], F32)

        P0 = pt("P0", [128, TS])
        P1 = pt("P1", [128, TS])
        P2 = pt("P2", [128, TS])
        P3 = pt("P3", [128, TS])
        P4 = pt("P4", [128, TS])
        P5 = pt("P5", [128, TS])
        S0 = pt("S0", [1, TS])
        S1 = pt("S1", [1, TS])

        p.emit("gpsimd", lambda g: g.memset(ones_bf[:, :], 1.0))
        p.emit("gpsimd", lambda g: g.memset(ones_f[:, :], 1.0))
        ctoks = []
        ctoks.append(p.emit("sync", lambda s: s.dma_start(out=Et[:, :], in_=E_h[:, :]), dma=True, ch="c0"))
        ctoks.append(p.emit("sync", lambda s: s.dma_start(out=E2t[:, :], in_=E2_h[:, :]), dma=True, ch="c1"))
        for pi in range(6):
            for k in range(NKC):
                ctoks.append(p.emit("sync", lambda s, pi=pi, k=k: s.dma_start(
                    out=mus[:, pi * 8 + k:pi * 8 + k + 1], in_=mu_h[pi, k]), dma=True, ch="c2"))
        mu_all = list(ctoks)
        cextra = []
        for k in range(NKC):
            cextra.append(p.emit("sync", lambda s, k=k: s.dma_start(
                out=remt[:, k:k + 1], in_=rem_h[k]), dma=True, ch="c3"))
            cextra.append(p.emit("sync", lambda s, k=k: s.dma_start(
                out=mixt[:, k:k + 1], in_=mix_h[k]), dma=True, ch="c4"))
        ctoks += cextra

        xtoks = []
        zst = []
        for ts in range(NTS4):
            gsl = slice(ts * TS, (ts + 1) * TS)
            s1tok = s2tok = None
            for k in range(NKC):
                ld = p.emit("sync", lambda s, k=k, gsl=gsl: s.dma_start(
                    out=xt[:, :], in_=x_h[k * 128:(k + 1) * 128, gsl]), deps=xtoks, dma=True, ch="xt")
                cb = p.emit("vector", lambda v: v.tensor_copy(xbt[:, :], xt[:, :]), deps=[ld, s1tok])
                sq = p.emit("scalar", lambda s: s.activation(out=sqt[:, :], in_=xt[:, :], func=AF.Square), deps=[ld, s2tok])
                s1tok = p.emit("tensor", lambda t, k=k: t.matmul(
                    S0[:, :], ones_bf[:, :], xbt[:, :], start=(k == 0), stop=(k == NKC - 1)),
                    deps=[cb, s1tok] + (bk.war("S0") if k == 0 else []))
                s2tok = p.emit("tensor", lambda t, k=k: t.matmul(
                    S1[:, :], ones_bf[:, :], sqt[:, :], start=(k == 0), stop=(k == NKC - 1)),
                    deps=[sq, s2tok] + (bk.war("S1") if k == 0 else []))
                xtoks = [cb, sq]
            t1 = p.emit("vector", lambda v: v.tensor_scalar_mul(mu[:, :], S0[:, :], 1.0 / D), deps=[s1tok])
            bk.read("S0", t1)
            t2 = p.emit("vector", lambda v: v.tensor_scalar_mul(srow[:, :], S1[:, :], 1.0 / D), deps=[s2tok])
            bk.read("S1", t2)
            t3 = p.emit("vector", lambda v: v.tensor_mul(srow2[:, :], mu[:, :], mu[:, :]), deps=[t1])
            t4 = p.emit("vector", lambda v: v.tensor_sub(srow[:, :], srow[:, :], srow2[:, :]), deps=[t2, t3])
            t4b = p.emit("vector", lambda v: v.tensor_scalar_add(srow[:, :], srow[:, :], 1e-5), deps=[t4])
            t5 = p.emit("scalar", lambda s: s.activation(out=srow[:, :], in_=srow[:, :], func=AF.Sqrt), deps=[t4b])
            t6 = p.emit("vector", lambda v: v.reciprocal(rstd[:, :], srow[:, :]), deps=[t5])
            t7 = p.emit("vector", lambda v: v.tensor_mul(murstd[:, :], mu[:, :], rstd[:, :]), slow=True, deps=[t6])
            b1 = p.emit("tensor", lambda t: t.matmul(P0[:, :], ones_f[:, :], rstd[:, :], start=True, stop=True),
                        deps=[t6] + bk.war("P0"))
            b2 = p.emit("tensor", lambda t: t.matmul(P1[:, :], ones_f[:, :], murstd[:, :], start=True, stop=True),
                        deps=[t7] + bk.war("P1"))
            for k in range(NKC):
                ld = p.emit("sync", lambda s, k=k, gsl=gsl: s.dma_start(
                    out=xt[:, :], in_=x_h[k * 128:(k + 1) * 128, gsl]), deps=xtoks, dma=True, ch="xt")
                m1 = p.emit("vector", lambda v: v.tensor_mul(tmp[:, :], xt[:, :], P0[:, :]), deps=[ld, b1])
                m2 = p.emit("vector", lambda v: v.tensor_sub(tmp[:, :], tmp[:, :], P1[:, :]), deps=[m1, b2])
                m3 = p.emit("vector", lambda v: v.tensor_copy(zbt[:, :], tmp[:, :]), deps=[m2] + zst)
                zs = p.emit("gpsimd", lambda g, k=k, gsl=gsl: g.dma_start(
                    out=zbf_h[k * 128:(k + 1) * 128, gsl], in_=zbt[:, :]), deps=[m3], dma=True, ch="zb")
                zst = [zs]
                xtoks = [m1, m2, m3]
            bk.read("P0", xtoks[0])
            bk.read("P1", xtoks[1])
        ln_done = xtoks + zst

        def build_zmu(pi, seg, deps):
            toks = []
            prev = list(deps)
            for k in range(NKC):
                ld = p.emit("sync", lambda s, k=k, seg=seg: s.dma_start(
                    out=zmu[k][:, :], in_=zbf_h[k * 128:(k + 1) * 128, seg * SEGW:(seg + 1) * SEGW]),
                    deps=prev + bk.war(("zmu", k)), dma=True, ch=f"zmu{k % 2}")
                a = p.emit("vector", lambda v, k=k, pi=pi: v.tensor_scalar_add(
                    zmu[k][:, :], zmu[k][:, :], mus[:, pi * 8 + k:pi * 8 + k + 1]),
                    deps=[ld] + mu_all[-1:])
                toks.append(a)
                prev = [a]
            return toks

        def load_w(dst_list, src, width, n, ch, deps):
            toks = []
            for i in range(n):
                toks.append(p.emit("gpsimd", lambda g, i=i, src=src, width=width, dst_list=dst_list: g.dma_start(
                    out=dst_list[i][:, 0:width], in_=src[i * 128:(i + 1) * 128, :]),
                    deps=deps + bk.war((ch, i)), dma=True, ch=f"wl_{ch}_{i}"))
            return toks

        def load_bias(dst, src, n, ch, deps):
            toks = []
            for i in range(n):
                toks.append(p.emit("sync", lambda s, i=i, src=src, dst=dst: s.dma_start(
                    out=dst[:, i:i + 1], in_=src[i]), deps=deps, dma=True, ch=ch))
            return toks

        path_idx = {"r": 0, "k": 1, "v": 2, "g": 3, "a": 4, "d": 5}

        def emit_big_mlp(pth, seg, deps):
            zt = build_zmu(path_idx[pth], seg, deps)
            wt1 = load_w(w1b, w1_h[pth], EP, NKC, "w1s", deps)
            bt1 = load_bias(b1t, b1_h[pth], NFC, "b1s", deps)
            bt2 = load_bias(b2t, b2_h[pth], NKC, "b2s", deps)
            hup = {}
            for f in range(NFC):
                for ts in range(NTS):
                    tsl = slice(ts * TS, (ts + 1) * TS)
                    hb = [P2, P3][ts % 2]
                    hkey = ("h", ts % 2)
                    mmt = None
                    for k in range(NKC):
                        mmt = p.emit("tensor", lambda t, k=k, f=f, tsl=tsl, hb=hb: t.matmul(
                            hb[:, :], w1b[k][:, f * 128:(f + 1) * 128], zmu[k][:, tsl],
                            start=(k == 0), stop=(k == NKC - 1)),
                            deps=[wt1[k], zt[k], mmt] + (bk.war(hkey) if k == 0 else []))
                    ge = p.emit("scalar", lambda s, f=f, tsl=tsl, hb=hb: s.activation(
                        out=hx[f][:, tsl], in_=hb[:, :], func=AF.Gelu, bias=b1t[:, f:f + 1]),
                        deps=[mmt, bt1[-1]] + bk.war(("hx", f, ts)))
                    bk.read(hkey, ge)
                    hup[(f, ts)] = ge
            tail = []
            wt2 = load_w(w2b, w2_h[pth], D, NFC, "w2s", list(deps))
            for c in range(NKC):
                for ts in range(NTS):
                    tsl = slice(ts * TS, (ts + 1) * TS)
                    yb = [P4, P5][ts % 2]
                    ykey = ("y", ts % 2)
                    mmt = None
                    for f in range(NFC):
                        mmt = p.emit("tensor", lambda t, f=f, c=c, tsl=tsl, yb=yb: t.matmul(
                            yb[:, :], w2b[f][:, c * 128:(c + 1) * 128], hx[f][:, tsl],
                            start=(f == 0), stop=(f == NFC - 1)),
                            deps=[wt2[f], hup[(f, ts)], mmt] + (bk.war(ykey) if f == 0 else []))
                    ev = p.emit("vector", lambda v, c=c, tsl=tsl, yb=yb: v.tensor_scalar(
                        out[c][:, tsl], yb[:, :], b2t[:, c:c + 1], None, ALU.add),
                        deps=[mmt, bt2[-1]] + bk.war(("out", c, ts)))
                    bk.read(ykey, ev)
                    for f in range(NFC):
                        bk.read(("hx", f, ts), mmt)
                    tail.append(ev)
            for k in range(NKC):
                bk.read(("w1s", k), tail[-1])
            for f in range(NFC):
                bk.read(("w2s", f), tail[-1])
            return tail

        def emit_lowrank(pth, seg, deps, post):
            zt = build_zmu(path_idx[pth], seg, deps)
            wtA = load_w(w1b, lrA_h[pth], LR, NKC, "w1s", deps)
            btl = load_bias(b1t, lrb_h[pth], NKC, "b1s", deps)
            wtB = load_w(w2b, lrB_h[pth], D, 1, "w2s", deps)
            t1up = {}
            for ts in range(NTS):
                tsl = slice(ts * TS, (ts + 1) * TS)
                hb = [P2, P3][ts % 2]
                hkey = ("h", ts % 2)
                mmt = None
                for k in range(NKC):
                    mmt = p.emit("tensor", lambda t, k=k, tsl=tsl, hb=hb: t.matmul(
                        hb[:, :], w1b[k][:, 0:LR], zmu[k][:, tsl],
                        start=(k == 0), stop=(k == NKC - 1)),
                        deps=[wtA[k], zt[k], mmt] + (bk.war(hkey) if k == 0 else []))
                cv = p.emit("vector", lambda v, tsl=tsl, hb=hb: v.tensor_copy(
                    hx[0][:, tsl], hb[:, :]), deps=[mmt] + bk.war(("hx", 0, ts)))
                bk.read(hkey, cv)
                t1up[ts] = cv
            tail = []
            for c in range(NKC):
                for ts in range(NTS):
                    tsl = slice(ts * TS, (ts + 1) * TS)
                    yb = [P4, P5][ts % 2]
                    ykey = ("y", ts % 2)
                    mmt = p.emit("tensor", lambda t, c=c, tsl=tsl, yb=yb: t.matmul(
                        yb[:, :], w2b[0][:, c * 128:(c + 1) * 128], hx[0][:, tsl],
                        start=True, stop=True),
                        deps=[wtB[0], t1up[ts]] + bk.war(ykey))
                    toks = post(c, ts, tsl, yb, [mmt, btl[-1]], seg)
                    bk.read(ykey, toks[0])
                    bk.read(("hx", 0, ts), mmt)
                    tail += toks
            bk.read(("w2s", 0), tail[-1])
            for k in range(NKC):
                bk.read(("w1s", k), tail[-1])
            return tail

        cur = ln_done + ctoks
        for seg in range(NSEG):
            soff = seg * SEGW

            def post_a(c, ts, tsl, yb, deps, seg):
                s1 = p.emit("scalar", lambda s, yb=yb, c=c: s.activation(
                    out=tmp[:, :], in_=yb[:, :], func=AF.Sigmoid, bias=b1t[:, c:c + 1]),
                    deps=deps + bk.war("tmp"))
                dr = p.emit("scalar", lambda s: s.drain(), deps=[s1])
                d1 = p.emit("gpsimd", lambda g, c=c, tsl=tsl, seg=seg: g.dma_start(
                    out=iclr_h[c * 128:(c + 1) * 128, seg * SEGW + tsl.start:seg * SEGW + tsl.stop], in_=tmp[:, :]),
                    deps=[dr], dma=True, ch="st_i")
                bk.read("tmp", d1)
                return [s1, d1]
            cur = emit_lowrank("a", seg, cur, post_a)

            def post_g(c, ts, tsl, yb, deps, seg):
                s1 = p.emit("scalar", lambda s, yb=yb: s.activation(
                    out=tmp[:, :], in_=yb[:, :], func=AF.Sigmoid), deps=deps + bk.war("tmp"))
                dr = p.emit("scalar", lambda s: s.drain(), deps=[s1])
                d1 = p.emit("gpsimd", lambda g, c=c, tsl=tsl, seg=seg: g.dma_start(
                    out=outs["o_gate"][c * 128:(c + 1) * 128, seg * SEGW + tsl.start:seg * SEGW + tsl.stop], in_=tmp[:, :]),
                    deps=[dr], dma=True, ch="st_g")
                bk.read("tmp", d1)
                return [s1, d1]
            cur = emit_lowrank("g", seg, cur, post_g)

            def post_d(c, ts, tsl, yb, deps, seg):
                s1 = p.emit("scalar", lambda s, yb=yb, c=c: s.activation(
                    out=tmp[:, :], in_=yb[:, :], func=AF.Tanh, bias=b1t[:, c:c + 1]),
                    deps=deps + bk.war("tmp"))
                s2 = p.emit("scalar", lambda s: s.activation(
                    out=tmp[:, :], in_=tmp[:, :], func=AF.Sigmoid), deps=[s1])
                dr = p.emit("scalar", lambda s: s.drain(), deps=[s2])
                d1 = p.emit("gpsimd", lambda g, c=c, tsl=tsl, seg=seg: g.dma_start(
                    out=outs["o_w"][c * 128:(c + 1) * 128, seg * SEGW + tsl.start:seg * SEGW + tsl.stop], in_=tmp[:, :]),
                    deps=[dr], dma=True, ch="st_w")
                bk.read("tmp", d1)
                return [s2, d1]
            cur = emit_lowrank("d", seg, cur, post_d)

            for pth, och in (("v", "o_v"), ("r", "o_r")):
                cur = emit_big_mlp(pth, seg, cur)
                stv = []
                for c in range(NKC):
                    dr = p.emit("scalar", lambda s: s.drain(), deps=cur)
                    stv.append(p.emit("gpsimd", lambda g, c=c, soff=soff, och=och: g.dma_start(
                        out=outs[och][c * 128:(c + 1) * 128, soff:soff + SEGW], in_=out[c][:, :]),
                        deps=[dr], dma=True, ch="st_" + och))
                    for ts in range(NTS):
                        bk.read(("out", c, ts), stv[-1])
                cur = cur + stv

            cur = emit_big_mlp("k", seg, cur)
            kst = []
            for c in range(NKC):
                for ts in range(NTS):
                    tsl = slice(ts * TS, (ts + 1) * TS)
                    gsl = slice(soff + ts * TS, soff + (ts + 1) * TS)
                    q1 = p.emit("vector", lambda v, c=c, tsl=tsl: v.tensor_scalar(
                        tmp[:, :], out[c][:, tsl], remt[:, c:c + 1], None, ALU.mult),
                        deps=cur + bk.war("tmp") + cextra[-1:])
                    q2 = p.emit("scalar", lambda s: s.activation(out=tmp2[:, :], in_=tmp[:, :], func=AF.Square),
                                deps=[q1] + bk.war("tmp2"))
                    q3 = p.emit("tensor", lambda t: t.matmul(P0[0:2, :], Et[:, :], tmp2[:, :], start=True, stop=True),
                                deps=[q2, ctoks[0]] + bk.war("P0"))
                    bk.read("tmp2", q3)
                    q4 = p.emit("vector", lambda v: v.tensor_copy(nrow[:, :], P0[0:2, :]), deps=[q3] + bk.war("nrow"))
                    bk.read("P0", q4)
                    q5 = p.emit("scalar", lambda s: s.activation(out=nrow[:, :], in_=nrow[:, :], func=AF.Sqrt), deps=[q4])
                    q6 = p.emit("vector", lambda v: v.tensor_scalar_max(nrow[:, :], nrow[:, :], 1e-12), deps=[q5])
                    q7 = p.emit("vector", lambda v: v.reciprocal(nrow[:, :], nrow[:, :]), slow=True, deps=[q6])
                    q8 = p.emit("tensor", lambda t: t.matmul(P1[:, :], E2t[:, :], nrow[:, :], start=True, stop=True),
                                deps=[q7, ctoks[1]] + bk.war("P1"))
                    bk.read("nrow", q8)
                    q9 = p.emit("vector", lambda v: v.tensor_mul(tmp2[:, :], tmp[:, :], P1[:, :]),
                                deps=[q8, q1] + bk.war("tmp2"))
                    bk.read("P1", q9)
                    bk.read("tmp", q9)
                    dr1 = p.emit("vector", lambda v: v.drain(), deps=[q9])
                    o1 = p.emit("gpsimd", lambda g, c=c, gsl=gsl: g.dma_start(
                        out=outs["o_kn"][c * 128:(c + 1) * 128, gsl], in_=tmp2[:, :]), deps=[dr1], dma=True, ch="st_kn")
                    li = p.emit("sync", lambda s, c=c, gsl=gsl: s.dma_start(
                        out=itile[:, :], in_=iclr_h[c * 128:(c + 1) * 128, gsl]),
                        deps=cur + bk.war("itile"), dma=True, ch="itile")
                    a1 = p.emit("vector", lambda v: v.tensor_mul(tmp3[:, :], tmp2[:, :], itile[:, :]),
                                deps=[li, q9] + bk.war("tmp3"))
                    bk.read("tmp2", a1)
                    dr2 = p.emit("vector", lambda v: v.drain(), deps=[a1])
                    o2 = p.emit("gpsimd", lambda g, c=c, gsl=gsl: g.dma_start(
                        out=outs["o_at"][c * 128:(c + 1) * 128, gsl], in_=tmp3[:, :]), deps=[dr2], dma=True, ch="st_at")
                    bk.read("tmp3", o2)
                    r1 = p.emit("vector", lambda v, c=c, tsl=tsl: v.tensor_scalar(
                        tmp[:, :], out[c][:, tsl], mixt[:, c:c + 1], None, ALU.mult),
                        deps=[o1, q9] + cextra[-1:])
                    r2 = p.emit("vector", lambda v: v.tensor_scalar_add(tmp2[:, :], itile[:, :], -1.0),
                                deps=[li, a1] + bk.war("tmp2"))
                    bk.read("itile", r2)
                    r3 = p.emit("vector", lambda v: v.tensor_mul(tmp[:, :], tmp[:, :], tmp2[:, :]), deps=[r1, r2])
                    bk.read("tmp2", r3)
                    r4 = p.emit("vector", lambda v, c=c, tsl=tsl: v.tensor_add(tmp[:, :], tmp[:, :], out[c][:, tsl]), deps=[r3])
                    dr3 = p.emit("vector", lambda v: v.drain(), deps=[r4])
                    o3 = p.emit("gpsimd", lambda g, c=c, gsl=gsl: g.dma_start(
                        out=outs["o_rk"][c * 128:(c + 1) * 128, gsl], in_=tmp[:, :]), deps=[dr3], dma=True, ch="st_rk")
                    bk.read("tmp", o3)
                    bk.read(("out", c, ts), o3)
                    kst += [o1, o2, o3]
            cur = cur + kst
        p.emit("gpsimd", lambda g: g.engine_nop(), deps=cur)

        with nc.Block() as block:
            p.realize(block)
    return nc


def run_dn_proj(xT, I):
    nc = _get("dn_proj", build_dn_proj)
    shards = _shard_fm(xT)
    EP, E0 = 1280, 1228

    def padW1(w):
        out = np.zeros((D, EP), np.float32)
        out[:, :E0] = w
        return out

    def padW2(w):
        out = np.zeros((EP, D), np.float32)
        out[:E0] = w
        return out

    def padb(b):
        out = np.zeros(EP, np.float32)
        out[:E0] = b
        return np.ascontiguousarray(out.reshape(10, 128, 1))

    Emat = np.zeros((128, 2), np.float32)
    Emat[:64, 0] = 1.0
    Emat[64:, 1] = 1.0
    base = {
        "Emat": Emat, "E2mat": np.ascontiguousarray(Emat.T),
        "mus": np.stack([I[f"L0_attn_mu_{c}"].reshape(8, 128, 1) for c in "rkvgad"]).astype(np.float32),
        "rem_mult": I["L0_attn_rem_mult"].reshape(8, 128, 1).astype(np.float32),
        "iclr_mix": I["L0_attn_iclr_mix"].reshape(8, 128, 1).astype(np.float32),
    }
    for pth in ("r", "k", "v"):
        base[f"{pth}W1"] = padW1(np.asarray(I[f"L0_attn_{pth}W1"]))
        base[f"{pth}b1"] = padb(np.asarray(I[f"L0_attn_{pth}b1"]))
        base[f"{pth}W2"] = padW2(np.asarray(I[f"L0_attn_{pth}W2"]))
        base[f"{pth}b2"] = np.ascontiguousarray(np.asarray(I[f"L0_attn_{pth}b2"]).reshape(8, 128, 1))
    for pth, nm in (("g", "g"), ("a", "a"), ("d", "d")):
        base[f"{pth}A"] = np.ascontiguousarray(np.asarray(I[f"L0_attn_{nm}A"]))
        base[f"{pth}B"] = np.ascontiguousarray(np.asarray(I[f"L0_attn_{nm}B"]))
    base["gb"] = np.zeros((8, 128, 1), np.float32)
    base["ab"] = np.ascontiguousarray(np.asarray(I["L0_attn_ab"]).reshape(8, 128, 1))
    base["db"] = np.ascontiguousarray(np.asarray(I["L0_attn_db"]).reshape(8, 128, 1))
    in_maps = [dict(base, x=s) for s in shards]
    res = run_bass_kernel_spmd(nc, in_maps, core_ids=list(range(NCORE)))
    names = ("o_r", "o_v", "o_kn", "o_at", "o_rk", "o_gate", "o_w")
    return {n: np.concatenate([r[n] for r in res.results], axis=1) for n in names}


DECAY_C = 0.7408182206817179


def build_dn_scan(dbg=0):
    NH = 4
    T = 8192
    L = 64
    BLKW = 512
    NBLK = T // BLKW
    NCH = BLKW // L
    NEWTON = 4

    nc = bass.Bass("TRN2", target_bir_lowering=False)
    ins = {}
    for s_ in ("w", "at", "kn", "rk", "v", "r"):
        ins[s_] = nc.dram_tensor(f"in_{s_}", [NH * 64, T], F32, kind="ExternalInput").ap()
    mask_h = nc.dram_tensor("masks", [4, 64, 64], F32, kind="ExternalInput").ap()
    oy_h = nc.dram_tensor("o_y", [NH * 64, T], F32, kind="ExternalOutput").ap()
    if dbg:
        dbgf_h = nc.dram_tensor("dbg_f", [64, 4 * 512], F32, kind="ExternalOutput").ap()
        dbgb_h = nc.dram_tensor("dbg_b", [64, 16 * 64], BF16, kind="ExternalOutput").ap()

    with ExitStack() as st:
        p = P(nc, st)
        bk = Banks()
        sb = lambda name, shape, dt=F32: st.enter_context(nc.sbuf_tensor(name, shape, dt))
        pt = lambda name, shape, dt=F32: st.enter_context(nc.psum_tensor(name, shape, dt))

        stw = {s_: sb(f"st_{s_}", [64, BLKW], F32) for s_ in ("w", "kn", "rk", "v", "r", "at")}
        gA = sb("gA", [64, BLKW], F32)
        gB = sb("gB", [64, BLKW], F32)
        Pt = sb("Pt", [64, BLKW], F32)
        Ppv = sb("Ppv", [64, BLKW], F32)
        Piv = sb("Piv", [64, BLKW], F32)
        apb = sb("apb", [64, BLKW], BF16)
        knb = sb("knb", [64, BLKW], BF16)
        vpb = sb("vpb", [64, BLKW], BF16)
        rkb = sb("rkb", [64, BLKW], BF16)
        rb = sb("rb", [64, BLKW], BF16)
        ob = sb("ob", [64, BLKW], F32)
        msu = sb("msu", [64, 64], F32)
        msl = sb("msl", [64, 64], F32)
        mui = sb("mui", [64, 64], F32)
        idnf = sb("idnf", [64, 64], F32)
        idnb = sb("idnb", [64, 64], BF16)
        W0f = [sb(f"W0f{j}", [64, 64], F32) for j in range(NH)]
        W0b = sb("W0b", [64, 64], BF16)
        w0tb = sb("w0tb", [64, 64], BF16)
        aprow = sb("aprow", [64, 64], BF16)
        rkrow = sb("rkrow", [64, 64], BF16)
        knrn = sb("knrn", [64, 64], BF16)
        vprow = sb("vprow", [64, 64], BF16)
        ctm = sb("ctm", [64, 64], BF16)
        cmf = sb("cmf", [64, 64], F32)
        Xa = sb("Xa", [64, 64], BF16)
        XTa = sb("XTa", [64, 64], BF16)
        Xb_ = sb("Xb_", [64, 64], BF16)
        XTb_ = sb("XTb_", [64, 64], BF16)
        Yt = sb("Yt", [64, 64], BF16)
        YTt = sb("YTt", [64, 64], BF16)
        btm = sb("btm", [64, 64], BF16)
        t1b = sb("t1b", [64, 64], BF16)
        utb = sb("utb", [64, 64], BF16)
        uttb = sb("uttb", [64, 64], BF16)
        ub = sb("ub", [64, 64], BF16)
        utb2 = sb("utb2", [64, 64], BF16)
        qb = sb("qb", [64, 64], BF16)
        qtb = sb("qtb", [64, 64], BF16)
        atb = sb("atb", [64, 64], BF16)
        btb = sb("btb", [64, 64], BF16)
        wtmp = sb("wtmp", [64, 64], F32)

        PS = [pt(f"ps{i}", [64, 64], F32) for i in range(6)]
        PTb = [pt(f"ptb{i}", [64, 64], BF16) for i in range(2)]

        ctoks = [
            p.emit("sync", lambda s: s.dma_start(out=msu[:, :], in_=mask_h[0]), dma=True, ch="c0"),
            p.emit("sync", lambda s: s.dma_start(out=mui[:, :], in_=mask_h[1]), dma=True, ch="c1"),
            p.emit("sync", lambda s: s.dma_start(out=idnf[:, :], in_=mask_h[2]), dma=True, ch="c2"),
            p.emit("sync", lambda s: s.dma_start(out=msl[:, :], in_=mask_h[3]), dma=True, ch="c3"),
        ]
        idnb_tok = p.emit("vector", lambda v: v.tensor_copy(idnb[:, :], idnf[:, :]), deps=[ctoks[2]])
        ctoks.append(idnb_tok)
        zt = [p.emit("vector", lambda v, j=j: v.memset(W0f[j][:, :], 0.0)) for j in range(NH)]

        def mm(bank, lhsT, rhs, deps, start=True, stop=True):
            key = ("ps", bank)
            d = deps + (bk.war(key) if start else [])
            return p.emit("tensor", lambda t: t.matmul(PS[bank][:, :], lhsT, rhs, start=start, stop=stop), deps=d)

        def evac(bank, out_ap, deps, src_scale=None):
            key = ("ps", bank)
            if src_scale is None:
                tok = p.emit("vector", lambda v: v.tensor_copy(out_ap, PS[bank][:, :]), deps=deps)
            else:
                tok = p.emit("vector", lambda v: v.tensor_scalar_mul(out_ap, PS[bank][:, :], src_scale), deps=deps)
            bk.read(key, tok)
            return tok

        def trans(pb, in_ap, out_ap, deps, scale=None):
            key = ("pt", pb)
            t1 = p.emit("tensor", lambda t: t.transpose(PTb[pb][:, :], in_ap, idnb[:, :]),
                        deps=deps + [idnb_tok] + bk.war(key))
            if scale is None:
                t2 = p.emit("vector", lambda v: v.tensor_copy(out_ap, PTb[pb][:, :]), deps=[t1])
            else:
                t2 = p.emit("vector", lambda v: v.tensor_scalar_mul(out_ap, PTb[pb][:, :], scale), deps=[t1])
            bk.read(key, t2)
            return t2

        head_tail = {j: zt[j:j + 1] + ctoks for j in range(NH)}
        all_stores = []
        for j in range(NH):
            hrows = slice(j * 64, (j + 1) * 64)
            for blk in range(NBLK):
                bsl = slice(blk * BLKW, (blk + 1) * BLKW)
                cur = head_tail[j]
                lts = {}
                for s_ in ("w", "at", "kn", "rk", "v", "r"):
                    lts[s_] = p.emit("sync", lambda s, s_=s_, hrows=hrows, bsl=bsl: s.dma_start(
                        out=stw[s_][:, :], in_=ins[s_][hrows, bsl]),
                        deps=cur + bk.war(("stw", s_)), dma=True, ch="ld_" + s_)
                c0 = p.emit("vector", lambda v: v.tensor_scalar_mul(gA[:, :], stw["w"][:, :], -DECAY_C),
                            deps=[lts["w"]] + bk.war("gA"))
                bk.read(("stw", "w"), c0)
                wsc_src = gA
                c1 = p.emit("vector", lambda v: v.tensor_copy(gB[:, :], gA[:, :]), deps=[c0] + bk.war("gB"))
                src, dst = gA, gB
                ctok = c1
                for sh in (1, 2, 4, 8, 16, 32):
                    ops = []
                    for c in range(NCH):
                        lo = c * L
                        o1 = p.emit("vector", lambda v, lo=lo, sh=sh, src=src, dst=dst: v.tensor_copy(
                            dst[:, lo:lo + sh], src[:, lo:lo + sh]), deps=[ctok])
                        o2 = p.emit("vector", lambda v, lo=lo, sh=sh, src=src, dst=dst: v.tensor_add(
                            dst[:, lo + sh:lo + L], src[:, lo + sh:lo + L], src[:, lo:lo + L - sh]), deps=[ctok])
                        ops += [o1, o2]
                    ctok = ops[-1]
                    src, dst = dst, src
                    ctoks_all = ops
                gsc = src
                wsc = None
                c2 = p.emit("vector", lambda v, dst=dst: v.tensor_scalar_mul(dst[:, :], stw["w"][:, :], -DECAY_C),
                            deps=[lts["w"], ctok])
                e1 = p.emit("scalar", lambda s, gsc=gsc: s.activation(out=Pt[:, :], in_=gsc[:, :], func=AF.Exp),
                            deps=[ctok] + bk.war("Pt"))
                c3 = p.emit("vector", lambda v, gsc=gsc, dst=dst: v.tensor_sub(Ppv[:, :], gsc[:, :], dst[:, :]),
                            deps=[ctok, c2] + bk.war("Ppv"))
                e2 = p.emit("scalar", lambda s: s.activation(out=Ppv[:, :], in_=Ppv[:, :], func=AF.Exp), deps=[c3])
                c4 = p.emit("vector", lambda v, gsc=gsc, dst=dst: v.tensor_scalar_mul(dst[:, :], gsc[:, :], -1.0),
                            deps=[ctok, c3, c2])
                e3 = p.emit("scalar", lambda s, dst=dst: s.activation(out=Piv[:, :], in_=dst[:, :], func=AF.Exp),
                            deps=[c4] + bk.war("Piv"))
                b1 = p.emit("vector", lambda v: v.tensor_mul(apb[:, :], stw["at"][:, :], Ppv[:, :]),
                            deps=[lts["at"], e2] + bk.war("apb"))
                bk.read(("stw", "at"), b1)
                bk.read("Ppv", b1)
                b2 = p.emit("vector", lambda v: v.tensor_mul(knb[:, :], stw["kn"][:, :], Piv[:, :]),
                            deps=[lts["kn"], e3] + bk.war("knb"))
                bk.read(("stw", "kn"), b2)
                b3 = p.emit("vector", lambda v: v.tensor_mul(vpb[:, :], stw["v"][:, :], Piv[:, :]),
                            deps=[lts["v"], e3] + bk.war("vpb"))
                bk.read(("stw", "v"), b3)
                bk.read("Piv", b3)
                b4 = p.emit("vector", lambda v: v.tensor_copy(rkb[:, :], stw["rk"][:, :]),
                            deps=[lts["rk"]] + bk.war("rkb"))
                bk.read(("stw", "rk"), b4)
                b5 = p.emit("vector", lambda v: v.tensor_copy(rb[:, :], stw["r"][:, :]),
                            deps=[lts["r"]] + bk.war("rb"))
                bk.read(("stw", "r"), b5)
                bk.read("gA", b3)
                bk.read("gB", b3)
                blk_prep = [b1, b2, b3, b4, b5, e1]
                if dbg and j == 0 and blk == 0:
                    dtk = []
                    for idx, (tl, tok) in enumerate([(gsc, ctok), (Pt, e1), (Ppv, e2), (Piv, e3)]):
                        drx = p.emit("vector", lambda v: v.drain(), deps=[tok, b3, b1])
                        dtk.append(p.emit("gpsimd", lambda g, idx=idx, tl=tl: g.dma_start(
                            out=dbgf_h[:, idx * BLKW:(idx + 1) * BLKW], in_=tl[:, :]), deps=[drx], dma=True, ch="dbg"))
                    blk_prep = blk_prep + dtk

                prev_chunk = list(cur) + blk_prep
                for c in range(NCH):
                    sl = slice(c * L, (c + 1) * L)
                    tr1 = trans(0, apb[:, sl], aprow[:, :], blk_prep + bk.war("aprow"))
                    bk.read("apb", tr1)
                    tr2 = trans(1, rkb[:, sl], rkrow[:, :], blk_prep + bk.war("rkrow"))
                    tr3 = trans(0, vpb[:, sl], vprow[:, :], blk_prep + [tr1] + bk.war("vprow"))
                    tr4 = trans(1, knb[:, sl], knrn[:, :], blk_prep + [tr2] + bk.war("knrn"), scale=-1.0)
                    wcp = p.emit("vector", lambda v, j=j: v.tensor_copy(W0b[:, :], W0f[j][:, :]),
                                 deps=prev_chunk + bk.war("W0b"))
                    tr5 = trans(0, W0b[:, :], w0tb[:, :], [wcp, tr3] + bk.war("w0tb"))
                    m1 = mm(0, apb[:, sl], knb[:, sl], [b1, b2])
                    m2 = mm(1, knb[:, sl], apb[:, sl], [b1, b2])
                    cm1 = p.emit("vector", lambda v: v.tensor_mul(cmf[:, :], PS[0][:, :], msl[:, :]), deps=[m1, ctoks[3]] + bk.war("cmf"))
                    bk.read(("ps", 0), cm1)
                    ct1 = p.emit("vector", lambda v: v.tensor_mul(wtmp[:, :], PS[1][:, :], msu[:, :]), deps=[m2, ctoks[0]] + bk.war("wtmp"))
                    bk.read(("ps", 1), ct1)
                    ct2 = p.emit("vector", lambda v: v.tensor_copy(ctm[:, :], wtmp[:, :]), deps=[ct1] + bk.war("ctm"))
                    x0 = p.emit("vector", lambda v: v.tensor_sub(Xa[:, :], idnf[:, :], cmf[:, :]), deps=[cm1, ctoks[2]] + bk.war("Xa"))
                    xt0 = p.emit("vector", lambda v: v.tensor_sub(XTa[:, :], idnf[:, :], wtmp[:, :]), deps=[ct1, ctoks[2]] + bk.war("XTa"))
                    bk.read("wtmp", xt0)
                    X, XT, Xo, XTo = Xa, XTa, Xb_, XTb_
                    xkey, xokey = ("Xa", "XTa"), ("Xb_", "XTb_")
                    xtok, xttok = x0, xt0
                    for it in range(NEWTON):
                        n1 = mm(2, ctm[:, :], X[:, :], [ct2, xtok])
                        y1 = p.emit("vector", lambda v, X=X: v.tensor_add(Yt[:, :], PS[2][:, :], X[:, :]),
                                    deps=[n1, xtok] + bk.war("Yt"))
                        bk.read(("ps", 2), y1)
                        n3 = mm(4, XT[:, :], Yt[:, :], [y1, xttok])
                        x1 = p.emit("vector", lambda v, X=X, Xo=Xo: v.scalar_tensor_tensor(
                            Xo[:, :], X[:, :], 2.0, PS[4][:, :], ALU.mult, ALU.subtract),
                            deps=[n3] + bk.war(xokey[0]))
                        bk.read(("ps", 4), x1)
                        n4 = mm(5, Yt[:, :], XT[:, :], [y1, xttok])
                        x2 = p.emit("vector", lambda v, XT=XT, XTo=XTo: v.scalar_tensor_tensor(
                            XTo[:, :], XT[:, :], 2.0, PS[5][:, :], ALU.mult, ALU.subtract),
                            deps=[n4] + bk.war(xokey[1]))
                        bk.read(("ps", 5), x2)
                        bk.read("Yt", n4)
                        bk.read(xkey[0], x1)
                        bk.read(xkey[1], x2)
                        X, XT, Xo, XTo = Xo, XTo, X, XT
                        xkey, xokey = xokey, xkey
                        xtok, xttok = x1, x2
                    m3 = mm(2, vpb[:, sl], apb[:, sl], [b1, b3])
                    bt1 = p.emit("vector", lambda v: v.tensor_mul(wtmp[:, :], PS[2][:, :], msu[:, :]),
                                 deps=[m3, ctoks[0]] + bk.war("wtmp"))
                    bk.read(("ps", 2), bt1)
                    bt2 = p.emit("vector", lambda v: v.tensor_copy(btm[:, :], wtmp[:, :]), deps=[bt1] + bk.war("btm"))
                    bk.read("wtmp", bt2)
                    m4 = mm(3, btm[:, :], rkrow[:, :], [bt2, tr2])
                    t1e = evac(3, t1b[:, :], [m4] + bk.war("t1b"))
                    m5 = mm(4, XT[:, :], t1b[:, :], [xttok, t1e])
                    u1 = evac(4, utb[:, :], [m5] + bk.war("utb"))
                    m6 = mm(5, t1b[:, :], XT[:, :], [xttok, t1e])
                    u2 = evac(5, uttb[:, :], [m6] + bk.war("uttb"))
                    bk.read("t1b", m6)
                    m7 = mm(2, XT[:, :], aprow[:, :], [xttok, tr1])
                    u3 = evac(2, ub[:, :], [m7] + bk.war("ub"))
                    m8 = mm(3, aprow[:, :], XT[:, :], [xttok, tr1])
                    u4 = evac(3, utb2[:, :], [m8] + bk.war("utb2"))
                    bk.read("aprow", m8)
                    m9 = mm(4, utb2[:, :], W0b[:, :], [u4, wcp])
                    q1 = p.emit("vector", lambda v: v.tensor_add(qb[:, :], PS[4][:, :], utb[:, :]),
                                deps=[m9, u1] + bk.war("qb"))
                    bk.read(("ps", 4), q1)
                    bk.read("utb", q1)
                    m10 = mm(5, W0b[:, :], utb2[:, :], [u4, wcp])
                    q2 = p.emit("vector", lambda v: v.tensor_add(qtb[:, :], PS[5][:, :], uttb[:, :]),
                                deps=[m10, u2] + bk.war("qtb"))
                    bk.read(("ps", 5), q2)
                    bk.read("uttb", q2)
                    bk.read("utb2", m10)
                    bk.read(xkey[1], m10)
                    m11 = mm(2, rkb[:, sl], rb[:, sl], [b4, b5])
                    a1_ = p.emit("vector", lambda v: v.tensor_mul(wtmp[:, :], PS[2][:, :], mui[:, :]),
                                 deps=[m11, ctoks[1]] + bk.war("wtmp"))
                    bk.read(("ps", 2), a1_)
                    a2_ = p.emit("vector", lambda v: v.tensor_copy(atb[:, :], wtmp[:, :]), deps=[a1_] + bk.war("atb"))
                    bk.read("wtmp", a2_)
                    m12 = mm(3, qtb[:, :], rb[:, sl], [q2, b5])
                    b1_ = p.emit("vector", lambda v: v.tensor_mul(wtmp[:, :], PS[3][:, :], mui[:, :]),
                                 deps=[m12, ctoks[1]] + bk.war("wtmp"))
                    bk.read(("ps", 3), b1_)
                    b2_ = p.emit("vector", lambda v: v.tensor_copy(btb[:, :], wtmp[:, :]), deps=[b1_] + bk.war("btb"))
                    bk.read("wtmp", b2_)
                    m13 = mm(4, w0tb[:, :], rb[:, sl], [tr5, b5], start=True, stop=False)
                    m14 = mm(4, vprow[:, :], atb[:, :], [tr3, a2_], start=False, stop=False)
                    m15 = mm(4, knrn[:, :], btb[:, :], [tr4, b2_], start=False, stop=True)
                    o1 = p.emit("vector", lambda v, sl=sl: v.tensor_mul(ob[:, sl], Pt[:, sl], PS[4][:, :]),
                                deps=[m15, e1] + bk.war(("ob", c)))
                    bk.read(("ps", 4), o1)
                    bk.read("atb", m14)
                    bk.read("btb", m15)
                    bk.read("vprow", m14)
                    bk.read("knrn", m15)
                    bk.read("rkrow", m4)
                    bk.read("w0tb", m13)
                    m16 = mm(5, vprow[:, :], rkrow[:, :], [tr3, tr2], start=True, stop=False)
                    m17 = mm(5, knrn[:, :], qb[:, :], [tr4, q1], start=False, stop=True)
                    bk.read("qb", m17)
                    bk.read("vprow", m16)
                    bk.read("knrn", m17)
                    bk.read("rkrow", m16)
                    w1_ = p.emit("vector", lambda v, j=j: v.tensor_add(wtmp[:, :], W0f[j][:, :], PS[5][:, :]),
                                 deps=[m17] + bk.war("wtmp"))
                    bk.read(("ps", 5), w1_)
                    w2_ = p.emit("vector", lambda v, j=j, c=c: v.tensor_scalar(
                        W0f[j][:, :], wtmp[:, :], Pt[:, c * L + L - 1:c * L + L], None, ALU.mult),
                        deps=[w1_, e1, wcp])
                    bk.read("wtmp", w2_)
                    bk.read("W0b", m10)
                    if dbg and j == 0 and blk == 0 and c == 0:
                        dtk2 = []
                        for idx, (tl, tok) in enumerate([(ctm, ct2), (XT, xttok), (t1b, t1e), (utb, u1),
                                                          (qtb, q2), (atb, a2_), (btb, b2_), (aprow, tr1),
                                                          (rkrow, tr2), (vprow, tr3), (knrn, tr4), (apb, b1),
                                                          (knb, b2), (vpb, b3)]):
                            drx = p.emit("vector", lambda v: v.drain(), deps=[tok, o1, w2_])
                            dtk2.append(p.emit("gpsimd", lambda g, idx=idx, tl=tl: g.dma_start(
                                out=dbgb_h[:, idx * 64:(idx + 1) * 64], in_=tl[:, 0:64]),
                                deps=[drx], dma=True, ch="dbg"))
                        prev_chunk = prev_chunk + dtk2
                    bk.read("apb", m8)
                    bk.read("knb", m2)
                    bk.read("vpb", m3)
                    bk.read("rkb", m11)
                    bk.read("rb", m13)
                    bk.read("Pt", w2_)
                    prev_chunk = [w2_, o1]
                drb = p.emit("vector", lambda v: v.drain(), deps=prev_chunk + [o1])
                so = p.emit("gpsimd", lambda g, hrows=hrows, bsl=bsl: g.dma_start(
                    out=oy_h[hrows, bsl], in_=ob[:, :]), deps=[drb], dma=True, ch="st_y")
                for cc in range(NCH):
                    bk.read(("ob", cc), so)
                all_stores.append(so)
                head_tail[j] = [w2_, so]
        p.emit("gpsimd", lambda g: g.engine_nop(), deps=all_stores)

        with nc.Block() as block:
            p.realize(block)
    return nc


def run_dn_scan(streams):
    nc = _get("dn_scan", build_dn_scan)
    msu = np.triu(np.ones((64, 64), np.float32), 1)
    mui = np.triu(np.ones((64, 64), np.float32), 0)
    idn = np.eye(64, dtype=np.float32)
    msl = np.tril(np.ones((64, 64), np.float32), -1)
    masks = np.stack([msu, mui, idn, msl])
    in_maps = []
    for c in range(NCORE):
        m = {"masks": masks}
        for s_ in ("w", "at", "kn", "rk", "v", "r"):
            slabs = []
            for pair in range(4 * c, 4 * c + 4):
                b, h = pair // 16, pair % 16
                slabs.append(streams[s_][h * 64:(h + 1) * 64, b * 8192:(b + 1) * 8192])
            m[f"in_{s_}"] = np.ascontiguousarray(np.concatenate(slabs, axis=0))
        in_maps.append(m)
    res = run_bass_kernel_spmd(nc, in_maps, core_ids=list(range(NCORE)))
    global _last_scan_res
    _last_scan_res = res
    y = np.zeros((D, TOK), np.float32)
    for c in range(NCORE):
        for i, pair in enumerate(range(4 * c, 4 * c + 4)):
            b, h = pair // 16, pair % 16
            y[h * 64:(h + 1) * 64, b * 8192:(b + 1) * 8192] = res.results[c]["o_y"][i * 64:(i + 1) * 64]
    return y


def build_dn_post():
    TS = 512
    SEGW = 1024
    NSEG = TPC // SEGW
    NTS = SEGW // TS
    NKC = D // 128
    EP = 1280
    NFC = EP // 128

    nc = bass.Bass("TRN2", target_bir_lowering=False)
    x_h = nc.dram_tensor("x", [D, TPC], F32, kind="ExternalInput").ap()
    y_h = nc.dram_tensor("y", [D, TPC], F32, kind="ExternalInput").ap()
    r_h = nc.dram_tensor("r", [D, TPC], F32, kind="ExternalInput").ap()
    rk_h = nc.dram_tensor("rk", [D, TPC], F32, kind="ExternalInput").ap()
    v_h = nc.dram_tensor("v", [D, TPC], F32, kind="ExternalInput").ap()
    g_h = nc.dram_tensor("gate", [D, TPC], F32, kind="ExternalInput").ap()
    ow1_h = nc.dram_tensor("oW1", [D, EP], F32, kind="ExternalInput").ap()
    ob1_h = nc.dram_tensor("ob1", [NFC, 128, 1], F32, kind="ExternalInput").ap()
    ow2_h = nc.dram_tensor("oW2", [EP, D], F32, kind="ExternalInput").ap()
    ob2_h = nc.dram_tensor("ob2", [NKC, 128, 1], F32, kind="ExternalInput").ap()
    lng_h = nc.dram_tensor("ln_g", [NKC, 128, 1], F32, kind="ExternalInput").ap()
    lnb_h = nc.dram_tensor("ln_b", [NKC, 128, 1], F32, kind="ExternalInput").ap()
    bm_h = nc.dram_tensor("bonus_mult", [NKC, 128, 1], F32, kind="ExternalInput").ap()
    E_h = nc.dram_tensor("Emat", [128, 2], F32, kind="ExternalInput").ap()
    E2_h = nc.dram_tensor("E2mat", [2, 128], F32, kind="ExternalInput").ap()
    xo_h = nc.dram_tensor("xo", [D, TPC], F32, kind="ExternalOutput").ap()

    with ExitStack() as st:
        p = P(nc, st)
        bk = Banks()
        sb = lambda name, shape, dt=F32: st.enter_context(nc.sbuf_tensor(name, shape, dt))
        pt = lambda name, shape, dt=F32: st.enter_context(nc.psum_tensor(name, shape, dt))

        o0 = [sb(f"o0_{k}", [128, SEGW], F32) for k in range(NKC)]
        zn = [sb(f"zn{k}", [128, SEGW], BF16) for k in range(NKC)]
        hx = [sb(f"hx{f}", [128, SEGW], BF16) for f in range(NFC)]
        w1b = [sb(f"w1b{k}", [128, EP], BF16) for k in range(NKC)]
        w2b = [sb(f"w2b{f}", [128, D], BF16) for f in range(NFC)]
        stage = sb("stage", [128, EP], F32)
        at1 = sb("at1", [128, TS], F32)
        at2 = sb("at2", [128, TS], F32)
        at3 = sb("at3", [128, TS], F32)
        sqt = sb("sqt", [128, TS], BF16)
        obt = sb("obt", [128, TS], BF16)
        ones_bf = sb("ones_bf", [128, 1], BF16)
        ones_f = sb("ones_f", [1, 128], F32)
        mu = sb("mu", [1, TS], F32)
        rstd = sb("rstd", [1, TS], F32)
        murstd = sb("murstd", [1, TS], F32)
        srow = sb("srow", [1, TS], F32)
        srow2 = sb("srow2", [1, TS], F32)
        nrow = sb("nrow", [2, TS], F32)
        b1t = sb("b1t", [128, NFC], F32)
        b2t = sb("b2t", [128, NKC], F32)
        lngt = sb("lngt", [128, NKC], F32)
        lnbt = sb("lnbt", [128, NKC], F32)
        bmt = sb("bmt", [128, NKC], F32)
        Et = sb("Et", [128, 2], F32)
        E2t = sb("E2t", [2, 128], F32)

        P0 = pt("P0", [128, TS])
        P1 = pt("P1", [128, TS])
        P2 = pt("P2", [128, TS])
        P3 = pt("P3", [128, TS])
        P4 = pt("P4", [128, TS])
        P5 = pt("P5", [128, TS])
        S0 = pt("S0", [1, TS])
        S1 = pt("S1", [1, TS])

        p.emit("gpsimd", lambda g: g.memset(ones_bf[:, :], 1.0))
        p.emit("gpsimd", lambda g: g.memset(ones_f[:, :], 1.0))
        ctoks = [
            p.emit("sync", lambda s: s.dma_start(out=Et[:, :], in_=E_h[:, :]), dma=True, ch="c0"),
            p.emit("sync", lambda s: s.dma_start(out=E2t[:, :], in_=E2_h[:, :]), dma=True, ch="c1"),
        ]
        for k in range(NKC):
            ctoks.append(p.emit("sync", lambda s, k=k: s.dma_start(out=lngt[:, k:k + 1], in_=lng_h[k]), dma=True, ch="c2"))
            ctoks.append(p.emit("sync", lambda s, k=k: s.dma_start(out=lnbt[:, k:k + 1], in_=lnb_h[k]), dma=True, ch="c3"))
            ctoks.append(p.emit("sync", lambda s, k=k: s.dma_start(out=bmt[:, k:k + 1], in_=bm_h[k]), dma=True, ch="c4"))
            ctoks.append(p.emit("sync", lambda s, k=k: s.dma_start(out=b2t[:, k:k + 1], in_=ob2_h[k]), dma=True, ch="c5"))
        for f in range(NFC):
            ctoks.append(p.emit("sync", lambda s, f=f: s.dma_start(out=b1t[:, f:f + 1], in_=ob1_h[f]), dma=True, ch="c6"))
        wtoks1 = []
        for k in range(NKC):
            wtoks1.append(p.emit("gpsimd", lambda g, k=k: g.dma_start(
                out=w1b[k][:, :], in_=ow1_h[k * 128:(k + 1) * 128, :]), dma=True, ch=f"wp1_{k}"))
        wtoks2 = []
        for f in range(NFC):
            wtoks2.append(p.emit("gpsimd", lambda g, f=f: g.dma_start(
                out=w2b[f][:, :], in_=ow2_h[f * 128:(f + 1) * 128, :]), dma=True, ch=f"wp2_{f}"))

        cur = list(ctoks)
        for seg in range(NSEG):
            soff = seg * SEGW
            o0done = []
            for ts in range(NTS):
                gsl = slice(soff + ts * TS, soff + (ts + 1) * TS)
                tsl = slice(ts * TS, (ts + 1) * TS)
                snorm = None
                for k in range(NKC):
                    l1 = p.emit("sync", lambda s, k=k, gsl=gsl: s.dma_start(
                        out=at1[:, :], in_=r_h[k * 128:(k + 1) * 128, gsl]), deps=cur + bk.war("at1"), dma=True, ch="ld1")
                    l2 = p.emit("sync", lambda s, k=k, gsl=gsl: s.dma_start(
                        out=at2[:, :], in_=rk_h[k * 128:(k + 1) * 128, gsl]), deps=cur + bk.war("at2"), dma=True, ch="ld2")
                    u1 = p.emit("vector", lambda v: v.tensor_mul(at3[:, :], at1[:, :], at2[:, :]),
                                deps=[l1, l2] + bk.war("at3"))
                    bk.read("at1", u1)
                    bk.read("at2", u1)
                    cb = p.emit("vector", lambda v: v.tensor_copy(sqt[:, :], at3[:, :]), deps=[u1] + bk.war("sqt"))
                    bk.read("at3", cb)
                    snorm = p.emit("tensor", lambda t, k=k: t.matmul(
                        P0[0:2, :], Et[:, :].bitcast(F32) if False else Et[:, :], at3[:, :],
                        start=True, stop=True), deps=[u1, ctoks[0]] + bk.war("P0"))
                    bk.read("at3", snorm)
                    q4 = p.emit("vector", lambda v: v.tensor_copy(nrow[:, :], P0[0:2, :]), deps=[snorm] + bk.war("nrow"))
                    bk.read("P0", q4)
                    q8 = p.emit("tensor", lambda t: t.matmul(P1[:, :], E2t[:, :], nrow[:, :], start=True, stop=True),
                                deps=[q4, ctoks[1]] + bk.war("P1"))
                    bk.read("nrow", q8)
                    l3 = p.emit("sync", lambda s, k=k, gsl=gsl: s.dma_start(
                        out=at1[:, :], in_=v_h[k * 128:(k + 1) * 128, gsl]), deps=[u1] + bk.war("at1"), dma=True, ch="ld1")
                    v1 = p.emit("vector", lambda v, k=k: v.tensor_scalar(
                        at2[:, :], at1[:, :], bmt[:, k:k + 1], None, ALU.mult), deps=[l3, u1] + bk.war("at2"))
                    bk.read("at1", v1)
                    v2 = p.emit("vector", lambda v: v.tensor_scalar_mul(at2[:, :], at2[:, :], 0.2), deps=[v1])
                    v3 = p.emit("vector", lambda v: v.tensor_mul(at2[:, :], at2[:, :], P1[:, :]), deps=[v2, q8])
                    bk.read("P1", v3)
                    l4 = p.emit("sync", lambda s, k=k, gsl=gsl: s.dma_start(
                        out=at3[:, :], in_=y_h[k * 128:(k + 1) * 128, gsl]), deps=[cb] + bk.war("at3"), dma=True, ch="ld3")
                    v4 = p.emit("vector", lambda v, k=k, tsl=tsl: v.tensor_add(
                        o0[k][:, tsl], at3[:, :], at2[:, :]), deps=[l4, v3] + bk.war(("o0", k, ts)))
                    bk.read("at3", v4)
                    bk.read("at2", v4)
                    o0done.append(v4)
            zn_done = []
            for ts in range(NTS):
                gsl = slice(soff + ts * TS, soff + (ts + 1) * TS)
                tsl = slice(ts * TS, (ts + 1) * TS)
                s1tok = s2tok = None
                for k in range(NKC):
                    cb = p.emit("vector", lambda v, k=k, tsl=tsl: v.tensor_copy(obt[:, :], o0[k][:, tsl]),
                                deps=o0done + [s1tok] + bk.war("obt"))
                    sq = p.emit("scalar", lambda s, k=k, tsl=tsl: s.activation(
                        out=sqt[:, :], in_=o0[k][:, tsl], func=AF.Square), deps=o0done + [s2tok] + bk.war("sqt"))
                    s1tok = p.emit("tensor", lambda t, k=k: t.matmul(
                        S0[:, :], ones_bf[:, :], obt[:, :], start=(k == 0), stop=(k == NKC - 1)),
                        deps=[cb, s1tok] + (bk.war("S0") if k == 0 else []))
                    bk.read("obt", s1tok)
                    s2tok = p.emit("tensor", lambda t, k=k: t.matmul(
                        S1[:, :], ones_bf[:, :], sqt[:, :], start=(k == 0), stop=(k == NKC - 1)),
                        deps=[sq, s2tok] + (bk.war("S1") if k == 0 else []))
                    bk.read("sqt", s2tok)
                t1 = p.emit("vector", lambda v: v.tensor_scalar_mul(mu[:, :], S0[:, :], 1.0 / D), deps=[s1tok])
                bk.read("S0", t1)
                t2 = p.emit("vector", lambda v: v.tensor_scalar_mul(srow[:, :], S1[:, :], 1.0 / D), deps=[s2tok])
                bk.read("S1", t2)
                t3 = p.emit("vector", lambda v: v.tensor_mul(srow2[:, :], mu[:, :], mu[:, :]), deps=[t1])
                t4 = p.emit("vector", lambda v: v.tensor_sub(srow[:, :], srow[:, :], srow2[:, :]), deps=[t2, t3])
                t4b = p.emit("vector", lambda v: v.tensor_scalar_add(srow[:, :], srow[:, :], 1e-6), deps=[t4])
                t5 = p.emit("scalar", lambda s: s.activation(out=srow[:, :], in_=srow[:, :], func=AF.Sqrt), deps=[t4b])
                t6 = p.emit("vector", lambda v: v.reciprocal(rstd[:, :], srow[:, :]), deps=[t5])
                t7 = p.emit("vector", lambda v: v.tensor_mul(murstd[:, :], mu[:, :], rstd[:, :]), slow=True, deps=[t6])
                b1 = p.emit("tensor", lambda t: t.matmul(P0[:, :], ones_f[:, :], rstd[:, :], start=True, stop=True),
                            deps=[t6] + bk.war("P0"))
                b2 = p.emit("tensor", lambda t: t.matmul(P1[:, :], ones_f[:, :], murstd[:, :], start=True, stop=True),
                            deps=[t7] + bk.war("P1"))
                mprev = []
                for k in range(NKC):
                    lg = p.emit("sync", lambda s, k=k, gsl=gsl: s.dma_start(
                        out=at1[:, :], in_=g_h[k * 128:(k + 1) * 128, gsl]), deps=mprev + bk.war("at1"), dma=True, ch="ld1")
                    m1 = p.emit("vector", lambda v, k=k, tsl=tsl: v.tensor_mul(at2[:, :], o0[k][:, tsl], P0[:, :]),
                                deps=[b1] + o0done + bk.war("at2"))
                    m2 = p.emit("vector", lambda v: v.tensor_sub(at2[:, :], at2[:, :], P1[:, :]), deps=[m1, b2])
                    m3 = p.emit("vector", lambda v, k=k: v.tensor_scalar(
                        at2[:, :], at2[:, :], lngt[:, k:k + 1], None, ALU.mult), deps=[m2, ctoks[2]])
                    m4 = p.emit("vector", lambda v, k=k: v.tensor_scalar(
                        at2[:, :], at2[:, :], lnbt[:, k:k + 1], None, ALU.add), deps=[m3])
                    m5 = p.emit("vector", lambda v, k=k, tsl=tsl: v.tensor_tensor(
                        zn[k][:, tsl], at2[:, :], at1[:, :], ALU.mult), deps=[m4, lg] + bk.war(("zn", k, ts)))
                    bk.read("at1", m5)
                    bk.read("at2", m5)
                    mprev = [m5]
                    zn_done.append(m5)
                bk.read("P0", mprev[0])
                bk.read("P1", mprev[0])
            hup = {}
            for f in range(NFC):
                for ts in range(NTS):
                    tsl = slice(ts * TS, (ts + 1) * TS)
                    hb = [P2, P3][ts % 2]
                    hkey = ("h", ts % 2)
                    mmt = None
                    for k in range(NKC):
                        mmt = p.emit("tensor", lambda t, k=k, f=f, tsl=tsl, hb=hb: t.matmul(
                            hb[:, :], w1b[k][:, f * 128:(f + 1) * 128], zn[k][:, tsl],
                            start=(k == 0), stop=(k == NKC - 1)),
                            deps=[wtoks1[k], mmt] + zn_done + (bk.war(hkey) if k == 0 else []))
                    ge = p.emit("scalar", lambda s, f=f, tsl=tsl, hb=hb: s.activation(
                        out=hx[f][:, tsl], in_=hb[:, :], func=AF.Gelu, bias=b1t[:, f:f + 1]),
                        deps=[mmt, ctoks[-1]] + bk.war(("hx", f, ts)))
                    bk.read(hkey, ge)
                    hup[(f, ts)] = ge
            stoks = []
            for c in range(NKC):
                for ts in range(NTS):
                    tsl = slice(ts * TS, (ts + 1) * TS)
                    gsl = slice(soff + ts * TS, soff + (ts + 1) * TS)
                    yb = [P4, P5][ts % 2]
                    ykey = ("y", ts % 2)
                    mmt = None
                    for f in range(NFC):
                        mmt = p.emit("tensor", lambda t, f=f, c=c, tsl=tsl, yb=yb: t.matmul(
                            yb[:, :], w2b[f][:, c * 128:(c + 1) * 128], hx[f][:, tsl],
                            start=(f == 0), stop=(f == NFC - 1)),
                            deps=[wtoks2[f], hup[(f, ts)], mmt] + (bk.war(ykey) if f == 0 else []))
                    lx = p.emit("sync", lambda s, c=c, gsl=gsl: s.dma_start(
                        out=at3[:, :], in_=x_h[c * 128:(c + 1) * 128, gsl]),
                        deps=bk.war("at3") + [mmt], dma=True, ch="ld3")
                    e1 = p.emit("vector", lambda v, c=c, yb=yb: v.tensor_scalar(
                        at2[:, :], yb[:, :], b2t[:, c:c + 1], None, ALU.add),
                        deps=[mmt, ctoks[5]] + bk.war("at2"))
                    bk.read(ykey, e1)
                    e2 = p.emit("vector", lambda v: v.tensor_add(at2[:, :], at2[:, :], at3[:, :]), deps=[e1, lx])
                    bk.read("at3", e2)
                    dr = p.emit("vector", lambda v: v.drain(), deps=[e2])
                    so = p.emit("gpsimd", lambda g, c=c, gsl=gsl: g.dma_start(
                        out=xo_h[c * 128:(c + 1) * 128, gsl], in_=at2[:, :]), deps=[dr], dma=True, ch="st")
                    bk.read("at2", so)
                    for f in range(NFC):
                        bk.read(("hx", f, ts), mmt)
                    stoks.append(so)
            for k in range(NKC):
                for ts in range(NTS):
                    bk.read(("o0", k, ts), stoks[-1])
                    bk.read(("zn", k, ts), stoks[-1])
            cur = stoks
        p.emit("gpsimd", lambda g: g.engine_nop(), deps=cur)

        with nc.Block() as block:
            p.realize(block)
    return nc


def run_dn_post(xT, yT, streams, I):
    nc = _get("dn_post", build_dn_post)
    EP, E0 = 1280, 1228

    def padW1(w):
        o = np.zeros((D, EP), np.float32); o[:, :E0] = w
        return o

    def padW2(w):
        o = np.zeros((EP, D), np.float32); o[:E0] = w
        return o

    def padb(b):
        o = np.zeros(EP, np.float32); o[:E0] = b
        return np.ascontiguousarray(o.reshape(10, 128, 1))

    Emat = np.zeros((128, 2), np.float32)
    Emat[:64, 0] = 1.0
    Emat[64:, 1] = 1.0
    base = {
        "oW1": padW1(np.asarray(I["L0_attn_oW1"])),
        "ob1": padb(np.asarray(I["L0_attn_ob1"])),
        "oW2": padW2(np.asarray(I["L0_attn_oW2"])),
        "ob2": np.ascontiguousarray(np.asarray(I["L0_attn_ob2"]).reshape(8, 128, 1)),
        "ln_g": np.ascontiguousarray(np.asarray(I["L0_attn_ln_g"]).reshape(8, 128, 1)),
        "ln_b": np.ascontiguousarray(np.asarray(I["L0_attn_ln_b"]).reshape(8, 128, 1)),
        "bonus_mult": np.ascontiguousarray(np.asarray(I["L0_attn_bonus_mult"]).reshape(8, 128, 1)),
        "Emat": Emat, "E2mat": np.ascontiguousarray(Emat.T),
    }
    res = run_bass_kernel_spmd(nc, [dict(base,
        x=np.ascontiguousarray(xT[:, c * TPC:(c + 1) * TPC]),
        y=np.ascontiguousarray(yT[:, c * TPC:(c + 1) * TPC]),
        r=np.ascontiguousarray(streams["r"][:, c * TPC:(c + 1) * TPC]),
        rk=np.ascontiguousarray(streams["rk"][:, c * TPC:(c + 1) * TPC]),
        v=np.ascontiguousarray(streams["v"][:, c * TPC:(c + 1) * TPC]),
        gate=np.ascontiguousarray(streams["gate"][:, c * TPC:(c + 1) * TPC]),
    ) for c in range(NCORE)], core_ids=list(range(NCORE)))
    return np.concatenate([r["xo"] for r in res.results], axis=1)


def build_qkv():
    TS = 512
    NTS4 = TPC // TS
    NKC = D // 128
    NFC = 3 * D // 128

    nc = bass.Bass("TRN2", target_bir_lowering=False)
    x_h = nc.dram_tensor("x", [D, TPC], F32, kind="ExternalInput").ap()
    w_h = nc.dram_tensor("w_qkv", [D, 3 * D], F32, kind="ExternalInput").ap()
    b_h = nc.dram_tensor("b_qkv", [NFC, 128, 1], F32, kind="ExternalInput").ap()
    o_h = nc.dram_tensor("o_qkv", [3 * D, TPC], F32, kind="ExternalOutput").ap()

    with ExitStack() as st:
        p = P(nc, st)
        bk = Banks()
        sb = lambda name, shape, dt=F32: st.enter_context(nc.sbuf_tensor(name, shape, dt))
        pt = lambda name, shape, dt=F32: st.enter_context(nc.psum_tensor(name, shape, dt))
        zn = [sb(f"zn{k}", [128, TPC], BF16) for k in range(NKC)]
        w1b = [sb(f"w1b{k}", [128, 3 * D], BF16) for k in range(NKC)]
        stage = sb("stage", [128, 3 * D], F32)
        xt = sb("xt", [128, TS], F32)
        xbt = sb("xbt", [128, TS], BF16)
        sqt = sb("sqt", [128, TS], BF16)
        tmp = sb("tmpf", [128, TS], F32)
        ones_bf = sb("ones_bf", [128, 1], BF16)
        ones_f = sb("ones_f", [1, 128], F32)
        mu = sb("mu", [1, TS], F32)
        rstd = sb("rstd", [1, TS], F32)
        murstd = sb("murstd", [1, TS], F32)
        srow = sb("srow", [1, TS], F32)
        srow2 = sb("srow2", [1, TS], F32)
        bt = sb("bt", [128, NFC], F32)
        P0 = pt("P0", [128, TS])
        P1 = pt("P1", [128, TS])
        P2 = pt("P2", [128, TS])
        P3 = pt("P3", [128, TS])
        S0 = pt("S0", [1, TS])
        S1 = pt("S1", [1, TS])
        p.emit("gpsimd", lambda g: g.memset(ones_bf[:, :], 1.0))
        p.emit("gpsimd", lambda g: g.memset(ones_f[:, :], 1.0))
        ctoks = []
        for f in range(NFC):
            ctoks.append(p.emit("sync", lambda s, f=f: s.dma_start(out=bt[:, f:f + 1], in_=b_h[f]), dma=True, ch="cb"))
        wtoks = []
        for k in range(NKC):
            wtoks.append(p.emit("gpsimd", lambda g, k=k: g.dma_start(
                out=w1b[k][:, :], in_=w_h[k * 128:(k + 1) * 128, :]),
                dma=True, ch=f"wq{k}"))
        xtoks = []
        zn_done = []
        for ts in range(NTS4):
            gsl = slice(ts * TS, (ts + 1) * TS)
            s1tok = s2tok = None
            for k in range(NKC):
                ld = p.emit("sync", lambda s, k=k, gsl=gsl: s.dma_start(
                    out=xt[:, :], in_=x_h[k * 128:(k + 1) * 128, gsl]), deps=xtoks, dma=True, ch="xt")
                cb = p.emit("vector", lambda v: v.tensor_copy(xbt[:, :], xt[:, :]), deps=[ld, s1tok])
                sq = p.emit("scalar", lambda s: s.activation(out=sqt[:, :], in_=xt[:, :], func=AF.Square), deps=[ld, s2tok])
                s1tok = p.emit("tensor", lambda t, k=k: t.matmul(
                    S0[:, :], ones_bf[:, :], xbt[:, :], start=(k == 0), stop=(k == NKC - 1)),
                    deps=[cb, s1tok] + (bk.war("S0") if k == 0 else []))
                s2tok = p.emit("tensor", lambda t, k=k: t.matmul(
                    S1[:, :], ones_bf[:, :], sqt[:, :], start=(k == 0), stop=(k == NKC - 1)),
                    deps=[sq, s2tok] + (bk.war("S1") if k == 0 else []))
                xtoks = [cb, sq]
            t1 = p.emit("vector", lambda v: v.tensor_scalar_mul(mu[:, :], S0[:, :], 1.0 / D), deps=[s1tok])
            bk.read("S0", t1)
            t2 = p.emit("vector", lambda v: v.tensor_scalar_mul(srow[:, :], S1[:, :], 1.0 / D), deps=[s2tok])
            bk.read("S1", t2)
            t3 = p.emit("vector", lambda v: v.tensor_mul(srow2[:, :], mu[:, :], mu[:, :]), deps=[t1])
            t4 = p.emit("vector", lambda v: v.tensor_sub(srow[:, :], srow[:, :], srow2[:, :]), deps=[t2, t3])
            t4b = p.emit("vector", lambda v: v.tensor_scalar_add(srow[:, :], srow[:, :], 1e-5), deps=[t4])
            t5 = p.emit("scalar", lambda s: s.activation(out=srow[:, :], in_=srow[:, :], func=AF.Sqrt), deps=[t4b])
            t6 = p.emit("vector", lambda v: v.reciprocal(rstd[:, :], srow[:, :]), deps=[t5])
            t7 = p.emit("vector", lambda v: v.tensor_mul(murstd[:, :], mu[:, :], rstd[:, :]), slow=True, deps=[t6])
            b1 = p.emit("tensor", lambda t: t.matmul(P0[:, :], ones_f[:, :], rstd[:, :], start=True, stop=True),
                        deps=[t6] + bk.war("P0"))
            b2 = p.emit("tensor", lambda t: t.matmul(P1[:, :], ones_f[:, :], murstd[:, :], start=True, stop=True),
                        deps=[t7] + bk.war("P1"))
            for k in range(NKC):
                ld = p.emit("sync", lambda s, k=k, gsl=gsl: s.dma_start(
                    out=xt[:, :], in_=x_h[k * 128:(k + 1) * 128, gsl]), deps=xtoks, dma=True, ch="xt")
                m1 = p.emit("vector", lambda v: v.tensor_mul(tmp[:, :], xt[:, :], P0[:, :]), deps=[ld, b1])
                m2 = p.emit("vector", lambda v, k=k, gsl=gsl: v.tensor_tensor(
                    zn[k][:, gsl], tmp[:, :], P1[:, :], ALU.subtract), deps=[m1, b2])
                xtoks = [m1, m2]
                zn_done.append(m2)
            bk.read("P0", xtoks[0])
            bk.read("P1", xtoks[1])
        stoks = []
        for f in range(NFC):
            for ts in range(NTS4):
                tsl = slice(ts * TS, (ts + 1) * TS)
                yb = [P2, P3][ts % 2]
                ykey = ("y", ts % 2)
                mmt = None
                for k in range(NKC):
                    mmt = p.emit("tensor", lambda t, k=k, f=f, tsl=tsl, yb=yb: t.matmul(
                        yb[:, :], w1b[k][:, f * 128:(f + 1) * 128], zn[k][:, tsl],
                        start=(k == 0), stop=(k == NKC - 1)),
                        deps=[wtoks[k], mmt] + zn_done + (bk.war(ykey) if k == 0 else []))
                e1 = p.emit("vector", lambda v, f=f, yb=yb: v.tensor_scalar(
                    tmp[:, :], yb[:, :], bt[:, f:f + 1], None, ALU.add),
                    deps=[mmt, ctoks[-1]] + bk.war("tmpf"))
                bk.read(ykey, e1)
                dr = p.emit("vector", lambda v: v.drain(), deps=[e1])
                so = p.emit("gpsimd", lambda g, f=f, tsl=tsl: g.dma_start(
                    out=o_h[f * 128:(f + 1) * 128, tsl], in_=tmp[:, :]), deps=[dr], dma=True, ch="st")
                bk.read("tmpf", so)
                stoks.append(so)
        p.emit("gpsimd", lambda g: g.engine_nop(), deps=stoks)
        with nc.Block() as block:
            p.realize(block)
    return nc


def run_qkv(xT, I):
    nc = _get("qkv", build_qkv)
    base = {
        "w_qkv": np.ascontiguousarray(np.asarray(I["L1_attn_w_qkv"])),
        "b_qkv": np.ascontiguousarray(np.asarray(I["L1_attn_b_qkv"]).reshape(24, 128, 1)),
    }
    res = run_bass_kernel_spmd(nc, [dict(base, x=np.ascontiguousarray(xT[:, c * TPC:(c + 1) * TPC]))
                                    for c in range(NCORE)], core_ids=list(range(NCORE)))
    return np.concatenate([r["o_qkv"] for r in res.results], axis=1)


def build_attn_core():
    NH = 4
    T = 8192
    QT = 128
    KB = 512
    NQT = T // QT
    GQ = 4
    SCALE = 0.125
    EOFF = -30.0

    nc = bass.Bass("TRN2", target_bir_lowering=False)
    q_h = nc.dram_tensor("q", [NH * 64, T], F32, kind="ExternalInput").ap()
    k_h = nc.dram_tensor("k", [NH * 64, T], F32, kind="ExternalInput").ap()
    v_h = nc.dram_tensor("v", [NH, T, 64], F32, kind="ExternalInput").ap()
    tm_h = nc.dram_tensor("tmask", [4, 4, 128, 128], BF16, kind="ExternalInput").ap()
    oy_h = nc.dram_tensor("o_y", [NH, T, 64], F32, kind="ExternalOutput").ap()

    with ExitStack() as st:
        p = P(nc, st)
        bk = Banks()
        sb = lambda name, shape, dt=F32: st.enter_context(nc.sbuf_tensor(name, shape, dt))
        pt = lambda name, shape, dt=F32: st.enter_context(nc.psum_tensor(name, shape, dt))
        kt = sb("kt", [64, T], BF16)
        vt = sb("vt", [128, T // 128 * 65], BF16)
        qt = [sb(f"qt{j}", [64, QT], BF16) for j in range(GQ)]
        esb = [sb(f"esb{j}", [128, KB], BF16) for j in range(GQ)]
        tmt = [[sb(f"tmt{i}_{s_}", [128, 128], BF16) for s_ in range(4)] for i in range(4)]
        Ot = [sb(f"Ot{j}", [128, 64], F32) for j in range(GQ)]
        linv = [sb(f"linv{j}", [128, 1], F32) for j in range(GQ)]
        eoff_t = sb("eoff_t", [128, 1], F32)
        sct_t = sb("sct_t", [128, 1], F32)
        PSs = [pt(f"PSs{i}", [128, 128]) for i in range(2)]
        PSo = [pt(f"PSo{j}", [128, 65]) for j in range(GQ)]

        ctoks = []
        for i in range(4):
            for s_ in range(4):
                ctoks.append(p.emit("sync", lambda s, i=i, s_=s_: s.dma_start(
                    out=tmt[i][s_][:, :], in_=tm_h[i, s_]), dma=True, ch="cm"))
        p.emit("gpsimd", lambda g: g.memset(eoff_t[:, :], EOFF))
        p.emit("gpsimd", lambda g: g.memset(sct_t[:, :], SCALE))
        cset = p.emit("gpsimd", lambda g: g.memset(vt[:, :], 1.0))

        all_st = []
        pair_tail = list(ctoks)
        for j in range(NH):
            hrows = slice(j * 64, (j + 1) * 64)
            ck = p.emit("gpsimd", lambda g, hrows=hrows: g.dma_start(
                out=kt[:, :], in_=k_h[hrows, :]), deps=pair_tail + bk.war("kt"), dma=True, ch="ldk")
            cv = p.emit("gpsimd", lambda g, j=j: g.dma_start(
                out=vt[:, :].rearrange("p (i d) -> p i d", d=65)[:, :, 0:64],
                in_=v_h[j].rearrange("(i p) d -> p i d", p=128)),
                deps=pair_tail + bk.war("vt") + [cset], dma=True, ch="ldv")
            pair_tail = []
            for qg in range(NQT // GQ):
                qis = [qg * GQ + jj for jj in range(GQ)]
                span = qg + 1
                ld_q = []
                for jj, qi in enumerate(qis):
                    qsl = slice(qi * QT, (qi + 1) * QT)
                    lq = p.emit("gpsimd", lambda g, hrows=hrows, qsl=qsl, jj=jj: g.dma_start(
                        out=qt[jj][:, :], in_=q_h[hrows, qsl]),
                        deps=bk.war(("qt", jj)), dma=True, ch=f"ldq{jj}")
                    ld_q.append(lq)
                avts = {jj: None for jj in range(GQ)}
                for kb in range(span):
                    diag = (kb == qg)
                    etoks = {}
                    for jj, qi in enumerate(qis):
                        for sub in range(4):
                            pb = (jj * 4 + sub) % 2
                            kof = kb * KB + sub * 128
                            m1 = p.emit("tensor", lambda t, jj=jj, pb=pb, kof=kof: t.matmul(
                                PSs[pb][:, :], kt[:, kof:kof + 128], qt[jj][:, :],
                                start=True, stop=True),
                                deps=[ld_q[jj], ck] + bk.war(("pss", pb)))
                            e1 = p.emit("scalar", lambda s, jj=jj, pb=pb, sub=sub: s.activation(
                                out=esb[jj][:, sub * 128:(sub + 1) * 128], in_=PSs[pb][:, :],
                                func=AF.Exp, bias=eoff_t[:, 0:1], scale=sct_t[:, 0:1]),
                                deps=[m1] + bk.war(("esb", jj, sub)))
                            bk.read(("pss", pb), e1)
                            if diag:
                                e2 = p.emit("vector", lambda v, jj=jj, sub=sub, qi=qi: v.tensor_mul(
                                    esb[jj][:, sub * 128:(sub + 1) * 128],
                                    esb[jj][:, sub * 128:(sub + 1) * 128],
                                    tmt[qi % 4][sub][:, :]), deps=[e1, ctoks[0]])
                            else:
                                e2 = e1
                            etoks[(jj, sub)] = e2
                    for jj, qi in enumerate(qis):
                        for sub in range(4):
                            avts[jj] = p.emit("tensor", lambda t, jj=jj, sub=sub, kb=kb: t.matmul(
                                PSo[jj][:, :], esb[jj][:, sub * 128:(sub + 1) * 128],
                                vt[:, (kb * 4 + sub) * 65:(kb * 4 + sub) * 65 + 65],
                                start=(kb == 0 and sub == 0), stop=(kb == span - 1 and sub == 3)),
                                deps=[etoks[(jj, sub)], cv, avts[jj]] + (bk.war(("pso", jj)) if kb == 0 and sub == 0 else []))
                            bk.read(("esb", jj, sub), avts[jj])
                for jj, qi in enumerate(qis):
                    qsl = slice(qi * QT, (qi + 1) * QT)
                    f1 = p.emit("vector", lambda v, jj=jj: v.reciprocal(linv[jj][:, :], PSo[jj][:, 64:65]),
                                slow=True, deps=[avts[jj]] + bk.war(("linv", jj)))
                    f2 = p.emit("vector", lambda v, jj=jj: v.tensor_scalar(
                        Ot[jj][:, :], PSo[jj][:, 0:64], linv[jj][:, 0:1], None, ALU.mult),
                        deps=[f1] + bk.war(("Ot", jj)))
                    bk.read(("pso", jj), f2)
                    bk.read(("linv", jj), f2)
                    dr = p.emit("vector", lambda v: v.drain(), deps=[f2])
                    so = p.emit("gpsimd", lambda g, j=j, qsl=qsl, jj=jj: g.dma_start(
                        out=oy_h[j, qsl, :], in_=Ot[jj][:, :]), deps=[dr], dma=True, ch=f"st{jj}")
                    bk.read(("Ot", jj), so)
                    all_st.append(so)
                    pair_tail.append(so)
            bk.read("kt", all_st[-1])
            bk.read("vt", all_st[-1])
            for jj in range(GQ):
                bk.read(("qt", jj), all_st[-1])
        p.emit("gpsimd", lambda g: g.engine_nop(), deps=all_st)
        with nc.Block() as block:
            p.realize(block)
    return nc


def run_attn_core(qkvT):
    nc = _get("attn_core", build_attn_core)
    import ml_dtypes
    tm = np.zeros((4, 4, 128, 128), ml_dtypes.bfloat16)
    for i in range(4):
        for s_ in range(4):
            kk = np.arange(128)[:, None] + s_ * 128
            q = np.arange(128)[None, :] + i * 128
            tm[i, s_] = (kk <= q).astype(ml_dtypes.bfloat16)
    in_maps = []
    for c in range(NCORE):
        m = {"tmask": tm}
        qs, ks, vs = [], [], []
        for pair in range(4 * c, 4 * c + 4):
            b, h = pair // 16, pair % 16
            sl = slice(b * 8192, (b + 1) * 8192)
            qs.append(qkvT[h * 64:(h + 1) * 64, sl])
            ks.append(qkvT[1024 + h * 64:1024 + (h + 1) * 64, sl])
            vs.append(np.ascontiguousarray(qkvT[2048 + h * 64:2048 + (h + 1) * 64, sl].T))
        m["q"] = np.ascontiguousarray(np.concatenate(qs, axis=0))
        m["k"] = np.ascontiguousarray(np.concatenate(ks, axis=0))
        m["v"] = np.ascontiguousarray(np.stack(vs, axis=0))
        in_maps.append(m)
    res = run_bass_kernel_spmd(nc, in_maps, core_ids=list(range(NCORE)))
    y = np.zeros((D, TOK), np.float32)
    for c in range(NCORE):
        for i, pair in enumerate(range(4 * c, 4 * c + 4)):
            b, h = pair // 16, pair % 16
            y[h * 64:(h + 1) * 64, b * 8192:(b + 1) * 8192] = res.results[c]["o_y"][i].T
    return y


def build_attn_post():
    TS = 512
    NTS4 = TPC // TS
    NKC = D // 128

    nc = bass.Bass("TRN2", target_bir_lowering=False)
    x_h = nc.dram_tensor("x", [D, TPC], F32, kind="ExternalInput").ap()
    y_h = nc.dram_tensor("y", [D, TPC], F32, kind="ExternalInput").ap()
    w_h = nc.dram_tensor("w_out", [D, D], F32, kind="ExternalInput").ap()
    b_h = nc.dram_tensor("b_out", [NKC, 128, 1], F32, kind="ExternalInput").ap()
    o_h = nc.dram_tensor("xo", [D, TPC], F32, kind="ExternalOutput").ap()

    with ExitStack() as st:
        p = P(nc, st)
        bk = Banks()
        sb = lambda name, shape, dt=F32: st.enter_context(nc.sbuf_tensor(name, shape, dt))
        pt = lambda name, shape, dt=F32: st.enter_context(nc.psum_tensor(name, shape, dt))
        yb = [sb(f"yb{k}", [128, TPC], BF16) for k in range(NKC)]
        w1b = [sb(f"w1b{k}", [128, D], BF16) for k in range(NKC)]
        stage = sb("stage", [128, D], F32)
        xt = sb("xt", [128, TS], F32)
        tmp = sb("tmpf", [128, TS], F32)
        bt = sb("bt", [128, NKC], F32)
        P2 = pt("P2", [128, TS])
        P3 = pt("P3", [128, TS])
        ctoks = []
        for f in range(NKC):
            ctoks.append(p.emit("sync", lambda s, f=f: s.dma_start(out=bt[:, f:f + 1], in_=b_h[f]), dma=True, ch="cb"))
        wtoks = []
        for k in range(NKC):
            wtoks.append(p.emit("gpsimd", lambda g, k=k: g.dma_start(
                out=w1b[k][:, :], in_=w_h[k * 128:(k + 1) * 128, :]),
                dma=True, ch=f"wq{k}"))
        ld_all = []
        ycv = None
        for k in range(NKC):
            ld = p.emit("sync", lambda s, k=k: s.dma_start(
                out=stage[:, :], in_=y_h[k * 128:(k + 1) * 128, 0:D]), deps=bk.war("stage"), dma=True, ch="wstage")
            ycv = p.emit("gpsimd", lambda g, k=k: g.tensor_copy(yb[k][:, 0:D], stage[:, :]), deps=[ld])
            bk.read("stage", ycv)
            ld2 = p.emit("sync", lambda s, k=k: s.dma_start(
                out=stage[:, :], in_=y_h[k * 128:(k + 1) * 128, D:TPC]), deps=bk.war("stage"), dma=True, ch="wstage")
            ycv = p.emit("gpsimd", lambda g, k=k: g.tensor_copy(yb[k][:, D:TPC], stage[:, :]), deps=[ld2])
            bk.read("stage", ycv)
            ld_all.append(ycv)
        stoks = []
        for f in range(NKC):
            for ts in range(NTS4):
                tsl = slice(ts * TS, (ts + 1) * TS)
                pb = [P2, P3][ts % 2]
                ykey = ("y", ts % 2)
                mmt = None
                for k in range(NKC):
                    mmt = p.emit("tensor", lambda t, k=k, f=f, tsl=tsl, pb=pb: t.matmul(
                        pb[:, :], w1b[k][:, f * 128:(f + 1) * 128], yb[k][:, tsl],
                        start=(k == 0), stop=(k == NKC - 1)),
                        deps=[wtoks[k], ld_all[k], mmt] + (bk.war(ykey) if k == 0 else []))
                lx = p.emit("sync", lambda s, f=f, tsl=tsl: s.dma_start(
                    out=xt[:, :], in_=x_h[f * 128:(f + 1) * 128, tsl]), deps=bk.war("xt") + [mmt], dma=True, ch="ldx")
                e1 = p.emit("vector", lambda v, f=f, pb=pb: v.tensor_scalar(
                    tmp[:, :], pb[:, :], bt[:, f:f + 1], None, ALU.add),
                    deps=[mmt, ctoks[-1]] + bk.war("tmpf"))
                bk.read(ykey, e1)
                e2 = p.emit("vector", lambda v: v.tensor_add(tmp[:, :], tmp[:, :], xt[:, :]), deps=[e1, lx])
                bk.read("xt", e2)
                dr = p.emit("vector", lambda v: v.drain(), deps=[e2])
                so = p.emit("gpsimd", lambda g, f=f, tsl=tsl: g.dma_start(
                    out=o_h[f * 128:(f + 1) * 128, tsl], in_=tmp[:, :]), deps=[dr], dma=True, ch="st")
                bk.read("tmpf", so)
                stoks.append(so)
        p.emit("gpsimd", lambda g: g.engine_nop(), deps=stoks)
        with nc.Block() as block:
            p.realize(block)
    return nc


def run_attn_post(xT, yT, I):
    nc = _get("attn_post", build_attn_post)
    base = {
        "w_out": np.ascontiguousarray(np.asarray(I["L1_attn_w_out"])),
        "b_out": np.ascontiguousarray(np.asarray(I["L1_attn_b_out"]).reshape(8, 128, 1)),
    }
    res = run_bass_kernel_spmd(nc, [dict(base,
        x=np.ascontiguousarray(xT[:, c * TPC:(c + 1) * TPC]),
        y=np.ascontiguousarray(yT[:, c * TPC:(c + 1) * TPC])) for c in range(NCORE)],
        core_ids=list(range(NCORE)))
    return np.concatenate([r["xo"] for r in res.results], axis=1)


def build_base_pre():
    TS = 512
    NTS4 = TPC // TS
    NKC = D // 128

    nc = bass.Bass("TRN2", target_bir_lowering=False)
    x_h = nc.dram_tensor("x", [D, TPC], F32, kind="ExternalInput").ap()
    cent_h = nc.dram_tensor("centT", [D, 8], F32, kind="ExternalInput").ap()
    lng_h = nc.dram_tensor("ln_g", [NKC, 128, 1], F32, kind="ExternalInput").ap()
    lnb_h = nc.dram_tensor("ln_b", [NKC, 128, 1], F32, kind="ExternalInput").ap()
    oz_h = nc.dram_tensor("o_z", [D, TPC], F32, kind="ExternalOutput").ap()
    on_h = nc.dram_tensor("o_n", [D, TPC], F32, kind="ExternalOutput").ap()
    osc_h = nc.dram_tensor("o_sc", [TPC, 8], F32, kind="ExternalOutput").ap()

    with ExitStack() as st:
        p = P(nc, st)
        bk = Banks()
        sb = lambda name, shape, dt=F32: st.enter_context(nc.sbuf_tensor(name, shape, dt))
        pt = lambda name, shape, dt=F32: st.enter_context(nc.psum_tensor(name, shape, dt))
        zt = [sb(f"zt{k}", [128, TPC], F32) for k in range(NKC)]
        nbf = [sb(f"nbf{k}", [128, TPC], BF16) for k in range(NKC)]
        xt = sb("xt", [128, TS], F32)
        xbt = sb("xbt", [128, TS], BF16)
        sqt = sb("sqt", [128, TS], BF16)
        tmp = sb("tmpf", [128, TS], F32)
        ones_bf = sb("ones_bf", [128, 1], BF16)
        ones_f = sb("ones_f", [1, 128], F32)
        mu = sb("mu", [1, TS], F32)
        rstd = sb("rstd", [1, TS], F32)
        murstd = sb("murstd", [1, TS], F32)
        srow = sb("srow", [1, TS], F32)
        srow2 = sb("srow2", [1, TS], F32)
        lngt = sb("lngt", [128, NKC], F32)
        lnbt = sb("lnbt", [128, NKC], F32)
        centb = sb("centb", [128, 64], BF16)
        centf = sb("centf", [128, 64], F32)
        sct = sb("sct", [128, 8], F32)
        P0 = pt("P0", [128, TS])
        P1 = pt("P1", [128, TS])
        P2 = pt("P2", [128, 8])
        S0 = pt("S0", [1, TS])
        S1 = pt("S1", [1, TS])
        p.emit("gpsimd", lambda g: g.memset(ones_bf[:, :], 1.0))
        p.emit("gpsimd", lambda g: g.memset(ones_f[:, :], 1.0))
        ctoks = []
        for k in range(NKC):
            ctoks.append(p.emit("sync", lambda s, k=k: s.dma_start(out=lngt[:, k:k + 1], in_=lng_h[k]), dma=True, ch="c1"))
            ctoks.append(p.emit("sync", lambda s, k=k: s.dma_start(out=lnbt[:, k:k + 1], in_=lnb_h[k]), dma=True, ch="c2"))
            ctoks.append(p.emit("sync", lambda s, k=k: s.dma_start(out=centf[:, k * 8:(k + 1) * 8], in_=cent_h[k * 128:(k + 1) * 128, :]), dma=True, ch="c3"))
        ccv = p.emit("vector", lambda v: v.tensor_copy(centb[:, :], centf[:, :]), deps=ctoks)

        def ln_pass(src_get, dst_write, eps, ts, deps0=()):
            s1tok = s2tok = None
            deps0 = list(deps0)
            for k in range(NKC):
                cb = p.emit("vector", lambda v, k=k: v.tensor_copy(xbt[:, :], src_get(k)), deps=[s1tok] + deps0)
                sq = p.emit("scalar", lambda s, k=k: s.activation(out=sqt[:, :], in_=src_get(k), func=AF.Square), deps=[s2tok] + deps0)
                s1tok = p.emit("tensor", lambda t, k=k: t.matmul(
                    S0[:, :], ones_bf[:, :], xbt[:, :], start=(k == 0), stop=(k == NKC - 1)),
                    deps=[cb, s1tok] + (bk.war("S0") if k == 0 else []))
                s2tok = p.emit("tensor", lambda t, k=k: t.matmul(
                    S1[:, :], ones_bf[:, :], sqt[:, :], start=(k == 0), stop=(k == NKC - 1)),
                    deps=[sq, s2tok] + (bk.war("S1") if k == 0 else []))
            t1 = p.emit("vector", lambda v: v.tensor_scalar_mul(mu[:, :], S0[:, :], 1.0 / D), deps=[s1tok])
            bk.read("S0", t1)
            t2 = p.emit("vector", lambda v: v.tensor_scalar_mul(srow[:, :], S1[:, :], 1.0 / D), deps=[s2tok])
            bk.read("S1", t2)
            t3 = p.emit("vector", lambda v: v.tensor_mul(srow2[:, :], mu[:, :], mu[:, :]), deps=[t1])
            t4 = p.emit("vector", lambda v: v.tensor_sub(srow[:, :], srow[:, :], srow2[:, :]), deps=[t2, t3])
            t4b = p.emit("vector", lambda v, eps=eps: v.tensor_scalar_add(srow[:, :], srow[:, :], eps), deps=[t4])
            t5 = p.emit("scalar", lambda s: s.activation(out=srow[:, :], in_=srow[:, :], func=AF.Sqrt), deps=[t4b])
            t6 = p.emit("vector", lambda v: v.reciprocal(rstd[:, :], srow[:, :]), deps=[t5])
            t7 = p.emit("vector", lambda v: v.tensor_mul(murstd[:, :], mu[:, :], rstd[:, :]), slow=True, deps=[t6])
            b1 = p.emit("tensor", lambda t: t.matmul(P0[:, :], ones_f[:, :], rstd[:, :], start=True, stop=True),
                        deps=[t6] + bk.war("P0"))
            b2 = p.emit("tensor", lambda t: t.matmul(P1[:, :], ones_f[:, :], murstd[:, :], start=True, stop=True),
                        deps=[t7] + bk.war("P1"))
            outs = []
            for k in range(NKC):
                m1 = p.emit("vector", lambda v, k=k: v.tensor_mul(tmp[:, :], src_get(k), P0[:, :]), deps=[b1] + bk.war("tmpf"))
                m2 = p.emit("vector", lambda v: v.tensor_sub(tmp[:, :], tmp[:, :], P1[:, :]), deps=[m1, b2])
                outs.append(dst_write(k, m2))
            bk.read("P0", outs[-1])
            bk.read("P1", outs[-1])
            return outs

        allst = []
        for ts in range(NTS4):
            gsl = slice(ts * TS, (ts + 1) * TS)
            lds = {}
            ldtoks = []
            for k in range(NKC):
                ldtoks.append(p.emit("sync", lambda s, k=k, gsl=gsl: s.dma_start(
                    out=zt[k][:, gsl], in_=x_h[k * 128:(k + 1) * 128, gsl]),
                    deps=bk.war(("zt", k)), dma=True, ch=f"ldx{k % 2}"))

            def src1(k, gsl=gsl, ldtoks=ldtoks):
                return zt[k][:, gsl]
            z_outs = []

            def dst1(k, m2, gsl=gsl):
                c1 = p.emit("vector", lambda v, k=k: v.tensor_copy(zt[k][:, gsl], tmp[:, :]), deps=[m2])
                bk.read("tmpf", c1)
                so = p.emit("gpsimd", lambda g, k=k: g.dma_start(
                    out=oz_h[k * 128:(k + 1) * 128, gsl], in_=zt[k][:, gsl]),
                    deps=[p.emit("vector", lambda v: v.drain(), deps=[c1])], dma=True, ch="stz")
                z_outs.append(so)
                return c1
            outs1 = ln_pass(src1, dst1, 1e-5, ts, deps0=ldtoks)

            def src2(k, gsl=gsl):
                return zt[k][:, gsl]

            n_outs = []

            def dst2(k, m2, gsl=gsl):
                m3 = p.emit("vector", lambda v, k=k: v.tensor_scalar(tmp[:, :], tmp[:, :], lngt[:, k:k + 1], None, ALU.mult), deps=[m2, ctoks[0]])
                m4 = p.emit("vector", lambda v, k=k: v.tensor_scalar(tmp[:, :], tmp[:, :], lnbt[:, k:k + 1], None, ALU.add), deps=[m3, ctoks[1]])
                c1 = p.emit("vector", lambda v, k=k: v.tensor_copy(nbf[k][:, gsl], tmp[:, :]), deps=[m4])
                bk.read("tmpf", c1)
                dr = p.emit("vector", lambda v: v.drain(), deps=[m4])
                so = p.emit("gpsimd", lambda g, k=k: g.dma_start(
                    out=on_h[k * 128:(k + 1) * 128, gsl], in_=tmp[:, :]), deps=[dr], dma=True, ch="stn")
                bk.read("tmpf", so)
                n_outs.append(so)
                return so
            outs2 = ln_pass(src2, dst2, 1e-5, ts, deps0=outs1)
            allst += z_outs + n_outs
        sc_st = []
        for tt in range(TPC // 128):
            ttl = slice(tt * 128, (tt + 1) * 128)
            mmt = None
            for k in range(NKC):
                mmt = p.emit("tensor", lambda t, k=k, ttl=ttl: t.matmul(
                    P2[:, :], nbf[k][:, ttl], centb[:, k * 8:(k + 1) * 8],
                    start=(k == 0), stop=(k == NKC - 1)),
                    deps=[ccv, mmt] + allst + (bk.war("P2") if k == 0 else []))
            cp = p.emit("vector", lambda v: v.tensor_copy(sct[:, :], P2[:, :]), deps=[mmt] + bk.war("sct"))
            bk.read("P2", cp)
            dr = p.emit("vector", lambda v: v.drain(), deps=[cp])
            so = p.emit("gpsimd", lambda g, ttl=ttl: g.dma_start(
                out=osc_h[ttl, :], in_=sct[:, :]), deps=[dr], dma=True, ch="stsc")
            bk.read("sct", so)
            sc_st.append(so)
        p.emit("gpsimd", lambda g: g.engine_nop(), deps=allst + sc_st)
        with nc.Block() as block:
            p.realize(block)
    return nc


def run_base_pre(xT, I):
    nc = _get("base_pre", build_base_pre)
    base = {
        "centT": np.ascontiguousarray(np.asarray(I["L1_ffn_centroids"]).T),
        "ln_g": np.ascontiguousarray(np.asarray(I["L1_ffn_ln_g"]).reshape(8, 128, 1)),
        "ln_b": np.ascontiguousarray(np.asarray(I["L1_ffn_ln_b"]).reshape(8, 128, 1)),
    }
    res = run_bass_kernel_spmd(nc, [dict(base, x=np.ascontiguousarray(xT[:, c * TPC:(c + 1) * TPC]))
                                    for c in range(NCORE)], core_ids=list(range(NCORE)))
    zT = np.concatenate([r["o_z"] for r in res.results], axis=1)
    nT = np.concatenate([r["o_n"] for r in res.results], axis=1)
    sc = np.concatenate([r["o_sc"] for r in res.results], axis=0)
    return zT, nT, sc


def build_base_assign():
    NTOK = TOK
    CCAP = NTOK // 8
    NIT = 28

    nc = bass.Bass("TRN2", target_bir_lowering=False)
    sc_h = nc.dram_tensor("scores8", [8, 128, 128], F32, kind="ExternalInput").ap()
    tri_h = nc.dram_tensor("triu_s", [128, 128], F32, kind="ExternalInput").ap()
    oe_h = nc.dram_tensor("o_e", [128, 128], F32, kind="ExternalOutput").ap()
    os_h = nc.dram_tensor("o_slot", [128, 128], F32, kind="ExternalOutput").ap()

    with ExitStack() as st:
        p = P(nc, st)
        bk = Banks()
        sb = lambda name, shape, dt=F32: st.enter_context(nc.sbuf_tensor(name, shape, dt))
        pt = lambda name, shape, dt=F32: st.enter_context(nc.psum_tensor(name, shape, dt))
        sc = [sb(f"sc{e}", [128, 128], F32) for e in range(8)]
        tk = sb("tk", [128, 128], F32)
        sm = sb("sm", [128, 128], F32)
        mk = sb("mk", [128, 128], F32)
        m2k = sb("m2k", [128, 128], F32)
        csA = sb("csA", [128, 128], F32)
        csB = sb("csB", [128, 128], F32)
        rankt = sb("rankt", [128, 128], F32)
        e_acc = sb("e_acc", [128, 128], F32)
        s_acc = sb("s_acc", [128, 128], F32)
        triu = sb("triu", [128, 128], F32)
        lo = sb("lo", [128, 1], F32)
        hi = sb("hi", [128, 1], F32)
        mid = sb("mid", [128, 1], F32)
        rsumc = sb("rsumc", [128, 1], F32)
        cnt = sb("cnt", [1, 1], F32)
        cge = sb("cge", [1, 1], F32)
        cbc = sb("cbc", [128, 1], F32)
        tmp1 = sb("tmp1", [128, 1], F32)
        need = sb("need", [128, 1], F32)
        rmax = sb("rmax", [128, 1], F32)
        rmaxb = sb("rmaxb", [128, 1], BF16)
        rowb = sb("rowb", [1, 128], F32)
        ones_f = sb("ones_f", [1, 128], F32)
        onesc = sb("onesc", [128, 1], F32)
        idnb = sb("idnb", [128, 128], BF16)
        PS0 = pt("PS0", [1, 128])
        PS1 = pt("PS1", [128, 128])
        PTR = pt("PTR", [128, 128], BF16)

        p.emit("gpsimd", lambda g: g.memset(ones_f[:, :], 1.0))
        p.emit("gpsimd", lambda g: g.memset(onesc[:, :], 1.0))
        zm = p.emit("gpsimd", lambda g: g.memset(idnb[:, :], 0.0))
        idt = p.emit("gpsimd", lambda g: g.affine_select(
            out=idnb[:, :], in_=idnb[:, :], compare_op=ALU.not_equal, fill=1.0, base=0,
            pattern=[[-1, 128]], channel_multiplier=1), deps=[zm])
        ctoks = [idt]
        ctoks.append(p.emit("sync", lambda s: s.dma_start(out=triu[:, :], in_=tri_h[:, :]), dma=True, ch="c0"))
        for e in range(8):
            ctoks.append(p.emit("sync", lambda s, e=e: s.dma_start(out=sc[e][:, :], in_=sc_h[e]), dma=True, ch="c1"))
        z1 = p.emit("vector", lambda v: v.memset(tk[:, :], 0.0))
        z2 = p.emit("vector", lambda v: v.memset(e_acc[:, :], 0.0))
        z3 = p.emit("vector", lambda v: v.memset(s_acc[:, :], 0.0))
        cur = ctoks + [z1, z2, z3]

        def count_gt(thr_tile, src_tile, deps):
            g1 = p.emit("vector", lambda v: v.tensor_scalar(mk[:, :], src_tile[:, :], thr_tile[:, 0:1], None, ALU.is_gt),
                        deps=deps + bk.war("mk"))
            g2 = p.emit("vector", lambda v: v.reduce_sum(rsumc[:, :], mk[:, :], axis=AX.X), slow=True, deps=[g1] + bk.war("rsumc"))
            g3 = p.emit("tensor", lambda t: t.matmul(PS0[0:1, 0:1], rsumc[:, :], onesc[:, :], start=True, stop=True),
                        deps=[g2] + bk.war("PS0"))
            g4 = p.emit("vector", lambda v: v.tensor_copy(cnt[:, :], PS0[0:1, 0:1]), deps=[g3] + bk.war("cnt"))
            bk.read("PS0", g4)
            bk.read("rsumc", g3)
            return g4, g1

        def bcast11(src11, dst, deps):
            b1 = p.emit("tensor", lambda t, dst=dst: t.matmul(PS1[:, 0:1], ones_f[:, :], src11[0:1, 0:1], start=True, stop=True),
                        deps=deps + bk.war("PS1"))
            b2 = p.emit("vector", lambda v, dst=dst: v.tensor_copy(dst[:, :], PS1[:, 0:1]), deps=[b1], slow=True)
            bk.read("PS1", b2)
            return b2

        def cumsum_excl(mask_tile, out_tile, deps):
            c1 = p.emit("vector", lambda v: v.tensor_copy(csA[:, :], mask_tile[:, :]), deps=deps + bk.war("csA"))
            srcb, dstb = csA, csB
            ct = c1
            for sh in (1, 2, 4, 8, 16, 32, 64):
                o1 = p.emit("vector", lambda v, sh=sh, srcb=srcb, dstb=dstb: v.tensor_copy(
                    dstb[:, 0:sh], srcb[:, 0:sh]), deps=[ct] + (bk.war("csB" if dstb is csB else "csA")))
                o2 = p.emit("vector", lambda v, sh=sh, srcb=srcb, dstb=dstb: v.tensor_add(
                    dstb[:, sh:128], srcb[:, sh:128], srcb[:, 0:128 - sh]), deps=[ct])
                ct = o2
                srcb, dstb = dstb, srcb
            c2 = p.emit("vector", lambda v, srcb=srcb: v.tensor_sub(out_tile[:, :], srcb[:, :], mask_tile[:, :]),
                        deps=[ct])
            c3 = p.emit("vector", lambda v, srcb=srcb: v.tensor_copy(rsumc[:, :], srcb[:, 127:128]),
                        deps=[ct] + bk.war("rsumc"))
            bk.read("csA", c3)
            bk.read("csB", c3)
            c4 = p.emit("tensor", lambda t: t.matmul(PS1[:, 0:1], triu[:, :], rsumc[:, :], start=True, stop=True),
                        deps=[c3, ctoks[1]] + bk.war("PS1"))
            bk.read("rsumc", c4)
            c5 = p.emit("vector", lambda v: v.tensor_copy(tmp1[:, :], PS1[:, 0:1]), deps=[c4] + bk.war("tmp1"), slow=True)
            bk.read("PS1", c5)
            c6 = p.emit("vector", lambda v: v.tensor_scalar(out_tile[:, :], out_tile[:, :], tmp1[:, 0:1], None, ALU.add),
                        deps=[c5, c2])
            bk.read("tmp1", c6)
            return c6

        for e in range(8):
            q0 = p.emit("vector", lambda v, e=e: v.scalar_tensor_tensor(
                sm[:, :], tk[:, :], -1e5, sc[e][:, :], ALU.mult, ALU.add), deps=cur + bk.war("sm"))
            q1 = p.emit("vector", lambda v: v.tensor_reduce(rmax[:, :], sm[:, :], AX.X, ALU.max, apply_absolute_value=True),
                        slow=True, deps=[q0] + bk.war("rmax"))
            q2 = p.emit("vector", lambda v: v.tensor_copy(rmaxb[:, :], rmax[:, :]), deps=[q1] + bk.war("rmaxb"))
            q3 = p.emit("tensor", lambda t: t.transpose(PTR[0:1, :], rmaxb[:, 0:1], idnb[:, :]),
                        deps=[q2, idt] + bk.war("PTR"))
            q4 = p.emit("vector", lambda v: v.tensor_copy(rowb[:, :], PTR[0:1, 0:128]), deps=[q3] + bk.war("rowb"))
            bk.read("PTR", q4)
            bk.read("rmaxb", q3)
            q5 = p.emit("vector", lambda v: v.reduce_max(cnt[:, :], rowb[:, :], axis=AX.X), slow=True, deps=[q4] + bk.war("cnt"))
            bk.read("rowb", q5)
            q6 = p.emit("vector", lambda v: v.tensor_scalar_add(cnt[:, :], cnt[:, :], 2.0), deps=[q5])
            q7 = bcast11(cnt, hi, [q6])
            q8 = p.emit("vector", lambda v: v.tensor_scalar_mul(lo[:, :], hi[:, :], -1.0), deps=[q7] + bk.war("lo"), slow=True)
            btok = [q7, q8]
            for it in range(NIT):
                i1 = p.emit("vector", lambda v: v.tensor_add(mid[:, :], lo[:, :], hi[:, :]), deps=btok + bk.war("mid"))
                i2 = p.emit("vector", lambda v: v.tensor_scalar_mul(mid[:, :], mid[:, :], 0.5), deps=[i1], slow=True)
                g4, _ = count_gt(mid, sm, [i2, q0])
                i3 = p.emit("vector", lambda v: v.tensor_scalar(cge[:, :], cnt[:, :], float(CCAP), None, ALU.is_ge),
                            deps=[g4] + bk.war("cge"))
                i4 = bcast11(cge, cbc, [i3])
                bk.read("cge", i4)
                i5 = p.emit("vector", lambda v: v.tensor_sub(tmp1[:, :], mid[:, :], lo[:, :]), deps=[i2, i4] + bk.war("tmp1"))
                i6 = p.emit("vector", lambda v: v.tensor_mul(tmp1[:, :], tmp1[:, :], cbc[:, :]), deps=[i5])
                i7 = p.emit("vector", lambda v: v.tensor_add(lo[:, :], lo[:, :], tmp1[:, :]), deps=[i6], slow=True)
                i8 = p.emit("vector", lambda v: v.tensor_sub(tmp1[:, :], mid[:, :], hi[:, :]), deps=[i7] + [i2])
                i9 = p.emit("vector", lambda v: v.tensor_scalar(cbc[:, :], cbc[:, :], -1.0, None, ALU.mult), deps=[i6])
                i9b = p.emit("vector", lambda v: v.tensor_scalar(cbc[:, :], cbc[:, :], 1.0, None, ALU.add), deps=[i9])
                i10 = p.emit("vector", lambda v: v.tensor_mul(tmp1[:, :], tmp1[:, :], cbc[:, :]), deps=[i8, i9b])
                i11 = p.emit("vector", lambda v: v.tensor_add(hi[:, :], hi[:, :], tmp1[:, :]), deps=[i10], slow=True)
                bk.read("tmp1", i11)
                bk.read("cbc", i10)
                btok = [i7, i11]
            g4, g1 = count_gt(hi, sm, btok)
            s1 = p.emit("vector", lambda v: v.tensor_copy(m2k[:, :], mk[:, :]), deps=[g1, g4] + bk.war("m2k"))
            bk.read("mk", s1)
            s2 = p.emit("vector", lambda v: v.tensor_scalar(need[:, :], cnt[:, :].rearrange("a b -> a b") if False else cnt[0:1, 0:1], -1.0, None, ALU.mult), deps=[g4] + bk.war("need")) if False else None
            s3 = p.emit("vector", lambda v: v.tensor_scalar(cge[:, :], cnt[:, :], -1.0, None, ALU.mult), deps=[g4] + bk.war("cge"))
            s4 = p.emit("vector", lambda v: v.tensor_scalar(cge[:, :], cge[:, :], float(CCAP), None, ALU.add), deps=[s3])
            s5 = bcast11(cge, need, [s4])
            bk.read("cge", s5)
            g4b, g1b = count_gt(lo, sm, [s1] + btok)
            s6 = p.emit("vector", lambda v: v.tensor_sub(mk[:, :], mk[:, :], m2k[:, :]), deps=[g1b, g4b, s1])
            s7 = cumsum_excl(mk, rankt, [s6] + bk.war("rankt"))
            s8 = p.emit("vector", lambda v: v.tensor_scalar(rankt[:, :], rankt[:, :], need[:, 0:1], None, ALU.is_lt),
                        deps=[s7, s5])
            bk.read("need", s8)
            s9 = p.emit("vector", lambda v: v.tensor_mul(mk[:, :], mk[:, :], rankt[:, :]), deps=[s8, s6])
            s10 = p.emit("vector", lambda v: v.tensor_add(m2k[:, :], m2k[:, :], mk[:, :]), deps=[s9, s1])
            bk.read("mk", s10)
            s11 = cumsum_excl(m2k, rankt, [s10] + bk.war("rankt"))
            s12 = p.emit("vector", lambda v: v.tensor_mul(rankt[:, :], rankt[:, :], m2k[:, :]), deps=[s11, s10])
            s13 = p.emit("vector", lambda v: v.tensor_add(s_acc[:, :], s_acc[:, :], rankt[:, :]), deps=[s12, z3])
            bk.read("rankt", s13)
            if e > 0:
                s14 = p.emit("vector", lambda v, e=e: v.tensor_scalar(csA[:, :], m2k[:, :], float(e), None, ALU.mult),
                             deps=[s10] + bk.war("csA"))
                s15 = p.emit("vector", lambda v: v.tensor_add(e_acc[:, :], e_acc[:, :], csA[:, :]), deps=[s14, z2])
                bk.read("csA", s15)
            else:
                s15 = z2
            s16 = p.emit("vector", lambda v: v.tensor_add(tk[:, :], tk[:, :], m2k[:, :]), deps=[s10, z1])
            bk.read("m2k", s16)
            bk.read("sm", s16)
            cur = [s13, s15, s16]
        dr = p.emit("vector", lambda v: v.drain(), deps=cur)
        st1 = p.emit("gpsimd", lambda g: g.dma_start(out=oe_h[:, :], in_=e_acc[:, :]), deps=[dr], dma=True, ch="st")
        st2 = p.emit("gpsimd", lambda g: g.dma_start(out=os_h[:, :], in_=s_acc[:, :]), deps=[dr], dma=True, ch="st")
        p.emit("gpsimd", lambda g: g.engine_nop(), deps=[st1, st2])
        with nc.Block() as block:
            p.realize(block)
    return nc


def run_base_assign(scores):
    nc = _get("base_assign", build_base_assign)
    sc8 = np.stack([np.ascontiguousarray(scores[:, e].reshape(128, 128).T) for e in range(8)])
    triu_s = np.triu(np.ones((128, 128), np.float32), 1)
    res = run_bass_kernel_spmd(nc, [{"scores8": sc8, "triu_s": triu_s}] * NCORE, core_ids=list(range(NCORE)))
    e_t = res.results[0]["o_e"].T.ravel()
    slot = res.results[0]["o_slot"].T.ravel()
    return e_t.astype(np.int64), slot.astype(np.int64)


def build_base_mlp():
    SEG = TPC
    TS = 512
    NTS = SEG // TS
    FF = 4096
    NKC = D // 128
    NFC = FF // 128
    GRP = 4

    nc = bass.Bass("TRN2", target_bir_lowering=False)
    n_h = nc.dram_tensor("n", [D, SEG], F32, kind="ExternalInput").ap()
    w1_h = nc.dram_tensor("w1", [D, FF], F32, kind="ExternalInput").ap()
    b1_h = nc.dram_tensor("b1", [NFC, 128, 1], F32, kind="ExternalInput").ap()
    w2_h = nc.dram_tensor("w2", [FF, D], F32, kind="ExternalInput").ap()
    b2_h = nc.dram_tensor("b2", [NKC, 128, 1], F32, kind="ExternalInput").ap()
    y_h = nc.dram_tensor("y", [D, SEG], F32, kind="ExternalOutput").ap()

    with ExitStack() as st:
        p = P(nc, st)
        bk = Banks()
        sb = lambda name, shape, dt=F32: st.enter_context(nc.sbuf_tensor(name, shape, dt))
        pt = lambda name, shape, dt=F32: st.enter_context(nc.psum_tensor(name, shape, dt))
        nb = [sb(f"nb{k}", [128, SEG], BF16) for k in range(NKC)]
        acc = [sb(f"acc{k}", [128, SEG], F32) for k in range(NKC)]
        w1b = [sb(f"w1b{k}", [128, FF], BF16) for k in range(NKC)]
        w2g = [sb(f"w2g{j}", [128, D], BF16) for j in range(2 * GRP)]
        stage = sb("stage", [128, 2048], F32)
        hgrp = [sb(f"h{g}", [128, SEG], BF16) for g in range(GRP)]
        b1t = sb("b1t", [128, NFC], F32)
        b2t = sb("b2t", [128, NKC], F32)
        ps_h = [pt(f"ps_h{i}", [128, TS]) for i in range(2)]
        ps_y = [pt(f"ps_y{i}", [128, TS]) for i in range(2)]

        ctoks = []
        for f in range(NFC):
            ctoks.append(p.emit("sync", lambda s, f=f: s.dma_start(out=b1t[:, f:f + 1], in_=b1_h[f]), dma=True, ch="cb1"))
        for k in range(NKC):
            ctoks.append(p.emit("sync", lambda s, k=k: s.dma_start(out=b2t[:, k:k + 1], in_=b2_h[k]), dma=True, ch="cb2"))
        ntoks = []
        for k in range(NKC):
            ntoks.append(p.emit("gpsimd", lambda g, k=k: g.dma_start(
                out=nb[k][:, :], in_=n_h[k * 128:(k + 1) * 128, :]),
                deps=ctoks[-1:], dma=True, ch=f"nst{k}"))
        wtoks = []
        for k in range(NKC):
            wtoks.append(p.emit("gpsimd", lambda g, k=k: g.dma_start(
                out=w1b[k][:, :], in_=w1_h[k * 128:(k + 1) * 128, :]),
                dma=True, ch=f"w1st{k}"))
        zacc = [p.emit("vector", lambda v, k=k: v.memset(acc[k][:, :], 0.0)) for k in range(NKC)]
        w2cv_last = wtoks[-1]
        new_tail = []
        for fg in range(NFC // GRP):
            w2toks = []
            for j in range(GRP):
                f = fg * GRP + j
                slot = (fg % 2) * GRP + j
                w2cv_last = p.emit("gpsimd", lambda g, f=f, slot=slot: g.dma_start(
                    out=w2g[slot][:, :], in_=w2_h[f * 128:(f + 1) * 128, :]),
                    deps=bk.war(("w2", slot)), dma=True, ch=f"w2st{slot}")
                w2toks.append(w2cv_last)
            htoks = {}
            for j in range(GRP):
                f = fg * GRP + j
                for ts_ in range(NTS):
                    tsl = slice(ts_ * TS, (ts_ + 1) * TS)
                    hb = ps_h[ts_ % 2]
                    hkey = ("h", ts_ % 2)
                    mmt = None
                    for k in range(NKC):
                        mmt = p.emit("tensor", lambda t, k=k, f=f, tsl=tsl, hb=hb: t.matmul(
                            hb[:, :], w1b[k][:, f * 128:(f + 1) * 128], nb[k][:, tsl],
                            start=(k == 0), stop=(k == NKC - 1)),
                            deps=[wtoks[k], ntoks[k], mmt] + (bk.war(hkey) if k == 0 else []))
                    ge = p.emit("scalar", lambda s, j=j, f=f, tsl=tsl, hb=hb: s.activation(
                        out=hgrp[j][:, tsl], in_=hb[:, :], func=AF.Gelu, bias=b1t[:, f:f + 1]),
                        deps=[mmt, ctoks[NFC - 1]] + bk.war(("hg", j, ts_)))
                    bk.read(hkey, ge)
                    htoks[(j, ts_)] = ge
            for c in range(NKC):
                for ts_ in range(NTS):
                    tsl = slice(ts_ * TS, (ts_ + 1) * TS)
                    yb = ps_y[ts_ % 2]
                    ykey = ("y", ts_ % 2)
                    mmt = None
                    for j in range(GRP):
                        slot = (fg % 2) * GRP + j
                        mmt = p.emit("tensor", lambda t, j=j, c=c, tsl=tsl, yb=yb, slot=slot: t.matmul(
                            yb[:, :], w2g[slot][:, c * 128:(c + 1) * 128], hgrp[j][:, tsl],
                            start=(j == 0), stop=(j == GRP - 1)),
                            deps=[w2toks[j], htoks[(j, ts_)], mmt] + (bk.war(ykey) if j == 0 else []))
                    fl = p.emit("vector", lambda v, c=c, tsl=tsl, yb=yb: v.tensor_add(
                        acc[c][:, tsl], acc[c][:, tsl], yb[:, :]), deps=[mmt, zacc[c]])
                    bk.read(ykey, fl)
                    for j in range(GRP):
                        bk.read(("w2", (fg % 2) * GRP + j), mmt)
                        bk.read(("hg", j, ts_), mmt)
                    new_tail.append(fl)
        stoks = []
        for c in range(NKC):
            e1 = p.emit("vector", lambda v, c=c: v.tensor_scalar(
                acc[c][:, :], acc[c][:, :], b2t[:, c:c + 1], None, ALU.add),
                deps=new_tail + [ctoks[-1]])
            dr = p.emit("vector", lambda v: v.drain(), deps=[e1])
            stoks.append(p.emit("gpsimd", lambda g, c=c: g.dma_start(
                out=y_h[c * 128:(c + 1) * 128, :], in_=acc[c][:, :]), deps=[dr], dma=True, ch="st"))
        p.emit("gpsimd", lambda g: g.engine_nop(), deps=stoks)
        with nc.Block() as block:
            p.realize(block)
    return nc


def run_base_mlp(n_gathered, I):
    nc = _get("base_mlp", build_base_mlp)
    in_maps = []
    for j in range(NCORE):
        in_maps.append({
            "n": np.ascontiguousarray(n_gathered[:, j * 2048:(j + 1) * 2048]),
            "w1": np.ascontiguousarray(np.asarray(I["L1_ffn_w1"][j])),
            "b1": np.ascontiguousarray(np.asarray(I["L1_ffn_b1"][j]).reshape(32, 128, 1)),
            "w2": np.ascontiguousarray(np.asarray(I["L1_ffn_w2"][j])),
            "b2": np.ascontiguousarray(np.asarray(I["L1_ffn_b2"][j]).reshape(8, 128, 1)),
        })
    res = run_bass_kernel_spmd(nc, in_maps, core_ids=list(range(NCORE)))
    return np.concatenate([r["y"] for r in res.results], axis=1)


def build_base_post():
    TS = 512
    NTS4 = TPC // TS
    NKC = D // 128
    nc = bass.Bass("TRN2", target_bir_lowering=False)
    x_h = nc.dram_tensor("x", [D, TPC], F32, kind="ExternalInput").ap()
    z_h = nc.dram_tensor("z", [D, TPC], F32, kind="ExternalInput").ap()
    y_h = nc.dram_tensor("y", [D, TPC], F32, kind="ExternalInput").ap()
    o_h = nc.dram_tensor("xo", [D, TPC], F32, kind="ExternalOutput").ap()
    with ExitStack() as st:
        p = P(nc, st)
        bk = Banks()
        sb = lambda name, shape, dt=F32: st.enter_context(nc.sbuf_tensor(name, shape, dt))
        a = sb("a", [128, TS], F32)
        b = sb("b", [128, TS], F32)
        c = sb("c", [128, TS], F32)
        stoks = []
        for k in range(NKC):
            for ts in range(NTS4):
                gsl = slice(ts * TS, (ts + 1) * TS)
                l1 = p.emit("sync", lambda s, k=k, gsl=gsl: s.dma_start(
                    out=a[:, :], in_=z_h[k * 128:(k + 1) * 128, gsl]), deps=bk.war("a"), dma=True, ch="l1")
                l2 = p.emit("sync", lambda s, k=k, gsl=gsl: s.dma_start(
                    out=b[:, :], in_=y_h[k * 128:(k + 1) * 128, gsl]), deps=bk.war("b"), dma=True, ch="l2")
                l3 = p.emit("sync", lambda s, k=k, gsl=gsl: s.dma_start(
                    out=c[:, :], in_=x_h[k * 128:(k + 1) * 128, gsl]), deps=bk.war("c"), dma=True, ch="l3")
                v1 = p.emit("vector", lambda v: v.tensor_add(a[:, :], a[:, :], b[:, :]), deps=[l1, l2])
                bk.read("b", v1)
                v2 = p.emit("vector", lambda v: v.scalar_tensor_tensor(
                    a[:, :], a[:, :], 0.5, c[:, :], ALU.mult, ALU.add), deps=[v1, l3])
                bk.read("c", v2)
                dr = p.emit("vector", lambda v: v.drain(), deps=[v2])
                so = p.emit("gpsimd", lambda g, k=k, gsl=gsl: g.dma_start(
                    out=o_h[k * 128:(k + 1) * 128, gsl], in_=a[:, :]), deps=[dr], dma=True, ch="st")
                bk.read("a", so)
                stoks.append(so)
        p.emit("gpsimd", lambda g: g.engine_nop(), deps=stoks)
        with nc.Block() as block:
            p.realize(block)
    return nc


def run_base_post(xT, zT, yT):
    nc = _get("base_post", build_base_post)
    res = run_bass_kernel_spmd(nc, [{
        "x": np.ascontiguousarray(xT[:, c * TPC:(c + 1) * TPC]),
        "z": np.ascontiguousarray(zT[:, c * TPC:(c + 1) * TPC]),
        "y": np.ascontiguousarray(yT[:, c * TPC:(c + 1) * TPC]),
    } for c in range(NCORE)], core_ids=list(range(NCORE)))
    return np.concatenate([r["xo"] for r in res.results], axis=1)


def run_l1_ffn(xT, I):
    zT, nT, scores = run_base_pre(xT, I)
    e_t, slot = run_base_assign(scores)
    pos = e_t * 2048 + slot
    token_at_pos = np.zeros(TOK, np.int64)
    token_at_pos[pos] = np.arange(TOK)
    n_g = np.ascontiguousarray(nT[:, token_at_pos])
    Y = run_base_mlp(n_g, I)
    yT = np.zeros_like(Y)
    yT[:, token_at_pos] = Y
    return run_base_post(xT, zT, yT)


# version 12
# speedup vs baseline: 1.1794x; 1.1794x over previous
import math
from contextlib import ExitStack

import numpy as np
import concourse.bass as bass
import concourse.mybir as mybir
from concourse.bass_utils import run_bass_kernel_spmd

F32 = mybir.dt.float32
BF16 = mybir.dt.bfloat16
AF = mybir.ActivationFunctionType
ALU = mybir.AluOpType
AX = mybir.AxisListType

NCORE = 8
D = 1024
TOK = 16384
TPC = TOK // NCORE

ENG = ("sync", "tensor", "vector", "scalar", "gpsimd")


class P:

    SERIALIZE = False
    SER_ENG = None
    FORCE_SLOW = False

    def __init__(self, nc, stack):
        self.nc = nc
        self.stack = stack
        self.sems = {e: stack.enter_context(nc.semaphore(f"prog_{e}")) for e in ENG}
        self.items = {e: [] for e in ENG}
        self.cum = {e: 0 for e in ENG}

    def _chan(self, ch):
        key = ("ch", ch)
        if key not in self.sems:
            self.sems[key] = self.stack.enter_context(self.nc.semaphore(f"ch_{ch}"))
            self.cum[key] = 0
        return key

    SER_SCOPE = None

    def emit(self, eng, fn, deps=(), dma=False, ch=None, slow=None):
        deps = list(deps)
        if P.SERIALIZE and getattr(self, "_last", None):
            deps.append(self._last)
        if P.SER_SCOPE and getattr(self, "scope", None) == P.SER_SCOPE and getattr(self, "_last", None):
            deps.append(self._last)
        if P.SER_ENG and getattr(self, "_last", None):
            if eng == P.SER_ENG or (self._last and self._last[0] == P.SER_ENG):
                deps.append(self._last)
        m = {}
        for d in deps:
            if d is None:
                continue
            k2, c2, was_dma = d[0], d[1], d[2]
            fast = d[3] if len(d) > 3 else False
            if k2 == eng and not was_dma and fast and not P.FORCE_SLOW:
                continue
            if m.get(k2, 0) < c2:
                m[k2] = c2
        if dma:
            assert ch is not None, "DMA emits need a channel"
            key = self._chan(ch)
            self.cum[key] += 16
            tok = (key, self.cum[key], True, False)
            self.items[eng].append((fn, m, 16, key))
        else:
            self.cum[eng] += 1
            if slow is None:
                slow = eng != "tensor"
            tok = (eng, self.cum[eng], False, not slow)
            self.items[eng].append((fn, m, 1, eng))
        self._last = tok
        return tok

    def realize(self, block):
        for e in ENG:
            items = self.items[e]
            if not items:
                continue

            def body(eng, items=items, e=e):
                water = {}
                for fn, m, inc, semkey in items:
                    for k2, c2 in m.items():
                        if water.get(k2, 0) < c2:
                            eng.wait_ge(self.sems[k2], c2)
                            water[k2] = c2
                    inst = fn(eng)
                    inst.then_inc(self.sems[semkey], inc)

            getattr(block, e)(body)


class Banks:

    def __init__(self):
        self.last = {}

    def war(self, key):
        t = self.last.get(key)
        return [t] if t is not None else []

    def read(self, key, tok):
        self.last[key] = tok


def make_identity(p, tile, sq, val=1.0):
    p.emit("gpsimd", lambda g: g.memset(tile, 0.0))
    return p.emit("gpsimd", lambda g: g.affine_select(
        out=tile, in_=tile, compare_op=ALU.not_equal, fill=val, base=0,
        pattern=[[-1, sq]], channel_multiplier=1))


def make_tri(p, tile, sq, val, kind, deps=()):
    cmp = {"low_incl": ALU.is_ge, "low_strict": ALU.is_gt,
           "up_incl": ALU.is_le, "up_strict": ALU.is_lt}[kind]
    p.emit("gpsimd", lambda g: g.memset(tile, val), deps=deps)
    return p.emit("gpsimd", lambda g: g.affine_select(
        out=tile, in_=tile, compare_op=cmp, fill=0.0, base=0,
        pattern=[[-1, sq]], channel_multiplier=1))


def build_ffn0(dbg=0, nexpert=9):
    SEG = 1024
    NSEG = TPC // SEG
    TS = 512
    NTS = SEG // TS
    FF = 4096
    NE = nexpert
    NKC = D // 128
    NFC = FF // 128
    GRP = 8

    nc = bass.Bass("TRN2", target_bir_lowering=False)
    if dbg:
        xn_h = nc.dram_tensor("dbg_xn", [D, TPC], BF16, kind="ExternalOutput").ap()
        g_h = nc.dram_tensor("dbg_gates", [128, 8 * (TPC // 128)], F32, kind="ExternalOutput").ap()
    if dbg >= 2:
        h_h = nc.dram_tensor("dbg_h", [128, 1024], BF16, kind="ExternalOutput").ap()
    if dbg >= 3:
        st_h = nc.dram_tensor("dbg_stats", [8, 2048], F32, kind="ExternalOutput").ap()
    if dbg:
        sc_h = nc.dram_tensor("dbg_scores", [128, 8 * (TPC // 128)], F32, kind="ExternalOutput").ap()
    if dbg >= 4:
        a1_h = nc.dram_tensor("dbg_acc_ln", [D, TPC], F32, kind="ExternalOutput").ap()
        a2_h = nc.dram_tensor("dbg_acc_exp", [D, TPC], F32, kind="ExternalOutput").ap()
    x_h = nc.dram_tensor("x", [D, TPC], F32, kind="ExternalInput").ap()
    su_h = nc.dram_tensor("shared_up", [D, FF], F32, kind="ExternalInput").ap()
    sd_h = nc.dram_tensor("shared_down", [FF, D], F32, kind="ExternalInput").ap()
    ru_h = nc.dram_tensor("routed_up", [8, D, FF], F32, kind="ExternalInput").ap()
    rd_h = nc.dram_tensor("routed_down", [8, FF, D], F32, kind="ExternalInput").ap()
    rw_h = nc.dram_tensor("router_w", [D, 8], F32, kind="ExternalInput").ap()
    xo_h = nc.dram_tensor("xo", [D, TPC], F32, kind="ExternalOutput").ap()

    with ExitStack() as st:
        p = P(nc, st)
        bk = Banks()
        sb = lambda name, shape, dt=F32: st.enter_context(nc.sbuf_tensor(name, shape, dt))

        xn = [sb(f"xn{k}", [128, SEG], BF16) for k in range(NKC)]
        acc = [sb(f"acc{k}", [128, SEG], F32) for k in range(NKC)]
        w1b = [sb(f"w1b{k}", [128, FF], BF16) for k in range(NKC)]
        w2g = [sb(f"w2g{j}", [128, D], BF16) for j in range(2 * GRP)]
        hgrp = [sb(f"h{g}", [128, SEG], BF16) for g in range(GRP)]
        xt2 = [sb(f"xt{i}", [128, TS], F32) for i in range(2)]
        xt = xt2[0]
        xbt2 = [sb(f"xbt{i}", [128, TS], BF16) for i in range(2)]
        xbt = xbt2[0]
        sqt2 = [sb(f"sqt{i}", [128, TS], BF16) for i in range(2)]
        sqt = sqt2[0]
        tmp = sb("tmpf", [128, TS], F32)
        ones_bf = sb("ones_bf", [128, 1], BF16)
        ones_f = sb("ones_f", [1, 128], F32)
        ones128 = sb("ones128", [128, 128], F32)
        idn128 = sb("idn128", [128, 128], F32)
        mu = sb("mu", [1, TS], F32)
        rstd = sb("rstd", [1, TS], F32)
        murstd = sb("murstd", [1, TS], F32)
        srow = sb("srow", [1, TS], F32)
        srow2 = sb("srow2", [1, TS], F32)
        rws = sb("rws", [128, 64], F32)
        rwsb = sb("rwsb", [128, 64], BF16)
        gates = sb("gates", [128, 8 * (SEG // 128)], F32)
        colbc = sb("colbc", [128, 128], F32)
        gate_bc = sb("gate_bc", [128, SEG], F32)
        prob = sb("prob", [128, 8], F32)
        pmax = sb("pmax", [128, 1], F32)
        pm2 = sb("pm2", [128, 1], F32)

        pt = lambda name, shape, dt=F32: st.enter_context(nc.psum_tensor(name, shape, dt))
        ps_s1 = pt("ps_s1", [1, TS])
        ps_s2 = pt("ps_s2", [1, TS])
        ps_b1 = pt("ps_b1", [128, TS])
        ps_b2 = pt("ps_b2", [128, TS])
        ps_h = [pt(f"ps_h{i}", [128, TS]) for i in range(2)]
        ps_y = [pt(f"ps_y{i}", [128, TS]) for i in range(2)]

        p.emit("gpsimd", lambda g: g.memset(ones_bf[:, :], 1.0))
        p.emit("gpsimd", lambda g: g.memset(ones_f[:, :], 1.0))
        p.emit("gpsimd", lambda g: g.memset(ones128[:, :], 1.0))
        make_identity(p, idn128[:, :], 128)

        rw_toks = []
        for k in range(NKC):
            ld = p.emit("sync", lambda s, k=k: s.dma_start(
                out=rws[:, k * 8:(k + 1) * 8], in_=rw_h[k * 128:(k + 1) * 128, :]), dma=True, ch="rws")
            rw_toks.append(ld)
        rw_cv = p.emit("vector", lambda v: v.tensor_copy(rwsb[:, :], rws[:, :]), deps=rw_toks)

        last_seg_stores = []
        dbg_toks = []
        for seg in range(NSEG):
            p.scope = "ln"
            xn_done = []
            if seg == 0:
                xtoks = []
            for ts in range(NTS):
                p.scope = "lnstats"
                tsl = slice(ts * TS, (ts + 1) * TS)
                gsl = slice(seg * SEG + ts * TS, seg * SEG + ts * TS + TS)
                s1tok = s2tok = None
                for k in range(NKC):
                    sl2 = k % 2
                    ld = p.emit("sync", lambda s, k=k, gsl=gsl, sl2=sl2: s.dma_start(
                        out=xt2[sl2][:, :], in_=x_h[k * 128:(k + 1) * 128, gsl]),
                        deps=(xtoks if k == 0 else []) + bk.war(("xt", sl2)), dma=True, ch=f"xt{sl2}")
                    cb = p.emit("vector", lambda v, sl2=sl2: v.tensor_copy(xbt2[sl2][:, :], xt2[sl2][:, :]),
                                deps=[ld, s1tok] + bk.war(("xbt", sl2)))
                    sq = p.emit("scalar", lambda s, sl2=sl2: s.activation(
                        out=sqt2[sl2][:, :], in_=xt2[sl2][:, :], func=AF.Square),
                        deps=[ld, s2tok] + bk.war(("sqt", sl2)))
                    bk.read(("xt", sl2), cb)
                    bk.read(("xt", sl2), sq)
                    s1tok = p.emit("tensor", lambda t, k=k, sl2=sl2: t.matmul(
                        ps_s1[:, :], ones_bf[:, :], xbt2[sl2][:, :], start=(k == 0), stop=(k == NKC - 1)),
                        deps=[cb, s1tok] + (bk.war("s1") if k == 0 else []))
                    bk.read(("xbt", sl2), s1tok)
                    s2tok = p.emit("tensor", lambda t, k=k, sl2=sl2: t.matmul(
                        ps_s2[:, :], ones_bf[:, :], sqt2[sl2][:, :], start=(k == 0), stop=(k == NKC - 1)),
                        deps=[sq, s2tok] + (bk.war("s2") if k == 0 else []))
                    bk.read(("sqt", sl2), s2tok)
                    xtoks = [cb, sq]
                p.scope = "lnmid"
                t1 = p.emit("vector", lambda v: v.tensor_scalar_mul(mu[:, :], ps_s1[:, :], 1.0 / D), deps=[s1tok])
                bk.read("s1", t1)
                t2 = p.emit("vector", lambda v: v.tensor_scalar_mul(srow[:, :], ps_s2[:, :], 1.0 / D), deps=[s2tok])
                bk.read("s2", t2)
                t3 = p.emit("vector", lambda v: v.tensor_mul(srow2[:, :], mu[:, :], mu[:, :]), deps=[t1])
                t4 = p.emit("vector", lambda v: v.tensor_sub(srow[:, :], srow[:, :], srow2[:, :]), deps=[t2, t3])
                t4b = p.emit("vector", lambda v: v.tensor_scalar_add(srow[:, :], srow[:, :], 1e-5), deps=[t4])
                t5 = p.emit("scalar", lambda s: s.activation(out=srow[:, :], in_=srow[:, :], func=AF.Sqrt), deps=[t4b])
                t6 = p.emit("vector", lambda v: v.reciprocal(rstd[:, :], srow[:, :]), deps=[t5])
                t7 = p.emit("vector", lambda v: v.tensor_mul(murstd[:, :], mu[:, :], rstd[:, :]), slow=True, deps=[t6])
                if dbg >= 3:
                    dbg_toks.append(p.emit("gpsimd", lambda g, seg=seg, ts=ts: g.dma_start(
                        out=st_h[seg * NTS + ts:seg * NTS + ts + 1, 0:TS], in_=mu[:, :]), deps=[t7], dma=True, ch="dbg"))
                    dbg_toks.append(p.emit("gpsimd", lambda g, seg=seg, ts=ts: g.dma_start(
                        out=st_h[4 + seg * NTS + ts:4 + seg * NTS + ts + 1, 0:TS], in_=rstd[:, :]), deps=[t7], dma=True, ch="dbg"))
                b1 = p.emit("tensor", lambda t: t.matmul(ps_b1[:, :], ones_f[:, :], rstd[:, :], start=True, stop=True),
                            deps=[t6] + bk.war("b1"))
                b2 = p.emit("tensor", lambda t: t.matmul(ps_b2[:, :], ones_f[:, :], murstd[:, :], start=True, stop=True),
                            deps=[t7] + bk.war("b2"))
                p.scope = "lnnorm"
                for k in range(NKC):
                    sl2 = k % 2
                    ld = p.emit("sync", lambda s, k=k, gsl=gsl, sl2=sl2: s.dma_start(
                        out=xt2[sl2][:, :], in_=x_h[k * 128:(k + 1) * 128, gsl]),
                        deps=(xtoks if k == 0 else []) + bk.war(("xt", sl2)), dma=True, ch=f"xt{sl2}")
                    m1 = p.emit("vector", lambda v, sl2=sl2: v.tensor_mul(tmp[:, :], xt2[sl2][:, :], ps_b1[:, :]), deps=[ld, b1])
                    bk.read(("xt", sl2), m1)
                    m2 = p.emit("vector", lambda v, k=k, tsl=tsl: v.tensor_sub(acc[k][:, tsl], tmp[:, :], ps_b2[:, :]), deps=[m1, b2])
                    m3 = p.emit("vector", lambda v, k=k, tsl=tsl: v.tensor_copy(xn[k][:, tsl], acc[k][:, tsl]), deps=[m2])
                    xtoks = [m1, m2, m3]
                    xn_done.append(m3)
                bk.read("b1", xtoks[0])
                bk.read("b2", xtoks[1])

            if dbg >= 4:
                for k in range(NKC):
                    dbg_toks.append(p.emit("gpsimd", lambda g, k=k, seg=seg: g.dma_start(
                        out=a1_h[k * 128:(k + 1) * 128, seg * SEG:(seg + 1) * SEG], in_=acc[k][:, :]),
                        deps=xn_done, dma=True, ch="dbg"))
            p.scope = "router"
            gate_done = []
            for tt in range(SEG // 128):
                ttl = slice(ts * 0 + tt * 128, tt * 128 + 128)
                mm = None
                for k in range(NKC):
                    mm = p.emit("tensor", lambda t, k=k, ttl=ttl: t.matmul(
                        ps_b1[0:128, 0:8], xn[k][:, ttl], rwsb[:, k * 8:(k + 1) * 8],
                        start=(k == 0), stop=(k == NKC - 1)),
                        deps=[rw_cv, mm] + xn_done + (bk.war("b1") if k == 0 else []))
                if dbg:
                    sccp = p.emit("vector", lambda v: v.tensor_copy(srow2[:, 0:8].tensor[0:128, 0:8] if False else colbc[:, 0:8], ps_b1[0:128, 0:8]), deps=[mm] + bk.war("colbc"))
                    bk.read("colbc", sccp)
                    drsc = p.emit("vector", lambda v: v.drain(), deps=[sccp])
                    dbg_toks.append(p.emit("gpsimd", lambda g, seg=seg, tt=tt: g.dma_start(
                        out=sc_h[:, (seg * (SEG // 128) + tt) * 8:(seg * (SEG // 128) + tt) * 8 + 8], in_=colbc[:, 0:8]),
                        deps=[drsc], dma=True, ch="dbg"))
                    bk.read("colbc", dbg_toks[-1])
                a0 = p.emit("vector", lambda v: v.tensor_copy(prob[:, :], ps_b1[0:128, 0:8]), deps=[mm])
                bk.read("b1", a0)
                a1 = p.emit("vector", lambda v: v.reduce_max(pmax[:, :], prob[:, :], axis=AX.X), deps=[a0])
                a2 = p.emit("vector", lambda v: v.tensor_scalar(prob[:, :], prob[:, :], pmax[:, 0:1], None, ALU.subtract), slow=True, deps=[a1])
                a3 = p.emit("scalar", lambda s: s.activation(out=prob[:, :], in_=prob[:, :], func=AF.Exp), deps=[a2])
                a4 = p.emit("vector", lambda v: v.reduce_sum(pmax[:, :], prob[:, :], axis=AX.X), deps=[a3])
                a5 = p.emit("vector", lambda v: v.reciprocal(pmax[:, :], pmax[:, :]), slow=True, deps=[a4])
                a6 = p.emit("vector", lambda v: v.tensor_scalar(prob[:, :], prob[:, :], pmax[:, 0:1], None, ALU.mult), deps=[a5])
                a7 = p.emit("vector", lambda v: v.reduce_max(pmax[:, :], prob[:, :], axis=AX.X), deps=[a6])
                a8 = p.emit("vector", lambda v: v.tensor_scalar(tmp[:, 0:8], prob[:, :], pmax[:, 0:1], None, ALU.is_lt), slow=True, deps=[a7])
                a9 = p.emit("vector", lambda v: v.tensor_mul(tmp[:, 0:8], prob[:, :], tmp[:, 0:8]), deps=[a8])
                a10 = p.emit("vector", lambda v: v.reduce_max(pm2[:, :], tmp[:, 0:8], axis=AX.X), deps=[a9])
                a11 = p.emit("vector", lambda v: v.tensor_scalar(tmp[:, 0:8], prob[:, :], pm2[:, 0:1], None, ALU.is_ge), slow=True, deps=[a10])
                a12 = p.emit("vector", lambda v, tt=tt: v.tensor_tensor(
                    gates[:, tt * 8:(tt + 1) * 8], prob[:, :], tmp[:, 0:8], ALU.mult), deps=[a11], slow=True)
                gate_done.append(a12)
            if dbg:
                for k in range(NKC):
                    dbg_toks.append(p.emit("gpsimd", lambda g, k=k, seg=seg: g.dma_start(
                        out=xn_h[k * 128:(k + 1) * 128, seg * SEG:(seg + 1) * SEG], in_=xn[k][:, :]),
                        deps=xn_done, dma=True, ch="dbg"))
                drg = p.emit("vector", lambda v: v.drain(), deps=gate_done)
                dbg_toks.append(p.emit("gpsimd", lambda g, seg=seg: g.dma_start(
                    out=g_h[:, seg * (8 * SEG // 128):(seg + 1) * (8 * SEG // 128)], in_=gates[:, :]),
                    deps=[drg], dma=True, ch="dbg"))

            p.scope = "wload"
            expert_tail = list(gate_done)
            if seg == 0:
                w1_ready = {}
            up_done_prev = list(gate_done)
            for e in range(NE):
                w1src = su_h if e == 0 else ru_h[e - 1]
                w2src = sd_h if e == 0 else rd_h[e - 1]
                if (seg, e) in w1_ready:
                    wtoks = w1_ready.pop((seg, e))
                else:
                    wtoks = []
                    for k in range(NKC):
                        wtoks.append(p.emit("gpsimd", lambda g, k=k, w1src=w1src: g.dma_start(
                            out=w1b[k][:, :], in_=w1src[k * 128:(k + 1) * 128, :]),
                            deps=up_done_prev + bk.war(("w1b", k)), dma=True, ch=f"w1c{k}"))
                gb_done = []
                if e > 0:
                    c2 = None
                    for tt in range(SEG // 128):
                        c1 = p.emit("vector", lambda v, tt=tt, e=e: v.tensor_scalar(
                            colbc[:, :], ones128[:, :], gates[:, tt * 8 + (e - 1):tt * 8 + e], None, ALU.mult),
                            deps=gate_done + bk.war("colbc"))
                        c2 = p.emit("tensor", lambda t: t.matmul(
                            ps_b2[:, 0:128], colbc[:, :], idn128[:, :], start=True, stop=True),
                            deps=[c1] + bk.war("b2"))
                        bk.read("colbc", c2)
                        c3 = p.emit("vector", lambda v, tt=tt: v.tensor_copy(
                            gate_bc[:, tt * 128:(tt + 1) * 128], ps_b2[:, 0:128]), deps=[c2])
                        bk.read("b2", c3)
                        gb_done.append(c3)
                new_tail = []
                up_done = []
                for fg in range(NFC // GRP):
                    p.scope = "wload"
                    w2toks = []
                    for j in range(GRP):
                        f = fg * GRP + j
                        slot = (fg % 2) * GRP + j
                        w2toks.append(p.emit("gpsimd", lambda g, f=f, w2src=w2src, slot=slot: g.dma_start(
                            out=w2g[slot][:, :], in_=w2src[f * 128:(f + 1) * 128, :]),
                            deps=bk.war(("w2", slot)), dma=True, ch=f"w2c{slot}"))
                    htoks = {}
                    p.scope = "up"
                    for j in range(GRP):
                        f = fg * GRP + j
                        for ts_ in range(NTS):
                            tsl = slice(ts_ * TS, (ts_ + 1) * TS)
                            hb = ps_h[(j * NTS + ts_) % 2]
                            hkey = ("h", (j * NTS + ts_) % 2)
                            mmt = None
                            for k in range(NKC):
                                mmt = p.emit("tensor", lambda t, k=k, f=f, tsl=tsl, hb=hb: t.matmul(
                                    hb[:, :], w1b[k][:, f * 128:(f + 1) * 128], xn[k][:, tsl],
                                    start=(k == 0), stop=(k == NKC - 1)),
                                    deps=[wtoks[k], mmt] + (bk.war(hkey) if k == 0 else []))
                            for k in range(NKC):
                                bk.read(("w1b", k), mmt)
                            up_done = [mmt]
                            ge = p.emit("scalar", lambda s, j=j, tsl=tsl, hb=hb: s.activation(
                                out=hgrp[j][:, tsl], in_=hb[:, :], func=AF.Gelu),
                                deps=[mmt] + bk.war(("hg", j, ts_)))
                            bk.read(hkey, ge)
                            htoks[(j, ts_)] = ge
                            if dbg >= 2 and seg == 0 and e == 0 and fg == 0 and j == 0:
                                dbg_toks.append(p.emit("gpsimd", lambda g, tsl=tsl: g.dma_start(
                                    out=h_h[:, tsl], in_=hgrp[0][:, tsl]), deps=[ge], dma=True, ch="dbg"))
                    p.scope = "down"
                    for c in range(NKC):
                        for ts_ in range(NTS):
                            tsl = slice(ts_ * TS, (ts_ + 1) * TS)
                            yb = ps_y[(c * NTS + ts_) % 2]
                            ykey = ("y", (c * NTS + ts_) % 2)
                            mmt = None
                            for j in range(GRP):
                                slot = (fg % 2) * GRP + j
                                mmt = p.emit("tensor", lambda t, j=j, c=c, tsl=tsl, yb=yb, slot=slot: t.matmul(
                                    yb[:, :], w2g[slot][:, c * 128:(c + 1) * 128], hgrp[j][:, tsl],
                                    start=(j == 0), stop=(j == GRP - 1)),
                                    deps=[w2toks[j], htoks[(j, ts_)], mmt] + (bk.war(ykey) if j == 0 else []))
                            if e == 0:
                                fl = p.emit("vector", lambda v, c=c, tsl=tsl, yb=yb: v.tensor_add(
                                    acc[c][:, tsl], acc[c][:, tsl], yb[:, :]), deps=[mmt] + xn_done)
                            else:
                                t_ = p.emit("vector", lambda v, tsl=tsl, yb=yb: v.tensor_mul(
                                    tmp[:, :], yb[:, :], gate_bc[:, tsl]), deps=[mmt] + gb_done)
                                fl = p.emit("vector", lambda v, c=c, tsl=tsl: v.tensor_add(
                                    acc[c][:, tsl], acc[c][:, tsl], tmp[:, :]), deps=[t_])
                            bk.read(ykey, fl)
                            for j in range(GRP):
                                bk.read(("w2", (fg % 2) * GRP + j), mmt)
                                bk.read(("hg", j, ts_), mmt)
                            new_tail.append(fl)
                up_done_prev = up_done
                if e + 1 < NE:
                    nsrc = ru_h[e]
                    toks = []
                    for k in range(NKC):
                        toks.append(p.emit("gpsimd", lambda g, k=k, nsrc=nsrc: g.dma_start(
                            out=w1b[k][:, :], in_=nsrc[k * 128:(k + 1) * 128, :]),
                            deps=up_done + bk.war(("w1b", k)), dma=True, ch=f"w1c{k}"))
                    w1_ready[(seg, e + 1)] = toks
                elif seg + 1 < NSEG:
                    toks = []
                    for k in range(NKC):
                        toks.append(p.emit("gpsimd", lambda g, k=k: g.dma_start(
                            out=w1b[k][:, :], in_=su_h[k * 128:(k + 1) * 128, :]),
                            deps=up_done + bk.war(("w1b", k)), dma=True, ch=f"w1c{k}"))
                    w1_ready[(seg + 1, 0)] = toks
                expert_tail = new_tail[-NKC * NTS:]

            if dbg >= 4:
                for k in range(NKC):
                    dbg_toks.append(p.emit("gpsimd", lambda g, k=k, seg=seg: g.dma_start(
                        out=a2_h[k * 128:(k + 1) * 128, seg * SEG:(seg + 1) * SEG], in_=acc[k][:, :]),
                        deps=expert_tail, dma=True, ch="dbg"))
            p.scope = "out"
            stoks = []
            ldprev = []
            for k in range(NKC):
                for ts_ in range(NTS):
                    tsl = slice(ts_ * TS, (ts_ + 1) * TS)
                    gsl = slice(seg * SEG + ts_ * TS, seg * SEG + ts_ * TS + TS)
                    ld = p.emit("sync", lambda s, k=k, gsl=gsl: s.dma_start(
                        out=xt2[0][:, :], in_=x_h[k * 128:(k + 1) * 128, gsl]),
                        deps=expert_tail + ldprev + bk.war(("xt", 0)), dma=True, ch="xt0")
                    f1 = p.emit("vector", lambda v, k=k, tsl=tsl: v.scalar_tensor_tensor(
                        acc[k][:, tsl], acc[k][:, tsl], 0.5, xt2[0][:, :], ALU.mult, ALU.add),
                        deps=[ld] + expert_tail + (dbg_toks[-NKC:] if dbg >= 4 else []))
                    ldprev = [f1]
                    bk.read(("xt", 0), f1)
                    dr = p.emit("vector", lambda v: v.drain(), deps=[f1])
                    stoks.append(p.emit("gpsimd", lambda g, k=k, tsl=tsl, gsl=gsl: g.dma_start(
                        out=xo_h[k * 128:(k + 1) * 128, gsl], in_=acc[k][:, tsl]), deps=[dr], dma=True, ch="st"))
            last_seg_stores = stoks
            xtoks = stoks + [ldprev[0]]

        p.emit("gpsimd", lambda g: g.engine_nop(), deps=last_seg_stores + dbg_toks)

        with nc.Block() as block:
            p.realize(block)
    return nc


def _shard_fm(xT):
    return [np.ascontiguousarray(xT[:, c * TPC:(c + 1) * TPC]) for c in range(NCORE)]


_cache = {}


def _get(name, fn):
    if name not in _cache:
        _cache[name] = fn()
    return _cache[name]


def run_ffn0(xT, I):
    nc = _get("ffn0", build_ffn0)
    shards = _shard_fm(xT)
    base = {
        "shared_up": np.ascontiguousarray(I["L0_ffn_shared_up"][0]),
        "shared_down": np.ascontiguousarray(I["L0_ffn_shared_down"][0]),
        "routed_up": np.ascontiguousarray(I["L0_ffn_routed_up"]),
        "routed_down": np.ascontiguousarray(I["L0_ffn_routed_down"]),
        "router_w": np.ascontiguousarray(I["L0_ffn_router_w"]),
    }
    in_maps = [dict(base, x=s) for s in shards]
    res = run_bass_kernel_spmd(nc, in_maps, core_ids=list(range(NCORE)))
    return np.concatenate([r["xo"] for r in res.results], axis=1)


def kernel(**inputs):
    I = inputs
    _ = (I.get("L0_attn_keys"), I.get("L0_attn_values"), I.get("L0_attn_attn_mask"))
    x = np.asarray(I["L0_attn_queries"], dtype=np.float32).reshape(TOK, D)
    xT = np.ascontiguousarray(x.T)
    xT = run_ffn0(xT, I)
    streams = run_dn_proj(xT, I)
    scan_in = {"w": streams["o_w"], "at": streams["o_at"], "kn": streams["o_kn"],
               "rk": streams["o_rk"], "v": streams["o_v"], "r": streams["o_r"]}
    yT = run_dn_scan(scan_in)
    xT = run_dn_post(xT, yT, {"r": streams["o_r"], "rk": streams["o_rk"],
                              "v": streams["o_v"], "gate": streams["o_gate"]}, I)
    xT = run_ffn0(xT, I)
    xT = run_l1_ffn(xT, I)
    qkvT = run_qkv(xT, I)
    yT = run_attn_core(qkvT)
    xT = run_attn_post(xT, yT, I)
    xT = run_l1_ffn(xT, I)
    return np.ascontiguousarray(xT.T).reshape(2, 8192, D)


def build_dn_proj():
    TS = 512
    SEGW = 1024
    NSEG = TPC // SEGW
    NTS = SEGW // TS
    NTS4 = TPC // TS
    NKC = D // 128
    EP = 1280
    NFC = EP // 128
    LR = 128

    nc = bass.Bass("TRN2", target_bir_lowering=False)
    w1_h = {}
    w2_h = {}
    b1_h = {}
    b2_h = {}
    x_h = nc.dram_tensor("x", [D, TPC], F32, kind="ExternalInput").ap()
    for pth in ("r", "k", "v"):
        w1_h[pth] = nc.dram_tensor(f"{pth}W1", [D, EP], F32, kind="ExternalInput").ap()
        b1_h[pth] = nc.dram_tensor(f"{pth}b1", [NFC, 128, 1], F32, kind="ExternalInput").ap()
        w2_h[pth] = nc.dram_tensor(f"{pth}W2", [EP, D], F32, kind="ExternalInput").ap()
        b2_h[pth] = nc.dram_tensor(f"{pth}b2", [NKC, 128, 1], F32, kind="ExternalInput").ap()
    lrA_h = {}
    lrB_h = {}
    lrb_h = {}
    for pth in ("g", "a", "d"):
        lrA_h[pth] = nc.dram_tensor(f"{pth}A", [D, LR], F32, kind="ExternalInput").ap()
        lrB_h[pth] = nc.dram_tensor(f"{pth}B", [LR, D], F32, kind="ExternalInput").ap()
        lrb_h[pth] = nc.dram_tensor(f"{pth}b", [NKC, 128, 1], F32, kind="ExternalInput").ap()
    mu_h = nc.dram_tensor("mus", [6, NKC, 128, 1], F32, kind="ExternalInput").ap()
    rem_h = nc.dram_tensor("rem_mult", [NKC, 128, 1], F32, kind="ExternalInput").ap()
    mix_h = nc.dram_tensor("iclr_mix", [NKC, 128, 1], F32, kind="ExternalInput").ap()
    E_h = nc.dram_tensor("Emat", [128, 2], F32, kind="ExternalInput").ap()
    E2_h = nc.dram_tensor("E2mat", [2, 128], F32, kind="ExternalInput").ap()
    outs = {}
    for o in ("o_r", "o_v", "o_kn", "o_at", "o_rk", "o_gate", "o_w"):
        outs[o] = nc.dram_tensor(o, [D, TPC], F32, kind="ExternalOutput").ap()
    zbf_h = nc.dram_tensor("zbf_scr", [D, TPC], BF16, kind="Internal").ap()
    iclr_h = nc.dram_tensor("iclr_scr", [D, TPC], F32, kind="Internal").ap()

    with ExitStack() as st:
        p = P(nc, st)
        bk = Banks()
        sb = lambda name, shape, dt=F32: st.enter_context(nc.sbuf_tensor(name, shape, dt))
        pt = lambda name, shape, dt=F32: st.enter_context(nc.psum_tensor(name, shape, dt))

        zmu = [sb(f"zmu{k}", [128, SEGW], BF16) for k in range(NKC)]
        hx = [sb(f"hx{f}", [128, SEGW], BF16) for f in range(NFC)]
        w1b = [sb(f"w1b{k}", [128, EP], BF16) for k in range(NKC)]
        w2b = [sb(f"w2b{f}", [128, D], BF16) for f in range(NFC)]
        out = [sb(f"out{k}", [128, SEGW], F32) for k in range(NKC)]
        stage = sb("stage", [128, EP], F32)
        xt = sb("xt", [128, TS], F32)
        xbt = sb("xbt", [128, TS], BF16)
        sqt = sb("sqt", [128, TS], BF16)
        tmp = sb("tmpf", [128, TS], F32)
        tmp2 = sb("tmp2", [128, TS], F32)
        tmp3 = sb("tmp3", [128, TS], F32)
        zbt = sb("zbt", [128, TS], BF16)
        itile = sb("itile", [128, TS], F32)
        ones_bf = sb("ones_bf", [128, 1], BF16)
        ones_f = sb("ones_f", [1, 128], F32)
        mu = sb("mu", [1, TS], F32)
        rstd = sb("rstd", [1, TS], F32)
        murstd = sb("murstd", [1, TS], F32)
        srow = sb("srow", [1, TS], F32)
        srow2 = sb("srow2", [1, TS], F32)
        mus = sb("mus_t", [128, 48], F32)
        remt = sb("remt", [128, 8], F32)
        mixt = sb("mixt", [128, 8], F32)
        b1t = sb("b1t", [128, NFC], F32)
        b2t = sb("b2t", [128, NKC], F32)
        Et = sb("Et", [128, 2], F32)
        E2t = sb("E2t", [2, 128], F32)
        nrow = sb("nrow", [2, TS], F32)

        P0 = pt("P0", [128, TS])
        P1 = pt("P1", [128, TS])
        P2 = pt("P2", [128, TS])
        P3 = pt("P3", [128, TS])
        P4 = pt("P4", [128, TS])
        P5 = pt("P5", [128, TS])
        S0 = pt("S0", [1, TS])
        S1 = pt("S1", [1, TS])

        p.emit("gpsimd", lambda g: g.memset(ones_bf[:, :], 1.0))
        p.emit("gpsimd", lambda g: g.memset(ones_f[:, :], 1.0))
        ctoks = []
        ctoks.append(p.emit("sync", lambda s: s.dma_start(out=Et[:, :], in_=E_h[:, :]), dma=True, ch="c0"))
        ctoks.append(p.emit("sync", lambda s: s.dma_start(out=E2t[:, :], in_=E2_h[:, :]), dma=True, ch="c1"))
        for pi in range(6):
            for k in range(NKC):
                ctoks.append(p.emit("sync", lambda s, pi=pi, k=k: s.dma_start(
                    out=mus[:, pi * 8 + k:pi * 8 + k + 1], in_=mu_h[pi, k]), dma=True, ch="c2"))
        mu_all = list(ctoks)
        cextra = []
        for k in range(NKC):
            cextra.append(p.emit("sync", lambda s, k=k: s.dma_start(
                out=remt[:, k:k + 1], in_=rem_h[k]), dma=True, ch="c3"))
            cextra.append(p.emit("sync", lambda s, k=k: s.dma_start(
                out=mixt[:, k:k + 1], in_=mix_h[k]), dma=True, ch="c4"))
        ctoks += cextra

        xtoks = []
        zst = []
        for ts in range(NTS4):
            gsl = slice(ts * TS, (ts + 1) * TS)
            s1tok = s2tok = None
            for k in range(NKC):
                ld = p.emit("sync", lambda s, k=k, gsl=gsl: s.dma_start(
                    out=xt[:, :], in_=x_h[k * 128:(k + 1) * 128, gsl]), deps=xtoks, dma=True, ch="xt")
                cb = p.emit("vector", lambda v: v.tensor_copy(xbt[:, :], xt[:, :]), deps=[ld, s1tok])
                sq = p.emit("scalar", lambda s: s.activation(out=sqt[:, :], in_=xt[:, :], func=AF.Square), deps=[ld, s2tok])
                s1tok = p.emit("tensor", lambda t, k=k: t.matmul(
                    S0[:, :], ones_bf[:, :], xbt[:, :], start=(k == 0), stop=(k == NKC - 1)),
                    deps=[cb, s1tok] + (bk.war("S0") if k == 0 else []))
                s2tok = p.emit("tensor", lambda t, k=k: t.matmul(
                    S1[:, :], ones_bf[:, :], sqt[:, :], start=(k == 0), stop=(k == NKC - 1)),
                    deps=[sq, s2tok] + (bk.war("S1") if k == 0 else []))
                xtoks = [cb, sq]
            t1 = p.emit("vector", lambda v: v.tensor_scalar_mul(mu[:, :], S0[:, :], 1.0 / D), deps=[s1tok])
            bk.read("S0", t1)
            t2 = p.emit("vector", lambda v: v.tensor_scalar_mul(srow[:, :], S1[:, :], 1.0 / D), deps=[s2tok])
            bk.read("S1", t2)
            t3 = p.emit("vector", lambda v: v.tensor_mul(srow2[:, :], mu[:, :], mu[:, :]), deps=[t1])
            t4 = p.emit("vector", lambda v: v.tensor_sub(srow[:, :], srow[:, :], srow2[:, :]), deps=[t2, t3])
            t4b = p.emit("vector", lambda v: v.tensor_scalar_add(srow[:, :], srow[:, :], 1e-5), deps=[t4])
            t5 = p.emit("scalar", lambda s: s.activation(out=srow[:, :], in_=srow[:, :], func=AF.Sqrt), deps=[t4b])
            t6 = p.emit("vector", lambda v: v.reciprocal(rstd[:, :], srow[:, :]), deps=[t5])
            t7 = p.emit("vector", lambda v: v.tensor_mul(murstd[:, :], mu[:, :], rstd[:, :]), slow=True, deps=[t6])
            b1 = p.emit("tensor", lambda t: t.matmul(P0[:, :], ones_f[:, :], rstd[:, :], start=True, stop=True),
                        deps=[t6] + bk.war("P0"))
            b2 = p.emit("tensor", lambda t: t.matmul(P1[:, :], ones_f[:, :], murstd[:, :], start=True, stop=True),
                        deps=[t7] + bk.war("P1"))
            for k in range(NKC):
                ld = p.emit("sync", lambda s, k=k, gsl=gsl: s.dma_start(
                    out=xt[:, :], in_=x_h[k * 128:(k + 1) * 128, gsl]), deps=xtoks, dma=True, ch="xt")
                m1 = p.emit("vector", lambda v: v.tensor_mul(tmp[:, :], xt[:, :], P0[:, :]), deps=[ld, b1])
                m2 = p.emit("vector", lambda v: v.tensor_sub(tmp[:, :], tmp[:, :], P1[:, :]), deps=[m1, b2])
                m3 = p.emit("vector", lambda v: v.tensor_copy(zbt[:, :], tmp[:, :]), deps=[m2] + zst)
                zs = p.emit("gpsimd", lambda g, k=k, gsl=gsl: g.dma_start(
                    out=zbf_h[k * 128:(k + 1) * 128, gsl], in_=zbt[:, :]), deps=[m3], dma=True, ch="zb")
                zst = [zs]
                xtoks = [m1, m2, m3]
            bk.read("P0", xtoks[0])
            bk.read("P1", xtoks[1])
        ln_done = xtoks + zst

        def build_zmu(pi, seg, deps):
            toks = []
            deps = list(deps)
            for k in range(NKC):
                ld = p.emit("sync", lambda s, k=k, seg=seg: s.dma_start(
                    out=zmu[k][:, :], in_=zbf_h[k * 128:(k + 1) * 128, seg * SEGW:(seg + 1) * SEGW]),
                    deps=deps + bk.war(("zmu", k)), dma=True, ch=f"zmu{k}")
                a = p.emit("vector", lambda v, k=k, pi=pi: v.tensor_scalar_add(
                    zmu[k][:, :], zmu[k][:, :], mus[:, pi * 8 + k:pi * 8 + k + 1]),
                    deps=[ld] + mu_all[-1:])
                toks.append(a)
            return toks

        def load_w(dst_list, src, width, n, ch, deps):
            toks = []
            for i in range(n):
                toks.append(p.emit("gpsimd", lambda g, i=i, src=src, width=width, dst_list=dst_list: g.dma_start(
                    out=dst_list[i][:, 0:width], in_=src[i * 128:(i + 1) * 128, :]),
                    deps=deps + bk.war((ch, i)), dma=True, ch=f"wl_{ch}_{i}"))
            return toks

        def load_bias(dst, src, n, ch, deps):
            toks = []
            for i in range(n):
                toks.append(p.emit("sync", lambda s, i=i, src=src, dst=dst: s.dma_start(
                    out=dst[:, i:i + 1], in_=src[i]), deps=deps, dma=True, ch=ch))
            return toks

        path_idx = {"r": 0, "k": 1, "v": 2, "g": 3, "a": 4, "d": 5}

        def emit_big_mlp(pth, seg, deps):
            zt = build_zmu(path_idx[pth], seg, deps)
            wt1 = load_w(w1b, w1_h[pth], EP, NKC, "w1s", deps)
            bt1 = load_bias(b1t, b1_h[pth], NFC, "b1s", deps)
            bt2 = load_bias(b2t, b2_h[pth], NKC, "b2s", deps)
            hup = {}
            for f in range(NFC):
                for ts in range(NTS):
                    tsl = slice(ts * TS, (ts + 1) * TS)
                    hb = [P2, P3][ts % 2]
                    hkey = ("h", ts % 2)
                    mmt = None
                    for k in range(NKC):
                        mmt = p.emit("tensor", lambda t, k=k, f=f, tsl=tsl, hb=hb: t.matmul(
                            hb[:, :], w1b[k][:, f * 128:(f + 1) * 128], zmu[k][:, tsl],
                            start=(k == 0), stop=(k == NKC - 1)),
                            deps=[wt1[k], zt[k], mmt] + (bk.war(hkey) if k == 0 else []))
                    ge = p.emit("scalar", lambda s, f=f, tsl=tsl, hb=hb: s.activation(
                        out=hx[f][:, tsl], in_=hb[:, :], func=AF.Gelu, bias=b1t[:, f:f + 1]),
                        deps=[mmt, bt1[-1]] + bk.war(("hx", f, ts)))
                    bk.read(hkey, ge)
                    hup[(f, ts)] = ge
            tail = []
            wt2 = load_w(w2b, w2_h[pth], D, NFC, "w2s", list(deps))
            for c in range(NKC):
                for ts in range(NTS):
                    tsl = slice(ts * TS, (ts + 1) * TS)
                    yb = [P4, P5][ts % 2]
                    ykey = ("y", ts % 2)
                    mmt = None
                    for f in range(NFC):
                        mmt = p.emit("tensor", lambda t, f=f, c=c, tsl=tsl, yb=yb: t.matmul(
                            yb[:, :], w2b[f][:, c * 128:(c + 1) * 128], hx[f][:, tsl],
                            start=(f == 0), stop=(f == NFC - 1)),
                            deps=[wt2[f], hup[(f, ts)], mmt] + (bk.war(ykey) if f == 0 else []))
                    ev = p.emit("vector", lambda v, c=c, tsl=tsl, yb=yb: v.tensor_scalar(
                        out[c][:, tsl], yb[:, :], b2t[:, c:c + 1], None, ALU.add),
                        deps=[mmt, bt2[-1]] + bk.war(("out", c, ts)))
                    bk.read(ykey, ev)
                    for f in range(NFC):
                        bk.read(("hx", f, ts), mmt)
                    tail.append(ev)
            for k in range(NKC):
                bk.read(("w1s", k), tail[-1])
            for f in range(NFC):
                bk.read(("w2s", f), tail[-1])
            return tail

        def emit_lowrank(pth, seg, deps, post):
            zt = build_zmu(path_idx[pth], seg, deps)
            wtA = load_w(w1b, lrA_h[pth], LR, NKC, "w1s", deps)
            btl = load_bias(b1t, lrb_h[pth], NKC, "b1s", deps)
            wtB = load_w(w2b, lrB_h[pth], D, 1, "w2s", deps)
            t1up = {}
            for ts in range(NTS):
                tsl = slice(ts * TS, (ts + 1) * TS)
                hb = [P2, P3][ts % 2]
                hkey = ("h", ts % 2)
                mmt = None
                for k in range(NKC):
                    mmt = p.emit("tensor", lambda t, k=k, tsl=tsl, hb=hb: t.matmul(
                        hb[:, :], w1b[k][:, 0:LR], zmu[k][:, tsl],
                        start=(k == 0), stop=(k == NKC - 1)),
                        deps=[wtA[k], zt[k], mmt] + (bk.war(hkey) if k == 0 else []))
                cv = p.emit("vector", lambda v, tsl=tsl, hb=hb: v.tensor_copy(
                    hx[0][:, tsl], hb[:, :]), deps=[mmt] + bk.war(("hx", 0, ts)))
                bk.read(hkey, cv)
                t1up[ts] = cv
            tail = []
            for c in range(NKC):
                for ts in range(NTS):
                    tsl = slice(ts * TS, (ts + 1) * TS)
                    yb = [P4, P5][ts % 2]
                    ykey = ("y", ts % 2)
                    mmt = p.emit("tensor", lambda t, c=c, tsl=tsl, yb=yb: t.matmul(
                        yb[:, :], w2b[0][:, c * 128:(c + 1) * 128], hx[0][:, tsl],
                        start=True, stop=True),
                        deps=[wtB[0], t1up[ts]] + bk.war(ykey))
                    toks = post(c, ts, tsl, yb, [mmt, btl[-1]], seg)
                    bk.read(ykey, toks[0])
                    bk.read(("hx", 0, ts), mmt)
                    tail += toks
            bk.read(("w2s", 0), tail[-1])
            for k in range(NKC):
                bk.read(("w1s", k), tail[-1])
            return tail

        cur = ln_done + ctoks
        for seg in range(NSEG):
            soff = seg * SEGW

            def post_a(c, ts, tsl, yb, deps, seg):
                s1 = p.emit("scalar", lambda s, yb=yb, c=c: s.activation(
                    out=tmp[:, :], in_=yb[:, :], func=AF.Sigmoid, bias=b1t[:, c:c + 1]),
                    deps=deps + bk.war("tmp"))
                dr = p.emit("scalar", lambda s: s.drain(), deps=[s1])
                d1 = p.emit("gpsimd", lambda g, c=c, tsl=tsl, seg=seg: g.dma_start(
                    out=iclr_h[c * 128:(c + 1) * 128, seg * SEGW + tsl.start:seg * SEGW + tsl.stop], in_=tmp[:, :]),
                    deps=[dr], dma=True, ch="st_i")
                bk.read("tmp", d1)
                return [s1, d1]
            cur = emit_lowrank("a", seg, cur, post_a)

            def post_g(c, ts, tsl, yb, deps, seg):
                s1 = p.emit("scalar", lambda s, yb=yb: s.activation(
                    out=tmp[:, :], in_=yb[:, :], func=AF.Sigmoid), deps=deps + bk.war("tmp"))
                dr = p.emit("scalar", lambda s: s.drain(), deps=[s1])
                d1 = p.emit("gpsimd", lambda g, c=c, tsl=tsl, seg=seg: g.dma_start(
                    out=outs["o_gate"][c * 128:(c + 1) * 128, seg * SEGW + tsl.start:seg * SEGW + tsl.stop], in_=tmp[:, :]),
                    deps=[dr], dma=True, ch="st_g")
                bk.read("tmp", d1)
                return [s1, d1]
            cur = emit_lowrank("g", seg, cur, post_g)

            def post_d(c, ts, tsl, yb, deps, seg):
                s1 = p.emit("scalar", lambda s, yb=yb, c=c: s.activation(
                    out=tmp[:, :], in_=yb[:, :], func=AF.Tanh, bias=b1t[:, c:c + 1]),
                    deps=deps + bk.war("tmp"))
                s2 = p.emit("scalar", lambda s: s.activation(
                    out=tmp[:, :], in_=tmp[:, :], func=AF.Sigmoid), deps=[s1])
                dr = p.emit("scalar", lambda s: s.drain(), deps=[s2])
                d1 = p.emit("gpsimd", lambda g, c=c, tsl=tsl, seg=seg: g.dma_start(
                    out=outs["o_w"][c * 128:(c + 1) * 128, seg * SEGW + tsl.start:seg * SEGW + tsl.stop], in_=tmp[:, :]),
                    deps=[dr], dma=True, ch="st_w")
                bk.read("tmp", d1)
                return [s2, d1]
            cur = emit_lowrank("d", seg, cur, post_d)

            for pth, och in (("v", "o_v"), ("r", "o_r")):
                cur = emit_big_mlp(pth, seg, cur)
                stv = []
                for c in range(NKC):
                    dr = p.emit("scalar", lambda s: s.drain(), deps=cur)
                    stv.append(p.emit("gpsimd", lambda g, c=c, soff=soff, och=och: g.dma_start(
                        out=outs[och][c * 128:(c + 1) * 128, soff:soff + SEGW], in_=out[c][:, :]),
                        deps=[dr], dma=True, ch="st_" + och))
                    for ts in range(NTS):
                        bk.read(("out", c, ts), stv[-1])
                cur = cur + stv

            cur = emit_big_mlp("k", seg, cur)
            kst = []
            for c in range(NKC):
                for ts in range(NTS):
                    tsl = slice(ts * TS, (ts + 1) * TS)
                    gsl = slice(soff + ts * TS, soff + (ts + 1) * TS)
                    q1 = p.emit("vector", lambda v, c=c, tsl=tsl: v.tensor_scalar(
                        tmp[:, :], out[c][:, tsl], remt[:, c:c + 1], None, ALU.mult),
                        deps=cur + bk.war("tmp") + cextra[-1:])
                    q2 = p.emit("scalar", lambda s: s.activation(out=tmp2[:, :], in_=tmp[:, :], func=AF.Square),
                                deps=[q1] + bk.war("tmp2"))
                    q3 = p.emit("tensor", lambda t: t.matmul(P0[0:2, :], Et[:, :], tmp2[:, :], start=True, stop=True),
                                deps=[q2, ctoks[0]] + bk.war("P0"))
                    bk.read("tmp2", q3)
                    q4 = p.emit("vector", lambda v: v.tensor_copy(nrow[:, :], P0[0:2, :]), deps=[q3] + bk.war("nrow"))
                    bk.read("P0", q4)
                    q5 = p.emit("scalar", lambda s: s.activation(out=nrow[:, :], in_=nrow[:, :], func=AF.Sqrt), deps=[q4])
                    q6 = p.emit("vector", lambda v: v.tensor_scalar_max(nrow[:, :], nrow[:, :], 1e-12), deps=[q5])
                    q7 = p.emit("vector", lambda v: v.reciprocal(nrow[:, :], nrow[:, :]), slow=True, deps=[q6])
                    q8 = p.emit("tensor", lambda t: t.matmul(P1[:, :], E2t[:, :], nrow[:, :], start=True, stop=True),
                                deps=[q7, ctoks[1]] + bk.war("P1"))
                    bk.read("nrow", q8)
                    q9 = p.emit("vector", lambda v: v.tensor_mul(tmp2[:, :], tmp[:, :], P1[:, :]),
                                deps=[q8, q1] + bk.war("tmp2"))
                    bk.read("P1", q9)
                    bk.read("tmp", q9)
                    dr1 = p.emit("vector", lambda v: v.drain(), deps=[q9])
                    o1 = p.emit("gpsimd", lambda g, c=c, gsl=gsl: g.dma_start(
                        out=outs["o_kn"][c * 128:(c + 1) * 128, gsl], in_=tmp2[:, :]), deps=[dr1], dma=True, ch="st_kn")
                    li = p.emit("sync", lambda s, c=c, gsl=gsl: s.dma_start(
                        out=itile[:, :], in_=iclr_h[c * 128:(c + 1) * 128, gsl]),
                        deps=cur + bk.war("itile"), dma=True, ch="itile")
                    a1 = p.emit("vector", lambda v: v.tensor_mul(tmp3[:, :], tmp2[:, :], itile[:, :]),
                                deps=[li, q9] + bk.war("tmp3"))
                    bk.read("tmp2", a1)
                    dr2 = p.emit("vector", lambda v: v.drain(), deps=[a1])
                    o2 = p.emit("gpsimd", lambda g, c=c, gsl=gsl: g.dma_start(
                        out=outs["o_at"][c * 128:(c + 1) * 128, gsl], in_=tmp3[:, :]), deps=[dr2], dma=True, ch="st_at")
                    bk.read("tmp3", o2)
                    r1 = p.emit("vector", lambda v, c=c, tsl=tsl: v.tensor_scalar(
                        tmp[:, :], out[c][:, tsl], mixt[:, c:c + 1], None, ALU.mult),
                        deps=[o1, q9] + cextra[-1:])
                    r2 = p.emit("vector", lambda v: v.tensor_scalar_add(tmp2[:, :], itile[:, :], -1.0),
                                deps=[li, a1] + bk.war("tmp2"))
                    bk.read("itile", r2)
                    r3 = p.emit("vector", lambda v: v.tensor_mul(tmp[:, :], tmp[:, :], tmp2[:, :]), deps=[r1, r2])
                    bk.read("tmp2", r3)
                    r4 = p.emit("vector", lambda v, c=c, tsl=tsl: v.tensor_add(tmp[:, :], tmp[:, :], out[c][:, tsl]), deps=[r3])
                    dr3 = p.emit("vector", lambda v: v.drain(), deps=[r4])
                    o3 = p.emit("gpsimd", lambda g, c=c, gsl=gsl: g.dma_start(
                        out=outs["o_rk"][c * 128:(c + 1) * 128, gsl], in_=tmp[:, :]), deps=[dr3], dma=True, ch="st_rk")
                    bk.read("tmp", o3)
                    bk.read(("out", c, ts), o3)
                    kst += [o1, o2, o3]
            cur = cur + kst
        p.emit("gpsimd", lambda g: g.engine_nop(), deps=cur)

        with nc.Block() as block:
            p.realize(block)
    return nc


def run_dn_proj(xT, I):
    nc = _get("dn_proj", build_dn_proj)
    shards = _shard_fm(xT)
    EP, E0 = 1280, 1228

    def padW1(w):
        out = np.zeros((D, EP), np.float32)
        out[:, :E0] = w
        return out

    def padW2(w):
        out = np.zeros((EP, D), np.float32)
        out[:E0] = w
        return out

    def padb(b):
        out = np.zeros(EP, np.float32)
        out[:E0] = b
        return np.ascontiguousarray(out.reshape(10, 128, 1))

    Emat = np.zeros((128, 2), np.float32)
    Emat[:64, 0] = 1.0
    Emat[64:, 1] = 1.0
    base = {
        "Emat": Emat, "E2mat": np.ascontiguousarray(Emat.T),
        "mus": np.stack([I[f"L0_attn_mu_{c}"].reshape(8, 128, 1) for c in "rkvgad"]).astype(np.float32),
        "rem_mult": I["L0_attn_rem_mult"].reshape(8, 128, 1).astype(np.float32),
        "iclr_mix": I["L0_attn_iclr_mix"].reshape(8, 128, 1).astype(np.float32),
    }
    for pth in ("r", "k", "v"):
        base[f"{pth}W1"] = padW1(np.asarray(I[f"L0_attn_{pth}W1"]))
        base[f"{pth}b1"] = padb(np.asarray(I[f"L0_attn_{pth}b1"]))
        base[f"{pth}W2"] = padW2(np.asarray(I[f"L0_attn_{pth}W2"]))
        base[f"{pth}b2"] = np.ascontiguousarray(np.asarray(I[f"L0_attn_{pth}b2"]).reshape(8, 128, 1))
    for pth, nm in (("g", "g"), ("a", "a"), ("d", "d")):
        base[f"{pth}A"] = np.ascontiguousarray(np.asarray(I[f"L0_attn_{nm}A"]))
        base[f"{pth}B"] = np.ascontiguousarray(np.asarray(I[f"L0_attn_{nm}B"]))
    base["gb"] = np.zeros((8, 128, 1), np.float32)
    base["ab"] = np.ascontiguousarray(np.asarray(I["L0_attn_ab"]).reshape(8, 128, 1))
    base["db"] = np.ascontiguousarray(np.asarray(I["L0_attn_db"]).reshape(8, 128, 1))
    in_maps = [dict(base, x=s) for s in shards]
    res = run_bass_kernel_spmd(nc, in_maps, core_ids=list(range(NCORE)))
    names = ("o_r", "o_v", "o_kn", "o_at", "o_rk", "o_gate", "o_w")
    return {n: np.concatenate([r[n] for r in res.results], axis=1) for n in names}


DECAY_C = 0.7408182206817179


def build_dn_scan(dbg=0):
    NH = 4
    T = 8192
    L = 64
    BLKW = 512
    NBLK = T // BLKW
    NCH = BLKW // L
    NEWTON = 4

    nc = bass.Bass("TRN2", target_bir_lowering=False)
    ins = {}
    for s_ in ("w", "at", "kn", "rk", "v", "r"):
        ins[s_] = nc.dram_tensor(f"in_{s_}", [NH * 64, T], F32, kind="ExternalInput").ap()
    mask_h = nc.dram_tensor("masks", [4, 64, 64], F32, kind="ExternalInput").ap()
    oy_h = nc.dram_tensor("o_y", [NH * 64, T], F32, kind="ExternalOutput").ap()
    if dbg:
        dbgf_h = nc.dram_tensor("dbg_f", [64, 4 * 512], F32, kind="ExternalOutput").ap()
        dbgb_h = nc.dram_tensor("dbg_b", [64, 16 * 64], BF16, kind="ExternalOutput").ap()

    with ExitStack() as st:
        p = P(nc, st)
        bk = Banks()
        sb = lambda name, shape, dt=F32: st.enter_context(nc.sbuf_tensor(name, shape, dt))
        pt = lambda name, shape, dt=F32: st.enter_context(nc.psum_tensor(name, shape, dt))

        stw = {s_: sb(f"st_{s_}", [64, BLKW], F32) for s_ in ("w", "kn", "rk", "v", "r", "at")}
        gA = sb("gA", [64, BLKW], F32)
        gB = sb("gB", [64, BLKW], F32)
        Pt = sb("Pt", [64, BLKW], F32)
        Ppv = sb("Ppv", [64, BLKW], F32)
        Piv = sb("Piv", [64, BLKW], F32)
        apb = sb("apb", [64, BLKW], BF16)
        knb = sb("knb", [64, BLKW], BF16)
        vpb = sb("vpb", [64, BLKW], BF16)
        rkb = sb("rkb", [64, BLKW], BF16)
        rb = sb("rb", [64, BLKW], BF16)
        ob = sb("ob", [64, BLKW], F32)
        msu = sb("msu", [64, 64], F32)
        msl = sb("msl", [64, 64], F32)
        mui = sb("mui", [64, 64], F32)
        idnf = sb("idnf", [64, 64], F32)
        idnb = sb("idnb", [64, 64], BF16)
        W0f = [sb(f"W0f{j}", [64, 64], F32) for j in range(NH)]
        W0b = sb("W0b", [64, 64], BF16)
        w0tb = sb("w0tb", [64, 64], BF16)
        aprow = sb("aprow", [64, 64], BF16)
        rkrow = sb("rkrow", [64, 64], BF16)
        knrn = sb("knrn", [64, 64], BF16)
        vprow = sb("vprow", [64, 64], BF16)
        ctm = sb("ctm", [64, 64], BF16)
        cmf = sb("cmf", [64, 64], F32)
        Xa = sb("Xa", [64, 64], BF16)
        XTa = sb("XTa", [64, 64], BF16)
        Xb_ = sb("Xb_", [64, 64], BF16)
        XTb_ = sb("XTb_", [64, 64], BF16)
        Yt = sb("Yt", [64, 64], BF16)
        YTt = sb("YTt", [64, 64], BF16)
        btm = sb("btm", [64, 64], BF16)
        t1b = sb("t1b", [64, 64], BF16)
        utb = sb("utb", [64, 64], BF16)
        uttb = sb("uttb", [64, 64], BF16)
        ub = sb("ub", [64, 64], BF16)
        utb2 = sb("utb2", [64, 64], BF16)
        qb = sb("qb", [64, 64], BF16)
        qtb = sb("qtb", [64, 64], BF16)
        atb = sb("atb", [64, 64], BF16)
        btb = sb("btb", [64, 64], BF16)
        wtmp = sb("wtmp", [64, 64], F32)

        PS = [pt(f"ps{i}", [64, 64], F32) for i in range(6)]
        PTb = [pt(f"ptb{i}", [64, 64], BF16) for i in range(2)]

        ctoks = [
            p.emit("sync", lambda s: s.dma_start(out=msu[:, :], in_=mask_h[0]), dma=True, ch="c0"),
            p.emit("sync", lambda s: s.dma_start(out=mui[:, :], in_=mask_h[1]), dma=True, ch="c1"),
            p.emit("sync", lambda s: s.dma_start(out=idnf[:, :], in_=mask_h[2]), dma=True, ch="c2"),
            p.emit("sync", lambda s: s.dma_start(out=msl[:, :], in_=mask_h[3]), dma=True, ch="c3"),
        ]
        idnb_tok = p.emit("vector", lambda v: v.tensor_copy(idnb[:, :], idnf[:, :]), deps=[ctoks[2]])
        ctoks.append(idnb_tok)
        zt = [p.emit("vector", lambda v, j=j: v.memset(W0f[j][:, :], 0.0)) for j in range(NH)]

        def mm(bank, lhsT, rhs, deps, start=True, stop=True):
            key = ("ps", bank)
            d = deps + (bk.war(key) if start else [])
            return p.emit("tensor", lambda t: t.matmul(PS[bank][:, :], lhsT, rhs, start=start, stop=stop), deps=d)

        def evac(bank, out_ap, deps, src_scale=None):
            key = ("ps", bank)
            if src_scale is None:
                tok = p.emit("vector", lambda v: v.tensor_copy(out_ap, PS[bank][:, :]), deps=deps)
            else:
                tok = p.emit("vector", lambda v: v.tensor_scalar_mul(out_ap, PS[bank][:, :], src_scale), deps=deps)
            bk.read(key, tok)
            return tok

        def trans(pb, in_ap, out_ap, deps, scale=None):
            key = ("pt", pb)
            t1 = p.emit("tensor", lambda t: t.transpose(PTb[pb][:, :], in_ap, idnb[:, :]),
                        deps=deps + [idnb_tok] + bk.war(key))
            if scale is None:
                t2 = p.emit("vector", lambda v: v.tensor_copy(out_ap, PTb[pb][:, :]), deps=[t1])
            else:
                t2 = p.emit("vector", lambda v: v.tensor_scalar_mul(out_ap, PTb[pb][:, :], scale), deps=[t1])
            bk.read(key, t2)
            return t2

        head_tail = {j: zt[j:j + 1] + ctoks for j in range(NH)}
        all_stores = []
        for j in range(NH):
            hrows = slice(j * 64, (j + 1) * 64)
            for blk in range(NBLK):
                bsl = slice(blk * BLKW, (blk + 1) * BLKW)
                cur = head_tail[j]
                lts = {}
                for s_ in ("w", "at", "kn", "rk", "v", "r"):
                    lts[s_] = p.emit("sync", lambda s, s_=s_, hrows=hrows, bsl=bsl: s.dma_start(
                        out=stw[s_][:, :], in_=ins[s_][hrows, bsl]),
                        deps=cur + bk.war(("stw", s_)), dma=True, ch="ld_" + s_)
                c0 = p.emit("vector", lambda v: v.tensor_scalar_mul(gA[:, :], stw["w"][:, :], -DECAY_C),
                            deps=[lts["w"]] + bk.war("gA"))
                bk.read(("stw", "w"), c0)
                wsc_src = gA
                c1 = p.emit("vector", lambda v: v.tensor_copy(gB[:, :], gA[:, :]), deps=[c0] + bk.war("gB"))
                src, dst = gA, gB
                ctok = c1
                for sh in (1, 2, 4, 8, 16, 32):
                    ops = []
                    for c in range(NCH):
                        lo = c * L
                        o1 = p.emit("vector", lambda v, lo=lo, sh=sh, src=src, dst=dst: v.tensor_copy(
                            dst[:, lo:lo + sh], src[:, lo:lo + sh]), deps=[ctok])
                        o2 = p.emit("vector", lambda v, lo=lo, sh=sh, src=src, dst=dst: v.tensor_add(
                            dst[:, lo + sh:lo + L], src[:, lo + sh:lo + L], src[:, lo:lo + L - sh]), deps=[ctok])
                        ops += [o1, o2]
                    ctok = ops[-1]
                    src, dst = dst, src
                    ctoks_all = ops
                gsc = src
                wsc = None
                c2 = p.emit("vector", lambda v, dst=dst: v.tensor_scalar_mul(dst[:, :], stw["w"][:, :], -DECAY_C),
                            deps=[lts["w"], ctok])
                e1 = p.emit("scalar", lambda s, gsc=gsc: s.activation(out=Pt[:, :], in_=gsc[:, :], func=AF.Exp),
                            deps=[ctok] + bk.war("Pt"))
                c3 = p.emit("vector", lambda v, gsc=gsc, dst=dst: v.tensor_sub(Ppv[:, :], gsc[:, :], dst[:, :]),
                            deps=[ctok, c2] + bk.war("Ppv"))
                e2 = p.emit("scalar", lambda s: s.activation(out=Ppv[:, :], in_=Ppv[:, :], func=AF.Exp), deps=[c3])
                c4 = p.emit("vector", lambda v, gsc=gsc, dst=dst: v.tensor_scalar_mul(dst[:, :], gsc[:, :], -1.0),
                            deps=[ctok, c3, c2])
                e3 = p.emit("scalar", lambda s, dst=dst: s.activation(out=Piv[:, :], in_=dst[:, :], func=AF.Exp),
                            deps=[c4] + bk.war("Piv"))
                b1 = p.emit("vector", lambda v: v.tensor_mul(apb[:, :], stw["at"][:, :], Ppv[:, :]),
                            deps=[lts["at"], e2] + bk.war("apb"))
                bk.read(("stw", "at"), b1)
                bk.read("Ppv", b1)
                b2 = p.emit("vector", lambda v: v.tensor_mul(knb[:, :], stw["kn"][:, :], Piv[:, :]),
                            deps=[lts["kn"], e3] + bk.war("knb"))
                bk.read(("stw", "kn"), b2)
                b3 = p.emit("vector", lambda v: v.tensor_mul(vpb[:, :], stw["v"][:, :], Piv[:, :]),
                            deps=[lts["v"], e3] + bk.war("vpb"))
                bk.read(("stw", "v"), b3)
                bk.read("Piv", b3)
                b4 = p.emit("vector", lambda v: v.tensor_copy(rkb[:, :], stw["rk"][:, :]),
                            deps=[lts["rk"]] + bk.war("rkb"))
                bk.read(("stw", "rk"), b4)
                b5 = p.emit("vector", lambda v: v.tensor_copy(rb[:, :], stw["r"][:, :]),
                            deps=[lts["r"]] + bk.war("rb"))
                bk.read(("stw", "r"), b5)
                bk.read("gA", b3)
                bk.read("gB", b3)
                blk_prep = [b1, b2, b3, b4, b5, e1]
                if dbg and j == 0 and blk == 0:
                    dtk = []
                    for idx, (tl, tok) in enumerate([(gsc, ctok), (Pt, e1), (Ppv, e2), (Piv, e3)]):
                        drx = p.emit("vector", lambda v: v.drain(), deps=[tok, b3, b1])
                        dtk.append(p.emit("gpsimd", lambda g, idx=idx, tl=tl: g.dma_start(
                            out=dbgf_h[:, idx * BLKW:(idx + 1) * BLKW], in_=tl[:, :]), deps=[drx], dma=True, ch="dbg"))
                    blk_prep = blk_prep + dtk

                prev_chunk = list(cur) + blk_prep
                for c in range(NCH):
                    sl = slice(c * L, (c + 1) * L)
                    tr1 = trans(0, apb[:, sl], aprow[:, :], blk_prep + bk.war("aprow"))
                    bk.read("apb", tr1)
                    tr2 = trans(1, rkb[:, sl], rkrow[:, :], blk_prep + bk.war("rkrow"))
                    tr3 = trans(0, vpb[:, sl], vprow[:, :], blk_prep + [tr1] + bk.war("vprow"))
                    tr4 = trans(1, knb[:, sl], knrn[:, :], blk_prep + [tr2] + bk.war("knrn"), scale=-1.0)
                    wcp = p.emit("vector", lambda v, j=j: v.tensor_copy(W0b[:, :], W0f[j][:, :]),
                                 deps=prev_chunk + bk.war("W0b"))
                    tr5 = trans(0, W0b[:, :], w0tb[:, :], [wcp, tr3] + bk.war("w0tb"))
                    m1 = mm(0, apb[:, sl], knb[:, sl], [b1, b2])
                    m2 = mm(1, knb[:, sl], apb[:, sl], [b1, b2])
                    cm1 = p.emit("vector", lambda v: v.tensor_mul(cmf[:, :], PS[0][:, :], msl[:, :]), deps=[m1, ctoks[3]] + bk.war("cmf"))
                    bk.read(("ps", 0), cm1)
                    ct1 = p.emit("vector", lambda v: v.tensor_mul(wtmp[:, :], PS[1][:, :], msu[:, :]), deps=[m2, ctoks[0]] + bk.war("wtmp"))
                    bk.read(("ps", 1), ct1)
                    ct2 = p.emit("vector", lambda v: v.tensor_copy(ctm[:, :], wtmp[:, :]), deps=[ct1] + bk.war("ctm"))
                    x0 = p.emit("vector", lambda v: v.tensor_sub(Xa[:, :], idnf[:, :], cmf[:, :]), deps=[cm1, ctoks[2]] + bk.war("Xa"))
                    xt0 = p.emit("vector", lambda v: v.tensor_sub(XTa[:, :], idnf[:, :], wtmp[:, :]), deps=[ct1, ctoks[2]] + bk.war("XTa"))
                    bk.read("wtmp", xt0)
                    X, XT, Xo, XTo = Xa, XTa, Xb_, XTb_
                    xkey, xokey = ("Xa", "XTa"), ("Xb_", "XTb_")
                    xtok, xttok = x0, xt0
                    for it in range(NEWTON):
                        n1 = mm(2, ctm[:, :], X[:, :], [ct2, xtok])
                        y1 = p.emit("vector", lambda v, X=X: v.tensor_add(Yt[:, :], PS[2][:, :], X[:, :]),
                                    deps=[n1, xtok] + bk.war("Yt"))
                        bk.read(("ps", 2), y1)
                        n3 = mm(4, XT[:, :], Yt[:, :], [y1, xttok])
                        x1 = p.emit("vector", lambda v, X=X, Xo=Xo: v.scalar_tensor_tensor(
                            Xo[:, :], X[:, :], 2.0, PS[4][:, :], ALU.mult, ALU.subtract),
                            deps=[n3] + bk.war(xokey[0]))
                        bk.read(("ps", 4), x1)
                        n4 = mm(5, Yt[:, :], XT[:, :], [y1, xttok])
                        x2 = p.emit("vector", lambda v, XT=XT, XTo=XTo: v.scalar_tensor_tensor(
                            XTo[:, :], XT[:, :], 2.0, PS[5][:, :], ALU.mult, ALU.subtract),
                            deps=[n4] + bk.war(xokey[1]))
                        bk.read(("ps", 5), x2)
                        bk.read("Yt", n4)
                        bk.read(xkey[0], x1)
                        bk.read(xkey[1], x2)
                        X, XT, Xo, XTo = Xo, XTo, X, XT
                        xkey, xokey = xokey, xkey
                        xtok, xttok = x1, x2
                    m3 = mm(2, vpb[:, sl], apb[:, sl], [b1, b3])
                    bt1 = p.emit("vector", lambda v: v.tensor_mul(wtmp[:, :], PS[2][:, :], msu[:, :]),
                                 deps=[m3, ctoks[0]] + bk.war("wtmp"))
                    bk.read(("ps", 2), bt1)
                    bt2 = p.emit("vector", lambda v: v.tensor_copy(btm[:, :], wtmp[:, :]), deps=[bt1] + bk.war("btm"))
                    bk.read("wtmp", bt2)
                    m4 = mm(3, btm[:, :], rkrow[:, :], [bt2, tr2])
                    t1e = evac(3, t1b[:, :], [m4] + bk.war("t1b"))
                    m5 = mm(4, XT[:, :], t1b[:, :], [xttok, t1e])
                    u1 = evac(4, utb[:, :], [m5] + bk.war("utb"))
                    m6 = mm(5, t1b[:, :], XT[:, :], [xttok, t1e])
                    u2 = evac(5, uttb[:, :], [m6] + bk.war("uttb"))
                    bk.read("t1b", m6)
                    m7 = mm(2, XT[:, :], aprow[:, :], [xttok, tr1])
                    u3 = evac(2, ub[:, :], [m7] + bk.war("ub"))
                    m8 = mm(3, aprow[:, :], XT[:, :], [xttok, tr1])
                    u4 = evac(3, utb2[:, :], [m8] + bk.war("utb2"))
                    bk.read("aprow", m8)
                    m9 = mm(4, utb2[:, :], W0b[:, :], [u4, wcp])
                    q1 = p.emit("vector", lambda v: v.tensor_add(qb[:, :], PS[4][:, :], utb[:, :]),
                                deps=[m9, u1] + bk.war("qb"))
                    bk.read(("ps", 4), q1)
                    bk.read("utb", q1)
                    m10 = mm(5, W0b[:, :], utb2[:, :], [u4, wcp])
                    q2 = p.emit("vector", lambda v: v.tensor_add(qtb[:, :], PS[5][:, :], uttb[:, :]),
                                deps=[m10, u2] + bk.war("qtb"))
                    bk.read(("ps", 5), q2)
                    bk.read("uttb", q2)
                    bk.read("utb2", m10)
                    bk.read(xkey[1], m10)
                    m11 = mm(2, rkb[:, sl], rb[:, sl], [b4, b5])
                    a1_ = p.emit("vector", lambda v: v.tensor_mul(wtmp[:, :], PS[2][:, :], mui[:, :]),
                                 deps=[m11, ctoks[1]] + bk.war("wtmp"))
                    bk.read(("ps", 2), a1_)
                    a2_ = p.emit("vector", lambda v: v.tensor_copy(atb[:, :], wtmp[:, :]), deps=[a1_] + bk.war("atb"))
                    bk.read("wtmp", a2_)
                    m12 = mm(3, qtb[:, :], rb[:, sl], [q2, b5])
                    b1_ = p.emit("vector", lambda v: v.tensor_mul(wtmp[:, :], PS[3][:, :], mui[:, :]),
                                 deps=[m12, ctoks[1]] + bk.war("wtmp"))
                    bk.read(("ps", 3), b1_)
                    b2_ = p.emit("vector", lambda v: v.tensor_copy(btb[:, :], wtmp[:, :]), deps=[b1_] + bk.war("btb"))
                    bk.read("wtmp", b2_)
                    m13 = mm(4, w0tb[:, :], rb[:, sl], [tr5, b5], start=True, stop=False)
                    m14 = mm(4, vprow[:, :], atb[:, :], [tr3, a2_], start=False, stop=False)
                    m15 = mm(4, knrn[:, :], btb[:, :], [tr4, b2_], start=False, stop=True)
                    o1 = p.emit("vector", lambda v, sl=sl: v.tensor_mul(ob[:, sl], Pt[:, sl], PS[4][:, :]),
                                deps=[m15, e1] + bk.war(("ob", c)))
                    bk.read(("ps", 4), o1)
                    bk.read("atb", m14)
                    bk.read("btb", m15)
                    bk.read("vprow", m14)
                    bk.read("knrn", m15)
                    bk.read("rkrow", m4)
                    bk.read("w0tb", m13)
                    m16 = mm(5, vprow[:, :], rkrow[:, :], [tr3, tr2], start=True, stop=False)
                    m17 = mm(5, knrn[:, :], qb[:, :], [tr4, q1], start=False, stop=True)
                    bk.read("qb", m17)
                    bk.read("vprow", m16)
                    bk.read("knrn", m17)
                    bk.read("rkrow", m16)
                    w1_ = p.emit("vector", lambda v, j=j: v.tensor_add(wtmp[:, :], W0f[j][:, :], PS[5][:, :]),
                                 deps=[m17] + bk.war("wtmp"))
                    bk.read(("ps", 5), w1_)
                    w2_ = p.emit("vector", lambda v, j=j, c=c: v.tensor_scalar(
                        W0f[j][:, :], wtmp[:, :], Pt[:, c * L + L - 1:c * L + L], None, ALU.mult),
                        deps=[w1_, e1, wcp])
                    bk.read("wtmp", w2_)
                    bk.read("W0b", m10)
                    if dbg and j == 0 and blk == 0 and c == 0:
                        dtk2 = []
                        for idx, (tl, tok) in enumerate([(ctm, ct2), (XT, xttok), (t1b, t1e), (utb, u1),
                                                          (qtb, q2), (atb, a2_), (btb, b2_), (aprow, tr1),
                                                          (rkrow, tr2), (vprow, tr3), (knrn, tr4), (apb, b1),
                                                          (knb, b2), (vpb, b3)]):
                            drx = p.emit("vector", lambda v: v.drain(), deps=[tok, o1, w2_])
                            dtk2.append(p.emit("gpsimd", lambda g, idx=idx, tl=tl: g.dma_start(
                                out=dbgb_h[:, idx * 64:(idx + 1) * 64], in_=tl[:, 0:64]),
                                deps=[drx], dma=True, ch="dbg"))
                        prev_chunk = prev_chunk + dtk2
                    bk.read("apb", m8)
                    bk.read("knb", m2)
                    bk.read("vpb", m3)
                    bk.read("rkb", m11)
                    bk.read("rb", m13)
                    bk.read("Pt", w2_)
                    prev_chunk = [w2_, o1]
                drb = p.emit("vector", lambda v: v.drain(), deps=prev_chunk + [o1])
                so = p.emit("gpsimd", lambda g, hrows=hrows, bsl=bsl: g.dma_start(
                    out=oy_h[hrows, bsl], in_=ob[:, :]), deps=[drb], dma=True, ch="st_y")
                for cc in range(NCH):
                    bk.read(("ob", cc), so)
                all_stores.append(so)
                head_tail[j] = [w2_, so]
        p.emit("gpsimd", lambda g: g.engine_nop(), deps=all_stores)

        with nc.Block() as block:
            p.realize(block)
    return nc


def run_dn_scan(streams):
    nc = _get("dn_scan", build_dn_scan)
    msu = np.triu(np.ones((64, 64), np.float32), 1)
    mui = np.triu(np.ones((64, 64), np.float32), 0)
    idn = np.eye(64, dtype=np.float32)
    msl = np.tril(np.ones((64, 64), np.float32), -1)
    masks = np.stack([msu, mui, idn, msl])
    in_maps = []
    for c in range(NCORE):
        m = {"masks": masks}
        for s_ in ("w", "at", "kn", "rk", "v", "r"):
            slabs = []
            for pair in range(4 * c, 4 * c + 4):
                b, h = pair // 16, pair % 16
                slabs.append(streams[s_][h * 64:(h + 1) * 64, b * 8192:(b + 1) * 8192])
            m[f"in_{s_}"] = np.ascontiguousarray(np.concatenate(slabs, axis=0))
        in_maps.append(m)
    res = run_bass_kernel_spmd(nc, in_maps, core_ids=list(range(NCORE)))
    global _last_scan_res
    _last_scan_res = res
    y = np.zeros((D, TOK), np.float32)
    for c in range(NCORE):
        for i, pair in enumerate(range(4 * c, 4 * c + 4)):
            b, h = pair // 16, pair % 16
            y[h * 64:(h + 1) * 64, b * 8192:(b + 1) * 8192] = res.results[c]["o_y"][i * 64:(i + 1) * 64]
    return y


def build_dn_post():
    TS = 512
    SEGW = 1024
    NSEG = TPC // SEGW
    NTS = SEGW // TS
    NKC = D // 128
    EP = 1280
    NFC = EP // 128

    nc = bass.Bass("TRN2", target_bir_lowering=False)
    x_h = nc.dram_tensor("x", [D, TPC], F32, kind="ExternalInput").ap()
    y_h = nc.dram_tensor("y", [D, TPC], F32, kind="ExternalInput").ap()
    r_h = nc.dram_tensor("r", [D, TPC], F32, kind="ExternalInput").ap()
    rk_h = nc.dram_tensor("rk", [D, TPC], F32, kind="ExternalInput").ap()
    v_h = nc.dram_tensor("v", [D, TPC], F32, kind="ExternalInput").ap()
    g_h = nc.dram_tensor("gate", [D, TPC], F32, kind="ExternalInput").ap()
    ow1_h = nc.dram_tensor("oW1", [D, EP], F32, kind="ExternalInput").ap()
    ob1_h = nc.dram_tensor("ob1", [NFC, 128, 1], F32, kind="ExternalInput").ap()
    ow2_h = nc.dram_tensor("oW2", [EP, D], F32, kind="ExternalInput").ap()
    ob2_h = nc.dram_tensor("ob2", [NKC, 128, 1], F32, kind="ExternalInput").ap()
    lng_h = nc.dram_tensor("ln_g", [NKC, 128, 1], F32, kind="ExternalInput").ap()
    lnb_h = nc.dram_tensor("ln_b", [NKC, 128, 1], F32, kind="ExternalInput").ap()
    bm_h = nc.dram_tensor("bonus_mult", [NKC, 128, 1], F32, kind="ExternalInput").ap()
    E_h = nc.dram_tensor("Emat", [128, 2], F32, kind="ExternalInput").ap()
    E2_h = nc.dram_tensor("E2mat", [2, 128], F32, kind="ExternalInput").ap()
    xo_h = nc.dram_tensor("xo", [D, TPC], F32, kind="ExternalOutput").ap()

    with ExitStack() as st:
        p = P(nc, st)
        bk = Banks()
        sb = lambda name, shape, dt=F32: st.enter_context(nc.sbuf_tensor(name, shape, dt))
        pt = lambda name, shape, dt=F32: st.enter_context(nc.psum_tensor(name, shape, dt))

        o0 = [sb(f"o0_{k}", [128, SEGW], F32) for k in range(NKC)]
        zn = [sb(f"zn{k}", [128, SEGW], BF16) for k in range(NKC)]
        hx = [sb(f"hx{f}", [128, SEGW], BF16) for f in range(NFC)]
        w1b = [sb(f"w1b{k}", [128, EP], BF16) for k in range(NKC)]
        w2b = [sb(f"w2b{f}", [128, D], BF16) for f in range(NFC)]
        stage = sb("stage", [128, EP], F32)
        at1 = sb("at1", [128, TS], F32)
        at2 = sb("at2", [128, TS], F32)
        at3 = sb("at3", [128, TS], F32)
        sqt = sb("sqt", [128, TS], BF16)
        obt = sb("obt", [128, TS], BF16)
        ones_bf = sb("ones_bf", [128, 1], BF16)
        ones_f = sb("ones_f", [1, 128], F32)
        mu = sb("mu", [1, TS], F32)
        rstd = sb("rstd", [1, TS], F32)
        murstd = sb("murstd", [1, TS], F32)
        srow = sb("srow", [1, TS], F32)
        srow2 = sb("srow2", [1, TS], F32)
        nrow = sb("nrow", [2, TS], F32)
        b1t = sb("b1t", [128, NFC], F32)
        b2t = sb("b2t", [128, NKC], F32)
        lngt = sb("lngt", [128, NKC], F32)
        lnbt = sb("lnbt", [128, NKC], F32)
        bmt = sb("bmt", [128, NKC], F32)
        Et = sb("Et", [128, 2], F32)
        E2t = sb("E2t", [2, 128], F32)

        P0 = pt("P0", [128, TS])
        P1 = pt("P1", [128, TS])
        P2 = pt("P2", [128, TS])
        P3 = pt("P3", [128, TS])
        P4 = pt("P4", [128, TS])
        P5 = pt("P5", [128, TS])
        S0 = pt("S0", [1, TS])
        S1 = pt("S1", [1, TS])

        p.emit("gpsimd", lambda g: g.memset(ones_bf[:, :], 1.0))
        p.emit("gpsimd", lambda g: g.memset(ones_f[:, :], 1.0))
        ctoks = [
            p.emit("sync", lambda s: s.dma_start(out=Et[:, :], in_=E_h[:, :]), dma=True, ch="c0"),
            p.emit("sync", lambda s: s.dma_start(out=E2t[:, :], in_=E2_h[:, :]), dma=True, ch="c1"),
        ]
        for k in range(NKC):
            ctoks.append(p.emit("sync", lambda s, k=k: s.dma_start(out=lngt[:, k:k + 1], in_=lng_h[k]), dma=True, ch="c2"))
            ctoks.append(p.emit("sync", lambda s, k=k: s.dma_start(out=lnbt[:, k:k + 1], in_=lnb_h[k]), dma=True, ch="c3"))
            ctoks.append(p.emit("sync", lambda s, k=k: s.dma_start(out=bmt[:, k:k + 1], in_=bm_h[k]), dma=True, ch="c4"))
            ctoks.append(p.emit("sync", lambda s, k=k: s.dma_start(out=b2t[:, k:k + 1], in_=ob2_h[k]), dma=True, ch="c5"))
        for f in range(NFC):
            ctoks.append(p.emit("sync", lambda s, f=f: s.dma_start(out=b1t[:, f:f + 1], in_=ob1_h[f]), dma=True, ch="c6"))
        wtoks1 = []
        for k in range(NKC):
            wtoks1.append(p.emit("gpsimd", lambda g, k=k: g.dma_start(
                out=w1b[k][:, :], in_=ow1_h[k * 128:(k + 1) * 128, :]), dma=True, ch=f"wp1_{k}"))
        wtoks2 = []
        for f in range(NFC):
            wtoks2.append(p.emit("gpsimd", lambda g, f=f: g.dma_start(
                out=w2b[f][:, :], in_=ow2_h[f * 128:(f + 1) * 128, :]), dma=True, ch=f"wp2_{f}"))

        cur = list(ctoks)
        for seg in range(NSEG):
            soff = seg * SEGW
            o0done = []
            for ts in range(NTS):
                gsl = slice(soff + ts * TS, soff + (ts + 1) * TS)
                tsl = slice(ts * TS, (ts + 1) * TS)
                snorm = None
                for k in range(NKC):
                    l1 = p.emit("sync", lambda s, k=k, gsl=gsl: s.dma_start(
                        out=at1[:, :], in_=r_h[k * 128:(k + 1) * 128, gsl]), deps=cur + bk.war("at1"), dma=True, ch="ld1")
                    l2 = p.emit("sync", lambda s, k=k, gsl=gsl: s.dma_start(
                        out=at2[:, :], in_=rk_h[k * 128:(k + 1) * 128, gsl]), deps=cur + bk.war("at2"), dma=True, ch="ld2")
                    u1 = p.emit("vector", lambda v: v.tensor_mul(at3[:, :], at1[:, :], at2[:, :]),
                                deps=[l1, l2] + bk.war("at3"))
                    bk.read("at1", u1)
                    bk.read("at2", u1)
                    cb = p.emit("vector", lambda v: v.tensor_copy(sqt[:, :], at3[:, :]), deps=[u1] + bk.war("sqt"))
                    bk.read("at3", cb)
                    snorm = p.emit("tensor", lambda t, k=k: t.matmul(
                        P0[0:2, :], Et[:, :].bitcast(F32) if False else Et[:, :], at3[:, :],
                        start=True, stop=True), deps=[u1, ctoks[0]] + bk.war("P0"))
                    bk.read("at3", snorm)
                    q4 = p.emit("vector", lambda v: v.tensor_copy(nrow[:, :], P0[0:2, :]), deps=[snorm] + bk.war("nrow"))
                    bk.read("P0", q4)
                    q8 = p.emit("tensor", lambda t: t.matmul(P1[:, :], E2t[:, :], nrow[:, :], start=True, stop=True),
                                deps=[q4, ctoks[1]] + bk.war("P1"))
                    bk.read("nrow", q8)
                    l3 = p.emit("sync", lambda s, k=k, gsl=gsl: s.dma_start(
                        out=at1[:, :], in_=v_h[k * 128:(k + 1) * 128, gsl]), deps=[u1] + bk.war("at1"), dma=True, ch="ld1")
                    v1 = p.emit("vector", lambda v, k=k: v.tensor_scalar(
                        at2[:, :], at1[:, :], bmt[:, k:k + 1], None, ALU.mult), deps=[l3, u1] + bk.war("at2"))
                    bk.read("at1", v1)
                    v2 = p.emit("vector", lambda v: v.tensor_scalar_mul(at2[:, :], at2[:, :], 0.2), deps=[v1])
                    v3 = p.emit("vector", lambda v: v.tensor_mul(at2[:, :], at2[:, :], P1[:, :]), deps=[v2, q8])
                    bk.read("P1", v3)
                    l4 = p.emit("sync", lambda s, k=k, gsl=gsl: s.dma_start(
                        out=at3[:, :], in_=y_h[k * 128:(k + 1) * 128, gsl]), deps=[cb] + bk.war("at3"), dma=True, ch="ld3")
                    v4 = p.emit("vector", lambda v, k=k, tsl=tsl: v.tensor_add(
                        o0[k][:, tsl], at3[:, :], at2[:, :]), deps=[l4, v3] + bk.war(("o0", k, ts)))
                    bk.read("at3", v4)
                    bk.read("at2", v4)
                    o0done.append(v4)
            zn_done = []
            for ts in range(NTS):
                gsl = slice(soff + ts * TS, soff + (ts + 1) * TS)
                tsl = slice(ts * TS, (ts + 1) * TS)
                s1tok = s2tok = None
                for k in range(NKC):
                    cb = p.emit("vector", lambda v, k=k, tsl=tsl: v.tensor_copy(obt[:, :], o0[k][:, tsl]),
                                deps=o0done + [s1tok] + bk.war("obt"))
                    sq = p.emit("scalar", lambda s, k=k, tsl=tsl: s.activation(
                        out=sqt[:, :], in_=o0[k][:, tsl], func=AF.Square), deps=o0done + [s2tok] + bk.war("sqt"))
                    s1tok = p.emit("tensor", lambda t, k=k: t.matmul(
                        S0[:, :], ones_bf[:, :], obt[:, :], start=(k == 0), stop=(k == NKC - 1)),
                        deps=[cb, s1tok] + (bk.war("S0") if k == 0 else []))
                    bk.read("obt", s1tok)
                    s2tok = p.emit("tensor", lambda t, k=k: t.matmul(
                        S1[:, :], ones_bf[:, :], sqt[:, :], start=(k == 0), stop=(k == NKC - 1)),
                        deps=[sq, s2tok] + (bk.war("S1") if k == 0 else []))
                    bk.read("sqt", s2tok)
                t1 = p.emit("vector", lambda v: v.tensor_scalar_mul(mu[:, :], S0[:, :], 1.0 / D), deps=[s1tok])
                bk.read("S0", t1)
                t2 = p.emit("vector", lambda v: v.tensor_scalar_mul(srow[:, :], S1[:, :], 1.0 / D), deps=[s2tok])
                bk.read("S1", t2)
                t3 = p.emit("vector", lambda v: v.tensor_mul(srow2[:, :], mu[:, :], mu[:, :]), deps=[t1])
                t4 = p.emit("vector", lambda v: v.tensor_sub(srow[:, :], srow[:, :], srow2[:, :]), deps=[t2, t3])
                t4b = p.emit("vector", lambda v: v.tensor_scalar_add(srow[:, :], srow[:, :], 1e-6), deps=[t4])
                t5 = p.emit("scalar", lambda s: s.activation(out=srow[:, :], in_=srow[:, :], func=AF.Sqrt), deps=[t4b])
                t6 = p.emit("vector", lambda v: v.reciprocal(rstd[:, :], srow[:, :]), deps=[t5])
                t7 = p.emit("vector", lambda v: v.tensor_mul(murstd[:, :], mu[:, :], rstd[:, :]), slow=True, deps=[t6])
                b1 = p.emit("tensor", lambda t: t.matmul(P0[:, :], ones_f[:, :], rstd[:, :], start=True, stop=True),
                            deps=[t6] + bk.war("P0"))
                b2 = p.emit("tensor", lambda t: t.matmul(P1[:, :], ones_f[:, :], murstd[:, :], start=True, stop=True),
                            deps=[t7] + bk.war("P1"))
                mprev = []
                for k in range(NKC):
                    lg = p.emit("sync", lambda s, k=k, gsl=gsl: s.dma_start(
                        out=at1[:, :], in_=g_h[k * 128:(k + 1) * 128, gsl]), deps=mprev + bk.war("at1"), dma=True, ch="ld1")
                    m1 = p.emit("vector", lambda v, k=k, tsl=tsl: v.tensor_mul(at2[:, :], o0[k][:, tsl], P0[:, :]),
                                deps=[b1] + o0done + bk.war("at2"))
                    m2 = p.emit("vector", lambda v: v.tensor_sub(at2[:, :], at2[:, :], P1[:, :]), deps=[m1, b2])
                    m3 = p.emit("vector", lambda v, k=k: v.tensor_scalar(
                        at2[:, :], at2[:, :], lngt[:, k:k + 1], None, ALU.mult), deps=[m2, ctoks[2]])
                    m4 = p.emit("vector", lambda v, k=k: v.tensor_scalar(
                        at2[:, :], at2[:, :], lnbt[:, k:k + 1], None, ALU.add), deps=[m3])
                    m5 = p.emit("vector", lambda v, k=k, tsl=tsl: v.tensor_tensor(
                        zn[k][:, tsl], at2[:, :], at1[:, :], ALU.mult), deps=[m4, lg] + bk.war(("zn", k, ts)))
                    bk.read("at1", m5)
                    bk.read("at2", m5)
                    mprev = [m5]
                    zn_done.append(m5)
                bk.read("P0", mprev[0])
                bk.read("P1", mprev[0])
            hup = {}
            for f in range(NFC):
                for ts in range(NTS):
                    tsl = slice(ts * TS, (ts + 1) * TS)
                    hb = [P2, P3][ts % 2]
                    hkey = ("h", ts % 2)
                    mmt = None
                    for k in range(NKC):
                        mmt = p.emit("tensor", lambda t, k=k, f=f, tsl=tsl, hb=hb: t.matmul(
                            hb[:, :], w1b[k][:, f * 128:(f + 1) * 128], zn[k][:, tsl],
                            start=(k == 0), stop=(k == NKC - 1)),
                            deps=[wtoks1[k], mmt] + zn_done + (bk.war(hkey) if k == 0 else []))
                    ge = p.emit("scalar", lambda s, f=f, tsl=tsl, hb=hb: s.activation(
                        out=hx[f][:, tsl], in_=hb[:, :], func=AF.Gelu, bias=b1t[:, f:f + 1]),
                        deps=[mmt, ctoks[-1]] + bk.war(("hx", f, ts)))
                    bk.read(hkey, ge)
                    hup[(f, ts)] = ge
            stoks = []
            for c in range(NKC):
                for ts in range(NTS):
                    tsl = slice(ts * TS, (ts + 1) * TS)
                    gsl = slice(soff + ts * TS, soff + (ts + 1) * TS)
                    yb = [P4, P5][ts % 2]
                    ykey = ("y", ts % 2)
                    mmt = None
                    for f in range(NFC):
                        mmt = p.emit("tensor", lambda t, f=f, c=c, tsl=tsl, yb=yb: t.matmul(
                            yb[:, :], w2b[f][:, c * 128:(c + 1) * 128], hx[f][:, tsl],
                            start=(f == 0), stop=(f == NFC - 1)),
                            deps=[wtoks2[f], hup[(f, ts)], mmt] + (bk.war(ykey) if f == 0 else []))
                    lx = p.emit("sync", lambda s, c=c, gsl=gsl: s.dma_start(
                        out=at3[:, :], in_=x_h[c * 128:(c + 1) * 128, gsl]),
                        deps=bk.war("at3") + [mmt], dma=True, ch="ld3")
                    e1 = p.emit("vector", lambda v, c=c, yb=yb: v.tensor_scalar(
                        at2[:, :], yb[:, :], b2t[:, c:c + 1], None, ALU.add),
                        deps=[mmt, ctoks[5]] + bk.war("at2"))
                    bk.read(ykey, e1)
                    e2 = p.emit("vector", lambda v: v.tensor_add(at2[:, :], at2[:, :], at3[:, :]), deps=[e1, lx])
                    bk.read("at3", e2)
                    dr = p.emit("vector", lambda v: v.drain(), deps=[e2])
                    so = p.emit("gpsimd", lambda g, c=c, gsl=gsl: g.dma_start(
                        out=xo_h[c * 128:(c + 1) * 128, gsl], in_=at2[:, :]), deps=[dr], dma=True, ch="st")
                    bk.read("at2", so)
                    for f in range(NFC):
                        bk.read(("hx", f, ts), mmt)
                    stoks.append(so)
            for k in range(NKC):
                for ts in range(NTS):
                    bk.read(("o0", k, ts), stoks[-1])
                    bk.read(("zn", k, ts), stoks[-1])
            cur = stoks
        p.emit("gpsimd", lambda g: g.engine_nop(), deps=cur)

        with nc.Block() as block:
            p.realize(block)
    return nc


def run_dn_post(xT, yT, streams, I):
    nc = _get("dn_post", build_dn_post)
    EP, E0 = 1280, 1228

    def padW1(w):
        o = np.zeros((D, EP), np.float32); o[:, :E0] = w
        return o

    def padW2(w):
        o = np.zeros((EP, D), np.float32); o[:E0] = w
        return o

    def padb(b):
        o = np.zeros(EP, np.float32); o[:E0] = b
        return np.ascontiguousarray(o.reshape(10, 128, 1))

    Emat = np.zeros((128, 2), np.float32)
    Emat[:64, 0] = 1.0
    Emat[64:, 1] = 1.0
    base = {
        "oW1": padW1(np.asarray(I["L0_attn_oW1"])),
        "ob1": padb(np.asarray(I["L0_attn_ob1"])),
        "oW2": padW2(np.asarray(I["L0_attn_oW2"])),
        "ob2": np.ascontiguousarray(np.asarray(I["L0_attn_ob2"]).reshape(8, 128, 1)),
        "ln_g": np.ascontiguousarray(np.asarray(I["L0_attn_ln_g"]).reshape(8, 128, 1)),
        "ln_b": np.ascontiguousarray(np.asarray(I["L0_attn_ln_b"]).reshape(8, 128, 1)),
        "bonus_mult": np.ascontiguousarray(np.asarray(I["L0_attn_bonus_mult"]).reshape(8, 128, 1)),
        "Emat": Emat, "E2mat": np.ascontiguousarray(Emat.T),
    }
    res = run_bass_kernel_spmd(nc, [dict(base,
        x=np.ascontiguousarray(xT[:, c * TPC:(c + 1) * TPC]),
        y=np.ascontiguousarray(yT[:, c * TPC:(c + 1) * TPC]),
        r=np.ascontiguousarray(streams["r"][:, c * TPC:(c + 1) * TPC]),
        rk=np.ascontiguousarray(streams["rk"][:, c * TPC:(c + 1) * TPC]),
        v=np.ascontiguousarray(streams["v"][:, c * TPC:(c + 1) * TPC]),
        gate=np.ascontiguousarray(streams["gate"][:, c * TPC:(c + 1) * TPC]),
    ) for c in range(NCORE)], core_ids=list(range(NCORE)))
    return np.concatenate([r["xo"] for r in res.results], axis=1)


def build_qkv():
    TS = 512
    NTS4 = TPC // TS
    NKC = D // 128
    NFC = 3 * D // 128

    nc = bass.Bass("TRN2", target_bir_lowering=False)
    x_h = nc.dram_tensor("x", [D, TPC], F32, kind="ExternalInput").ap()
    w_h = nc.dram_tensor("w_qkv", [D, 3 * D], F32, kind="ExternalInput").ap()
    b_h = nc.dram_tensor("b_qkv", [NFC, 128, 1], F32, kind="ExternalInput").ap()
    o_h = nc.dram_tensor("o_qkv", [3 * D, TPC], F32, kind="ExternalOutput").ap()

    with ExitStack() as st:
        p = P(nc, st)
        bk = Banks()
        sb = lambda name, shape, dt=F32: st.enter_context(nc.sbuf_tensor(name, shape, dt))
        pt = lambda name, shape, dt=F32: st.enter_context(nc.psum_tensor(name, shape, dt))
        zn = [sb(f"zn{k}", [128, TPC], BF16) for k in range(NKC)]
        w1b = [sb(f"w1b{k}", [128, 3 * D], BF16) for k in range(NKC)]
        stage = sb("stage", [128, 3 * D], F32)
        xt = sb("xt", [128, TS], F32)
        xbt = sb("xbt", [128, TS], BF16)
        sqt = sb("sqt", [128, TS], BF16)
        tmp = sb("tmpf", [128, TS], F32)
        ones_bf = sb("ones_bf", [128, 1], BF16)
        ones_f = sb("ones_f", [1, 128], F32)
        mu = sb("mu", [1, TS], F32)
        rstd = sb("rstd", [1, TS], F32)
        murstd = sb("murstd", [1, TS], F32)
        srow = sb("srow", [1, TS], F32)
        srow2 = sb("srow2", [1, TS], F32)
        bt = sb("bt", [128, NFC], F32)
        P0 = pt("P0", [128, TS])
        P1 = pt("P1", [128, TS])
        P2 = pt("P2", [128, TS])
        P3 = pt("P3", [128, TS])
        S0 = pt("S0", [1, TS])
        S1 = pt("S1", [1, TS])
        p.emit("gpsimd", lambda g: g.memset(ones_bf[:, :], 1.0))
        p.emit("gpsimd", lambda g: g.memset(ones_f[:, :], 1.0))
        ctoks = []
        for f in range(NFC):
            ctoks.append(p.emit("sync", lambda s, f=f: s.dma_start(out=bt[:, f:f + 1], in_=b_h[f]), dma=True, ch="cb"))
        wtoks = []
        for k in range(NKC):
            wtoks.append(p.emit("gpsimd", lambda g, k=k: g.dma_start(
                out=w1b[k][:, :], in_=w_h[k * 128:(k + 1) * 128, :]),
                dma=True, ch=f"wq{k}"))
        xtoks = []
        zn_done = []
        for ts in range(NTS4):
            gsl = slice(ts * TS, (ts + 1) * TS)
            s1tok = s2tok = None
            for k in range(NKC):
                ld = p.emit("sync", lambda s, k=k, gsl=gsl: s.dma_start(
                    out=xt[:, :], in_=x_h[k * 128:(k + 1) * 128, gsl]), deps=xtoks, dma=True, ch="xt")
                cb = p.emit("vector", lambda v: v.tensor_copy(xbt[:, :], xt[:, :]), deps=[ld, s1tok])
                sq = p.emit("scalar", lambda s: s.activation(out=sqt[:, :], in_=xt[:, :], func=AF.Square), deps=[ld, s2tok])
                s1tok = p.emit("tensor", lambda t, k=k: t.matmul(
                    S0[:, :], ones_bf[:, :], xbt[:, :], start=(k == 0), stop=(k == NKC - 1)),
                    deps=[cb, s1tok] + (bk.war("S0") if k == 0 else []))
                s2tok = p.emit("tensor", lambda t, k=k: t.matmul(
                    S1[:, :], ones_bf[:, :], sqt[:, :], start=(k == 0), stop=(k == NKC - 1)),
                    deps=[sq, s2tok] + (bk.war("S1") if k == 0 else []))
                xtoks = [cb, sq]
            t1 = p.emit("vector", lambda v: v.tensor_scalar_mul(mu[:, :], S0[:, :], 1.0 / D), deps=[s1tok])
            bk.read("S0", t1)
            t2 = p.emit("vector", lambda v: v.tensor_scalar_mul(srow[:, :], S1[:, :], 1.0 / D), deps=[s2tok])
            bk.read("S1", t2)
            t3 = p.emit("vector", lambda v: v.tensor_mul(srow2[:, :], mu[:, :], mu[:, :]), deps=[t1])
            t4 = p.emit("vector", lambda v: v.tensor_sub(srow[:, :], srow[:, :], srow2[:, :]), deps=[t2, t3])
            t4b = p.emit("vector", lambda v: v.tensor_scalar_add(srow[:, :], srow[:, :], 1e-5), deps=[t4])
            t5 = p.emit("scalar", lambda s: s.activation(out=srow[:, :], in_=srow[:, :], func=AF.Sqrt), deps=[t4b])
            t6 = p.emit("vector", lambda v: v.reciprocal(rstd[:, :], srow[:, :]), deps=[t5])
            t7 = p.emit("vector", lambda v: v.tensor_mul(murstd[:, :], mu[:, :], rstd[:, :]), slow=True, deps=[t6])
            b1 = p.emit("tensor", lambda t: t.matmul(P0[:, :], ones_f[:, :], rstd[:, :], start=True, stop=True),
                        deps=[t6] + bk.war("P0"))
            b2 = p.emit("tensor", lambda t: t.matmul(P1[:, :], ones_f[:, :], murstd[:, :], start=True, stop=True),
                        deps=[t7] + bk.war("P1"))
            for k in range(NKC):
                ld = p.emit("sync", lambda s, k=k, gsl=gsl: s.dma_start(
                    out=xt[:, :], in_=x_h[k * 128:(k + 1) * 128, gsl]), deps=xtoks, dma=True, ch="xt")
                m1 = p.emit("vector", lambda v: v.tensor_mul(tmp[:, :], xt[:, :], P0[:, :]), deps=[ld, b1])
                m2 = p.emit("vector", lambda v, k=k, gsl=gsl: v.tensor_tensor(
                    zn[k][:, gsl], tmp[:, :], P1[:, :], ALU.subtract), deps=[m1, b2])
                xtoks = [m1, m2]
                zn_done.append(m2)
            bk.read("P0", xtoks[0])
            bk.read("P1", xtoks[1])
        stoks = []
        for f in range(NFC):
            for ts in range(NTS4):
                tsl = slice(ts * TS, (ts + 1) * TS)
                yb = [P2, P3][ts % 2]
                ykey = ("y", ts % 2)
                mmt = None
                for k in range(NKC):
                    mmt = p.emit("tensor", lambda t, k=k, f=f, tsl=tsl, yb=yb: t.matmul(
                        yb[:, :], w1b[k][:, f * 128:(f + 1) * 128], zn[k][:, tsl],
                        start=(k == 0), stop=(k == NKC - 1)),
                        deps=[wtoks[k], mmt] + zn_done + (bk.war(ykey) if k == 0 else []))
                e1 = p.emit("vector", lambda v, f=f, yb=yb: v.tensor_scalar(
                    tmp[:, :], yb[:, :], bt[:, f:f + 1], None, ALU.add),
                    deps=[mmt, ctoks[-1]] + bk.war("tmpf"))
                bk.read(ykey, e1)
                dr = p.emit("vector", lambda v: v.drain(), deps=[e1])
                so = p.emit("gpsimd", lambda g, f=f, tsl=tsl: g.dma_start(
                    out=o_h[f * 128:(f + 1) * 128, tsl], in_=tmp[:, :]), deps=[dr], dma=True, ch="st")
                bk.read("tmpf", so)
                stoks.append(so)
        p.emit("gpsimd", lambda g: g.engine_nop(), deps=stoks)
        with nc.Block() as block:
            p.realize(block)
    return nc


def run_qkv(xT, I):
    nc = _get("qkv", build_qkv)
    base = {
        "w_qkv": np.ascontiguousarray(np.asarray(I["L1_attn_w_qkv"])),
        "b_qkv": np.ascontiguousarray(np.asarray(I["L1_attn_b_qkv"]).reshape(24, 128, 1)),
    }
    res = run_bass_kernel_spmd(nc, [dict(base, x=np.ascontiguousarray(xT[:, c * TPC:(c + 1) * TPC]))
                                    for c in range(NCORE)], core_ids=list(range(NCORE)))
    return np.concatenate([r["o_qkv"] for r in res.results], axis=1)


def build_attn_core():
    NH = 4
    T = 8192
    QT = 128
    KB = 512
    NQT = T // QT
    GQ = 4
    SCALE = 0.125
    EOFF = -30.0

    nc = bass.Bass("TRN2", target_bir_lowering=False)
    q_h = nc.dram_tensor("q", [NH * 64, T], F32, kind="ExternalInput").ap()
    k_h = nc.dram_tensor("k", [NH * 64, T], F32, kind="ExternalInput").ap()
    v_h = nc.dram_tensor("v", [NH, T, 64], F32, kind="ExternalInput").ap()
    tm_h = nc.dram_tensor("tmask", [4, 4, 128, 128], BF16, kind="ExternalInput").ap()
    oy_h = nc.dram_tensor("o_y", [NH, T, 64], F32, kind="ExternalOutput").ap()

    with ExitStack() as st:
        p = P(nc, st)
        bk = Banks()
        sb = lambda name, shape, dt=F32: st.enter_context(nc.sbuf_tensor(name, shape, dt))
        pt = lambda name, shape, dt=F32: st.enter_context(nc.psum_tensor(name, shape, dt))
        kt = sb("kt", [64, T], BF16)
        vt = sb("vt", [128, T // 128 * 65], BF16)
        qt = [sb(f"qt{j}", [64, QT], BF16) for j in range(GQ)]
        esb = [sb(f"esb{j}", [128, KB], BF16) for j in range(GQ)]
        tmt = [[sb(f"tmt{i}_{s_}", [128, 128], BF16) for s_ in range(4)] for i in range(4)]
        Ot = [sb(f"Ot{j}", [128, 64], F32) for j in range(GQ)]
        linv = [sb(f"linv{j}", [128, 1], F32) for j in range(GQ)]
        eoff_t = sb("eoff_t", [128, 1], F32)
        sct_t = sb("sct_t", [128, 1], F32)
        PSs = [pt(f"PSs{i}", [128, 128]) for i in range(2)]
        PSo = [pt(f"PSo{j}", [128, 65]) for j in range(GQ)]

        ctoks = []
        for i in range(4):
            for s_ in range(4):
                ctoks.append(p.emit("sync", lambda s, i=i, s_=s_: s.dma_start(
                    out=tmt[i][s_][:, :], in_=tm_h[i, s_]), dma=True, ch="cm"))
        p.emit("gpsimd", lambda g: g.memset(eoff_t[:, :], EOFF))
        p.emit("gpsimd", lambda g: g.memset(sct_t[:, :], SCALE))
        cset = p.emit("gpsimd", lambda g: g.memset(vt[:, :], 1.0))

        all_st = []
        pair_tail = list(ctoks)
        for j in range(NH):
            hrows = slice(j * 64, (j + 1) * 64)
            ck = p.emit("gpsimd", lambda g, hrows=hrows: g.dma_start(
                out=kt[:, :], in_=k_h[hrows, :]), deps=pair_tail + bk.war("kt"), dma=True, ch="ldk")
            cv = p.emit("gpsimd", lambda g, j=j: g.dma_start(
                out=vt[:, :].rearrange("p (i d) -> p i d", d=65)[:, :, 0:64],
                in_=v_h[j].rearrange("(i p) d -> p i d", p=128)),
                deps=pair_tail + bk.war("vt") + [cset], dma=True, ch="ldv")
            pair_tail = []
            for qg in range(NQT // GQ):
                qis = [qg * GQ + jj for jj in range(GQ)]
                span = qg + 1
                ld_q = []
                for jj, qi in enumerate(qis):
                    qsl = slice(qi * QT, (qi + 1) * QT)
                    lq = p.emit("gpsimd", lambda g, hrows=hrows, qsl=qsl, jj=jj: g.dma_start(
                        out=qt[jj][:, :], in_=q_h[hrows, qsl]),
                        deps=bk.war(("qt", jj)), dma=True, ch=f"ldq{jj}")
                    ld_q.append(lq)
                avts = {jj: None for jj in range(GQ)}
                for kb in range(span):
                    diag = (kb == qg)
                    etoks = {}
                    for jj, qi in enumerate(qis):
                        for sub in range(4):
                            pb = (jj * 4 + sub) % 2
                            kof = kb * KB + sub * 128
                            m1 = p.emit("tensor", lambda t, jj=jj, pb=pb, kof=kof: t.matmul(
                                PSs[pb][:, :], kt[:, kof:kof + 128], qt[jj][:, :],
                                start=True, stop=True),
                                deps=[ld_q[jj], ck] + bk.war(("pss", pb)))
                            e1 = p.emit("scalar", lambda s, jj=jj, pb=pb, sub=sub: s.activation(
                                out=esb[jj][:, sub * 128:(sub + 1) * 128], in_=PSs[pb][:, :],
                                func=AF.Exp, bias=eoff_t[:, 0:1], scale=sct_t[:, 0:1]),
                                deps=[m1] + bk.war(("esb", jj, sub)))
                            bk.read(("pss", pb), e1)
                            if diag:
                                e2 = p.emit("vector", lambda v, jj=jj, sub=sub, qi=qi: v.tensor_mul(
                                    esb[jj][:, sub * 128:(sub + 1) * 128],
                                    esb[jj][:, sub * 128:(sub + 1) * 128],
                                    tmt[qi % 4][sub][:, :]), deps=[e1, ctoks[0]])
                            else:
                                e2 = e1
                            etoks[(jj, sub)] = e2
                    for jj, qi in enumerate(qis):
                        for sub in range(4):
                            avts[jj] = p.emit("tensor", lambda t, jj=jj, sub=sub, kb=kb: t.matmul(
                                PSo[jj][:, :], esb[jj][:, sub * 128:(sub + 1) * 128],
                                vt[:, (kb * 4 + sub) * 65:(kb * 4 + sub) * 65 + 65],
                                start=(kb == 0 and sub == 0), stop=(kb == span - 1 and sub == 3)),
                                deps=[etoks[(jj, sub)], cv, avts[jj]] + (bk.war(("pso", jj)) if kb == 0 and sub == 0 else []))
                            bk.read(("esb", jj, sub), avts[jj])
                for jj, qi in enumerate(qis):
                    qsl = slice(qi * QT, (qi + 1) * QT)
                    f1 = p.emit("vector", lambda v, jj=jj: v.reciprocal(linv[jj][:, :], PSo[jj][:, 64:65]),
                                slow=True, deps=[avts[jj]] + bk.war(("linv", jj)))
                    f2 = p.emit("vector", lambda v, jj=jj: v.tensor_scalar(
                        Ot[jj][:, :], PSo[jj][:, 0:64], linv[jj][:, 0:1], None, ALU.mult),
                        deps=[f1] + bk.war(("Ot", jj)))
                    bk.read(("pso", jj), f2)
                    bk.read(("linv", jj), f2)
                    dr = p.emit("vector", lambda v: v.drain(), deps=[f2])
                    so = p.emit("gpsimd", lambda g, j=j, qsl=qsl, jj=jj: g.dma_start(
                        out=oy_h[j, qsl, :], in_=Ot[jj][:, :]), deps=[dr], dma=True, ch=f"st{jj}")
                    bk.read(("Ot", jj), so)
                    all_st.append(so)
                    pair_tail.append(so)
            bk.read("kt", all_st[-1])
            bk.read("vt", all_st[-1])
            for jj in range(GQ):
                bk.read(("qt", jj), all_st[-1])
        p.emit("gpsimd", lambda g: g.engine_nop(), deps=all_st)
        with nc.Block() as block:
            p.realize(block)
    return nc


def run_attn_core(qkvT):
    nc = _get("attn_core", build_attn_core)
    import ml_dtypes
    tm = np.zeros((4, 4, 128, 128), ml_dtypes.bfloat16)
    for i in range(4):
        for s_ in range(4):
            kk = np.arange(128)[:, None] + s_ * 128
            q = np.arange(128)[None, :] + i * 128
            tm[i, s_] = (kk <= q).astype(ml_dtypes.bfloat16)
    in_maps = []
    for c in range(NCORE):
        m = {"tmask": tm}
        qs, ks, vs = [], [], []
        for pair in range(4 * c, 4 * c + 4):
            b, h = pair // 16, pair % 16
            sl = slice(b * 8192, (b + 1) * 8192)
            qs.append(qkvT[h * 64:(h + 1) * 64, sl])
            ks.append(qkvT[1024 + h * 64:1024 + (h + 1) * 64, sl])
            vs.append(np.ascontiguousarray(qkvT[2048 + h * 64:2048 + (h + 1) * 64, sl].T))
        m["q"] = np.ascontiguousarray(np.concatenate(qs, axis=0))
        m["k"] = np.ascontiguousarray(np.concatenate(ks, axis=0))
        m["v"] = np.ascontiguousarray(np.stack(vs, axis=0))
        in_maps.append(m)
    res = run_bass_kernel_spmd(nc, in_maps, core_ids=list(range(NCORE)))
    y = np.zeros((D, TOK), np.float32)
    for c in range(NCORE):
        for i, pair in enumerate(range(4 * c, 4 * c + 4)):
            b, h = pair // 16, pair % 16
            y[h * 64:(h + 1) * 64, b * 8192:(b + 1) * 8192] = res.results[c]["o_y"][i].T
    return y


def build_attn_post():
    TS = 512
    NTS4 = TPC // TS
    NKC = D // 128

    nc = bass.Bass("TRN2", target_bir_lowering=False)
    x_h = nc.dram_tensor("x", [D, TPC], F32, kind="ExternalInput").ap()
    y_h = nc.dram_tensor("y", [D, TPC], F32, kind="ExternalInput").ap()
    w_h = nc.dram_tensor("w_out", [D, D], F32, kind="ExternalInput").ap()
    b_h = nc.dram_tensor("b_out", [NKC, 128, 1], F32, kind="ExternalInput").ap()
    o_h = nc.dram_tensor("xo", [D, TPC], F32, kind="ExternalOutput").ap()

    with ExitStack() as st:
        p = P(nc, st)
        bk = Banks()
        sb = lambda name, shape, dt=F32: st.enter_context(nc.sbuf_tensor(name, shape, dt))
        pt = lambda name, shape, dt=F32: st.enter_context(nc.psum_tensor(name, shape, dt))
        yb = [sb(f"yb{k}", [128, TPC], BF16) for k in range(NKC)]
        w1b = [sb(f"w1b{k}", [128, D], BF16) for k in range(NKC)]
        stage = sb("stage", [128, D], F32)
        xt = sb("xt", [128, TS], F32)
        tmp = sb("tmpf", [128, TS], F32)
        bt = sb("bt", [128, NKC], F32)
        P2 = pt("P2", [128, TS])
        P3 = pt("P3", [128, TS])
        ctoks = []
        for f in range(NKC):
            ctoks.append(p.emit("sync", lambda s, f=f: s.dma_start(out=bt[:, f:f + 1], in_=b_h[f]), dma=True, ch="cb"))
        wtoks = []
        for k in range(NKC):
            wtoks.append(p.emit("gpsimd", lambda g, k=k: g.dma_start(
                out=w1b[k][:, :], in_=w_h[k * 128:(k + 1) * 128, :]),
                dma=True, ch=f"wq{k}"))
        ld_all = []
        ycv = None
        for k in range(NKC):
            ld = p.emit("sync", lambda s, k=k: s.dma_start(
                out=stage[:, :], in_=y_h[k * 128:(k + 1) * 128, 0:D]), deps=bk.war("stage"), dma=True, ch="wstage")
            ycv = p.emit("gpsimd", lambda g, k=k: g.tensor_copy(yb[k][:, 0:D], stage[:, :]), deps=[ld])
            bk.read("stage", ycv)
            ld2 = p.emit("sync", lambda s, k=k: s.dma_start(
                out=stage[:, :], in_=y_h[k * 128:(k + 1) * 128, D:TPC]), deps=bk.war("stage"), dma=True, ch="wstage")
            ycv = p.emit("gpsimd", lambda g, k=k: g.tensor_copy(yb[k][:, D:TPC], stage[:, :]), deps=[ld2])
            bk.read("stage", ycv)
            ld_all.append(ycv)
        stoks = []
        for f in range(NKC):
            for ts in range(NTS4):
                tsl = slice(ts * TS, (ts + 1) * TS)
                pb = [P2, P3][ts % 2]
                ykey = ("y", ts % 2)
                mmt = None
                for k in range(NKC):
                    mmt = p.emit("tensor", lambda t, k=k, f=f, tsl=tsl, pb=pb: t.matmul(
                        pb[:, :], w1b[k][:, f * 128:(f + 1) * 128], yb[k][:, tsl],
                        start=(k == 0), stop=(k == NKC - 1)),
                        deps=[wtoks[k], ld_all[k], mmt] + (bk.war(ykey) if k == 0 else []))
                lx = p.emit("sync", lambda s, f=f, tsl=tsl: s.dma_start(
                    out=xt[:, :], in_=x_h[f * 128:(f + 1) * 128, tsl]), deps=bk.war("xt") + [mmt], dma=True, ch="ldx")
                e1 = p.emit("vector", lambda v, f=f, pb=pb: v.tensor_scalar(
                    tmp[:, :], pb[:, :], bt[:, f:f + 1], None, ALU.add),
                    deps=[mmt, ctoks[-1]] + bk.war("tmpf"))
                bk.read(ykey, e1)
                e2 = p.emit("vector", lambda v: v.tensor_add(tmp[:, :], tmp[:, :], xt[:, :]), deps=[e1, lx])
                bk.read("xt", e2)
                dr = p.emit("vector", lambda v: v.drain(), deps=[e2])
                so = p.emit("gpsimd", lambda g, f=f, tsl=tsl: g.dma_start(
                    out=o_h[f * 128:(f + 1) * 128, tsl], in_=tmp[:, :]), deps=[dr], dma=True, ch="st")
                bk.read("tmpf", so)
                stoks.append(so)
        p.emit("gpsimd", lambda g: g.engine_nop(), deps=stoks)
        with nc.Block() as block:
            p.realize(block)
    return nc


def run_attn_post(xT, yT, I):
    nc = _get("attn_post", build_attn_post)
    base = {
        "w_out": np.ascontiguousarray(np.asarray(I["L1_attn_w_out"])),
        "b_out": np.ascontiguousarray(np.asarray(I["L1_attn_b_out"]).reshape(8, 128, 1)),
    }
    res = run_bass_kernel_spmd(nc, [dict(base,
        x=np.ascontiguousarray(xT[:, c * TPC:(c + 1) * TPC]),
        y=np.ascontiguousarray(yT[:, c * TPC:(c + 1) * TPC])) for c in range(NCORE)],
        core_ids=list(range(NCORE)))
    return np.concatenate([r["xo"] for r in res.results], axis=1)


def build_base_pre():
    TS = 512
    NTS4 = TPC // TS
    NKC = D // 128

    nc = bass.Bass("TRN2", target_bir_lowering=False)
    x_h = nc.dram_tensor("x", [D, TPC], F32, kind="ExternalInput").ap()
    cent_h = nc.dram_tensor("centT", [D, 8], F32, kind="ExternalInput").ap()
    lng_h = nc.dram_tensor("ln_g", [NKC, 128, 1], F32, kind="ExternalInput").ap()
    lnb_h = nc.dram_tensor("ln_b", [NKC, 128, 1], F32, kind="ExternalInput").ap()
    oz_h = nc.dram_tensor("o_z", [D, TPC], F32, kind="ExternalOutput").ap()
    on_h = nc.dram_tensor("o_n", [D, TPC], F32, kind="ExternalOutput").ap()
    osc_h = nc.dram_tensor("o_sc", [TPC, 8], F32, kind="ExternalOutput").ap()

    with ExitStack() as st:
        p = P(nc, st)
        bk = Banks()
        sb = lambda name, shape, dt=F32: st.enter_context(nc.sbuf_tensor(name, shape, dt))
        pt = lambda name, shape, dt=F32: st.enter_context(nc.psum_tensor(name, shape, dt))
        zt = [sb(f"zt{k}", [128, TPC], F32) for k in range(NKC)]
        nbf = [sb(f"nbf{k}", [128, TPC], BF16) for k in range(NKC)]
        xt = sb("xt", [128, TS], F32)
        xbt = sb("xbt", [128, TS], BF16)
        sqt = sb("sqt", [128, TS], BF16)
        tmp = sb("tmpf", [128, TS], F32)
        ones_bf = sb("ones_bf", [128, 1], BF16)
        ones_f = sb("ones_f", [1, 128], F32)
        mu = sb("mu", [1, TS], F32)
        rstd = sb("rstd", [1, TS], F32)
        murstd = sb("murstd", [1, TS], F32)
        srow = sb("srow", [1, TS], F32)
        srow2 = sb("srow2", [1, TS], F32)
        lngt = sb("lngt", [128, NKC], F32)
        lnbt = sb("lnbt", [128, NKC], F32)
        centb = sb("centb", [128, 64], BF16)
        centf = sb("centf", [128, 64], F32)
        sct = sb("sct", [128, 8], F32)
        P0 = pt("P0", [128, TS])
        P1 = pt("P1", [128, TS])
        P2 = pt("P2", [128, 8])
        S0 = pt("S0", [1, TS])
        S1 = pt("S1", [1, TS])
        p.emit("gpsimd", lambda g: g.memset(ones_bf[:, :], 1.0))
        p.emit("gpsimd", lambda g: g.memset(ones_f[:, :], 1.0))
        ctoks = []
        for k in range(NKC):
            ctoks.append(p.emit("sync", lambda s, k=k: s.dma_start(out=lngt[:, k:k + 1], in_=lng_h[k]), dma=True, ch="c1"))
            ctoks.append(p.emit("sync", lambda s, k=k: s.dma_start(out=lnbt[:, k:k + 1], in_=lnb_h[k]), dma=True, ch="c2"))
            ctoks.append(p.emit("sync", lambda s, k=k: s.dma_start(out=centf[:, k * 8:(k + 1) * 8], in_=cent_h[k * 128:(k + 1) * 128, :]), dma=True, ch="c3"))
        ccv = p.emit("vector", lambda v: v.tensor_copy(centb[:, :], centf[:, :]), deps=ctoks)

        def ln_pass(src_get, dst_write, eps, ts, deps0=()):
            s1tok = s2tok = None
            deps0 = list(deps0)
            for k in range(NKC):
                cb = p.emit("vector", lambda v, k=k: v.tensor_copy(xbt[:, :], src_get(k)), deps=[s1tok] + deps0)
                sq = p.emit("scalar", lambda s, k=k: s.activation(out=sqt[:, :], in_=src_get(k), func=AF.Square), deps=[s2tok] + deps0)
                s1tok = p.emit("tensor", lambda t, k=k: t.matmul(
                    S0[:, :], ones_bf[:, :], xbt[:, :], start=(k == 0), stop=(k == NKC - 1)),
                    deps=[cb, s1tok] + (bk.war("S0") if k == 0 else []))
                s2tok = p.emit("tensor", lambda t, k=k: t.matmul(
                    S1[:, :], ones_bf[:, :], sqt[:, :], start=(k == 0), stop=(k == NKC - 1)),
                    deps=[sq, s2tok] + (bk.war("S1") if k == 0 else []))
            t1 = p.emit("vector", lambda v: v.tensor_scalar_mul(mu[:, :], S0[:, :], 1.0 / D), deps=[s1tok])
            bk.read("S0", t1)
            t2 = p.emit("vector", lambda v: v.tensor_scalar_mul(srow[:, :], S1[:, :], 1.0 / D), deps=[s2tok])
            bk.read("S1", t2)
            t3 = p.emit("vector", lambda v: v.tensor_mul(srow2[:, :], mu[:, :], mu[:, :]), deps=[t1])
            t4 = p.emit("vector", lambda v: v.tensor_sub(srow[:, :], srow[:, :], srow2[:, :]), deps=[t2, t3])
            t4b = p.emit("vector", lambda v, eps=eps: v.tensor_scalar_add(srow[:, :], srow[:, :], eps), deps=[t4])
            t5 = p.emit("scalar", lambda s: s.activation(out=srow[:, :], in_=srow[:, :], func=AF.Sqrt), deps=[t4b])
            t6 = p.emit("vector", lambda v: v.reciprocal(rstd[:, :], srow[:, :]), deps=[t5])
            t7 = p.emit("vector", lambda v: v.tensor_mul(murstd[:, :], mu[:, :], rstd[:, :]), slow=True, deps=[t6])
            b1 = p.emit("tensor", lambda t: t.matmul(P0[:, :], ones_f[:, :], rstd[:, :], start=True, stop=True),
                        deps=[t6] + bk.war("P0"))
            b2 = p.emit("tensor", lambda t: t.matmul(P1[:, :], ones_f[:, :], murstd[:, :], start=True, stop=True),
                        deps=[t7] + bk.war("P1"))
            outs = []
            for k in range(NKC):
                m1 = p.emit("vector", lambda v, k=k: v.tensor_mul(tmp[:, :], src_get(k), P0[:, :]), deps=[b1] + bk.war("tmpf"))
                m2 = p.emit("vector", lambda v: v.tensor_sub(tmp[:, :], tmp[:, :], P1[:, :]), deps=[m1, b2])
                outs.append(dst_write(k, m2))
            bk.read("P0", outs[-1])
            bk.read("P1", outs[-1])
            return outs

        allst = []
        for ts in range(NTS4):
            gsl = slice(ts * TS, (ts + 1) * TS)
            lds = {}
            ldtoks = []
            for k in range(NKC):
                ldtoks.append(p.emit("sync", lambda s, k=k, gsl=gsl: s.dma_start(
                    out=zt[k][:, gsl], in_=x_h[k * 128:(k + 1) * 128, gsl]),
                    deps=bk.war(("zt", k)), dma=True, ch=f"ldx{k % 2}"))

            def src1(k, gsl=gsl, ldtoks=ldtoks):
                return zt[k][:, gsl]
            z_outs = []

            def dst1(k, m2, gsl=gsl):
                c1 = p.emit("vector", lambda v, k=k: v.tensor_copy(zt[k][:, gsl], tmp[:, :]), deps=[m2])
                bk.read("tmpf", c1)
                so = p.emit("gpsimd", lambda g, k=k: g.dma_start(
                    out=oz_h[k * 128:(k + 1) * 128, gsl], in_=zt[k][:, gsl]),
                    deps=[p.emit("vector", lambda v: v.drain(), deps=[c1])], dma=True, ch="stz")
                z_outs.append(so)
                return c1
            outs1 = ln_pass(src1, dst1, 1e-5, ts, deps0=ldtoks)

            def src2(k, gsl=gsl):
                return zt[k][:, gsl]

            n_outs = []

            def dst2(k, m2, gsl=gsl):
                m3 = p.emit("vector", lambda v, k=k: v.tensor_scalar(tmp[:, :], tmp[:, :], lngt[:, k:k + 1], None, ALU.mult), deps=[m2, ctoks[0]])
                m4 = p.emit("vector", lambda v, k=k: v.tensor_scalar(tmp[:, :], tmp[:, :], lnbt[:, k:k + 1], None, ALU.add), deps=[m3, ctoks[1]])
                c1 = p.emit("vector", lambda v, k=k: v.tensor_copy(nbf[k][:, gsl], tmp[:, :]), deps=[m4])
                bk.read("tmpf", c1)
                dr = p.emit("vector", lambda v: v.drain(), deps=[m4])
                so = p.emit("gpsimd", lambda g, k=k: g.dma_start(
                    out=on_h[k * 128:(k + 1) * 128, gsl], in_=tmp[:, :]), deps=[dr], dma=True, ch="stn")
                bk.read("tmpf", so)
                n_outs.append(so)
                return so
            outs2 = ln_pass(src2, dst2, 1e-5, ts, deps0=outs1)
            allst += z_outs + n_outs
        sc_st = []
        for tt in range(TPC // 128):
            ttl = slice(tt * 128, (tt + 1) * 128)
            mmt = None
            for k in range(NKC):
                mmt = p.emit("tensor", lambda t, k=k, ttl=ttl: t.matmul(
                    P2[:, :], nbf[k][:, ttl], centb[:, k * 8:(k + 1) * 8],
                    start=(k == 0), stop=(k == NKC - 1)),
                    deps=[ccv, mmt] + allst + (bk.war("P2") if k == 0 else []))
            cp = p.emit("vector", lambda v: v.tensor_copy(sct[:, :], P2[:, :]), deps=[mmt] + bk.war("sct"))
            bk.read("P2", cp)
            dr = p.emit("vector", lambda v: v.drain(), deps=[cp])
            so = p.emit("gpsimd", lambda g, ttl=ttl: g.dma_start(
                out=osc_h[ttl, :], in_=sct[:, :]), deps=[dr], dma=True, ch="stsc")
            bk.read("sct", so)
            sc_st.append(so)
        p.emit("gpsimd", lambda g: g.engine_nop(), deps=allst + sc_st)
        with nc.Block() as block:
            p.realize(block)
    return nc


def run_base_pre(xT, I):
    nc = _get("base_pre", build_base_pre)
    base = {
        "centT": np.ascontiguousarray(np.asarray(I["L1_ffn_centroids"]).T),
        "ln_g": np.ascontiguousarray(np.asarray(I["L1_ffn_ln_g"]).reshape(8, 128, 1)),
        "ln_b": np.ascontiguousarray(np.asarray(I["L1_ffn_ln_b"]).reshape(8, 128, 1)),
    }
    res = run_bass_kernel_spmd(nc, [dict(base, x=np.ascontiguousarray(xT[:, c * TPC:(c + 1) * TPC]))
                                    for c in range(NCORE)], core_ids=list(range(NCORE)))
    zT = np.concatenate([r["o_z"] for r in res.results], axis=1)
    nT = np.concatenate([r["o_n"] for r in res.results], axis=1)
    sc = np.concatenate([r["o_sc"] for r in res.results], axis=0)
    return zT, nT, sc


def build_base_assign():
    NTOK = TOK
    CCAP = NTOK // 8
    NIT = 28

    nc = bass.Bass("TRN2", target_bir_lowering=False)
    sc_h = nc.dram_tensor("scores8", [8, 128, 128], F32, kind="ExternalInput").ap()
    tri_h = nc.dram_tensor("triu_s", [128, 128], F32, kind="ExternalInput").ap()
    oe_h = nc.dram_tensor("o_e", [128, 128], F32, kind="ExternalOutput").ap()
    os_h = nc.dram_tensor("o_slot", [128, 128], F32, kind="ExternalOutput").ap()

    with ExitStack() as st:
        p = P(nc, st)
        bk = Banks()
        sb = lambda name, shape, dt=F32: st.enter_context(nc.sbuf_tensor(name, shape, dt))
        pt = lambda name, shape, dt=F32: st.enter_context(nc.psum_tensor(name, shape, dt))
        sc = [sb(f"sc{e}", [128, 128], F32) for e in range(8)]
        tk = sb("tk", [128, 128], F32)
        sm = sb("sm", [128, 128], F32)
        mk = sb("mk", [128, 128], F32)
        m2k = sb("m2k", [128, 128], F32)
        csA = sb("csA", [128, 128], F32)
        csB = sb("csB", [128, 128], F32)
        rankt = sb("rankt", [128, 128], F32)
        e_acc = sb("e_acc", [128, 128], F32)
        s_acc = sb("s_acc", [128, 128], F32)
        triu = sb("triu", [128, 128], F32)
        lo = sb("lo", [128, 1], F32)
        hi = sb("hi", [128, 1], F32)
        mid = sb("mid", [128, 1], F32)
        rsumc = sb("rsumc", [128, 1], F32)
        cnt = sb("cnt", [1, 1], F32)
        cge = sb("cge", [1, 1], F32)
        cbc = sb("cbc", [128, 1], F32)
        tmp1 = sb("tmp1", [128, 1], F32)
        need = sb("need", [128, 1], F32)
        rmax = sb("rmax", [128, 1], F32)
        rmaxb = sb("rmaxb", [128, 1], BF16)
        rowb = sb("rowb", [1, 128], F32)
        ones_f = sb("ones_f", [1, 128], F32)
        onesc = sb("onesc", [128, 1], F32)
        idnb = sb("idnb", [128, 128], BF16)
        PS0 = pt("PS0", [1, 128])
        PS1 = pt("PS1", [128, 128])
        PTR = pt("PTR", [128, 128], BF16)

        p.emit("gpsimd", lambda g: g.memset(ones_f[:, :], 1.0))
        p.emit("gpsimd", lambda g: g.memset(onesc[:, :], 1.0))
        zm = p.emit("gpsimd", lambda g: g.memset(idnb[:, :], 0.0))
        idt = p.emit("gpsimd", lambda g: g.affine_select(
            out=idnb[:, :], in_=idnb[:, :], compare_op=ALU.not_equal, fill=1.0, base=0,
            pattern=[[-1, 128]], channel_multiplier=1), deps=[zm])
        ctoks = [idt]
        ctoks.append(p.emit("sync", lambda s: s.dma_start(out=triu[:, :], in_=tri_h[:, :]), dma=True, ch="c0"))
        for e in range(8):
            ctoks.append(p.emit("sync", lambda s, e=e: s.dma_start(out=sc[e][:, :], in_=sc_h[e]), dma=True, ch="c1"))
        z1 = p.emit("vector", lambda v: v.memset(tk[:, :], 0.0))
        z2 = p.emit("vector", lambda v: v.memset(e_acc[:, :], 0.0))
        z3 = p.emit("vector", lambda v: v.memset(s_acc[:, :], 0.0))
        cur = ctoks + [z1, z2, z3]

        def count_gt(thr_tile, src_tile, deps):
            g1 = p.emit("vector", lambda v: v.tensor_scalar(mk[:, :], src_tile[:, :], thr_tile[:, 0:1], None, ALU.is_gt),
                        deps=deps + bk.war("mk"))
            g2 = p.emit("vector", lambda v: v.reduce_sum(rsumc[:, :], mk[:, :], axis=AX.X), slow=True, deps=[g1] + bk.war("rsumc"))
            g3 = p.emit("tensor", lambda t: t.matmul(PS0[0:1, 0:1], rsumc[:, :], onesc[:, :], start=True, stop=True),
                        deps=[g2] + bk.war("PS0"))
            g4 = p.emit("vector", lambda v: v.tensor_copy(cnt[:, :], PS0[0:1, 0:1]), deps=[g3] + bk.war("cnt"))
            bk.read("PS0", g4)
            bk.read("rsumc", g3)
            return g4, g1

        def bcast11(src11, dst, deps):
            b1 = p.emit("tensor", lambda t, dst=dst: t.matmul(PS1[:, 0:1], ones_f[:, :], src11[0:1, 0:1], start=True, stop=True),
                        deps=deps + bk.war("PS1"))
            b2 = p.emit("vector", lambda v, dst=dst: v.tensor_copy(dst[:, :], PS1[:, 0:1]), deps=[b1], slow=True)
            bk.read("PS1", b2)
            return b2

        def cumsum_excl(mask_tile, out_tile, deps):
            c1 = p.emit("vector", lambda v: v.tensor_copy(csA[:, :], mask_tile[:, :]), deps=deps + bk.war("csA"))
            srcb, dstb = csA, csB
            ct = c1
            for sh in (1, 2, 4, 8, 16, 32, 64):
                o1 = p.emit("vector", lambda v, sh=sh, srcb=srcb, dstb=dstb: v.tensor_copy(
                    dstb[:, 0:sh], srcb[:, 0:sh]), deps=[ct] + (bk.war("csB" if dstb is csB else "csA")))
                o2 = p.emit("vector", lambda v, sh=sh, srcb=srcb, dstb=dstb: v.tensor_add(
                    dstb[:, sh:128], srcb[:, sh:128], srcb[:, 0:128 - sh]), deps=[ct])
                ct = o2
                srcb, dstb = dstb, srcb
            c2 = p.emit("vector", lambda v, srcb=srcb: v.tensor_sub(out_tile[:, :], srcb[:, :], mask_tile[:, :]),
                        deps=[ct])
            c3 = p.emit("vector", lambda v, srcb=srcb: v.tensor_copy(rsumc[:, :], srcb[:, 127:128]),
                        deps=[ct] + bk.war("rsumc"))
            bk.read("csA", c3)
            bk.read("csB", c3)
            c4 = p.emit("tensor", lambda t: t.matmul(PS1[:, 0:1], triu[:, :], rsumc[:, :], start=True, stop=True),
                        deps=[c3, ctoks[1]] + bk.war("PS1"))
            bk.read("rsumc", c4)
            c5 = p.emit("vector", lambda v: v.tensor_copy(tmp1[:, :], PS1[:, 0:1]), deps=[c4] + bk.war("tmp1"), slow=True)
            bk.read("PS1", c5)
            c6 = p.emit("vector", lambda v: v.tensor_scalar(out_tile[:, :], out_tile[:, :], tmp1[:, 0:1], None, ALU.add),
                        deps=[c5, c2])
            bk.read("tmp1", c6)
            return c6

        for e in range(8):
            q0 = p.emit("vector", lambda v, e=e: v.scalar_tensor_tensor(
                sm[:, :], tk[:, :], -1e5, sc[e][:, :], ALU.mult, ALU.add), deps=cur + bk.war("sm"))
            q1 = p.emit("vector", lambda v: v.tensor_reduce(rmax[:, :], sm[:, :], AX.X, ALU.max, apply_absolute_value=True),
                        slow=True, deps=[q0] + bk.war("rmax"))
            q2 = p.emit("vector", lambda v: v.tensor_copy(rmaxb[:, :], rmax[:, :]), deps=[q1] + bk.war("rmaxb"))
            q3 = p.emit("tensor", lambda t: t.transpose(PTR[0:1, :], rmaxb[:, 0:1], idnb[:, :]),
                        deps=[q2, idt] + bk.war("PTR"))
            q4 = p.emit("vector", lambda v: v.tensor_copy(rowb[:, :], PTR[0:1, 0:128]), deps=[q3] + bk.war("rowb"))
            bk.read("PTR", q4)
            bk.read("rmaxb", q3)
            q5 = p.emit("vector", lambda v: v.reduce_max(cnt[:, :], rowb[:, :], axis=AX.X), slow=True, deps=[q4] + bk.war("cnt"))
            bk.read("rowb", q5)
            q6 = p.emit("vector", lambda v: v.tensor_scalar_add(cnt[:, :], cnt[:, :], 2.0), deps=[q5])
            q7 = bcast11(cnt, hi, [q6])
            q8 = p.emit("vector", lambda v: v.tensor_scalar_mul(lo[:, :], hi[:, :], -1.0), deps=[q7] + bk.war("lo"), slow=True)
            btok = [q7, q8]
            for it in range(NIT):
                i1 = p.emit("vector", lambda v: v.tensor_add(mid[:, :], lo[:, :], hi[:, :]), deps=btok + bk.war("mid"))
                i2 = p.emit("vector", lambda v: v.tensor_scalar_mul(mid[:, :], mid[:, :], 0.5), deps=[i1], slow=True)
                g4, _ = count_gt(mid, sm, [i2, q0])
                i3 = p.emit("vector", lambda v: v.tensor_scalar(cge[:, :], cnt[:, :], float(CCAP), None, ALU.is_ge),
                            deps=[g4] + bk.war("cge"))
                i4 = bcast11(cge, cbc, [i3])
                bk.read("cge", i4)
                i5 = p.emit("vector", lambda v: v.tensor_sub(tmp1[:, :], mid[:, :], lo[:, :]), deps=[i2, i4] + bk.war("tmp1"))
                i6 = p.emit("vector", lambda v: v.tensor_mul(tmp1[:, :], tmp1[:, :], cbc[:, :]), deps=[i5])
                i7 = p.emit("vector", lambda v: v.tensor_add(lo[:, :], lo[:, :], tmp1[:, :]), deps=[i6], slow=True)
                i8 = p.emit("vector", lambda v: v.tensor_sub(tmp1[:, :], mid[:, :], hi[:, :]), deps=[i7] + [i2])
                i9 = p.emit("vector", lambda v: v.tensor_scalar(cbc[:, :], cbc[:, :], -1.0, None, ALU.mult), deps=[i6])
                i9b = p.emit("vector", lambda v: v.tensor_scalar(cbc[:, :], cbc[:, :], 1.0, None, ALU.add), deps=[i9])
                i10 = p.emit("vector", lambda v: v.tensor_mul(tmp1[:, :], tmp1[:, :], cbc[:, :]), deps=[i8, i9b])
                i11 = p.emit("vector", lambda v: v.tensor_add(hi[:, :], hi[:, :], tmp1[:, :]), deps=[i10], slow=True)
                bk.read("tmp1", i11)
                bk.read("cbc", i10)
                btok = [i7, i11]
            g4, g1 = count_gt(hi, sm, btok)
            s1 = p.emit("vector", lambda v: v.tensor_copy(m2k[:, :], mk[:, :]), deps=[g1, g4] + bk.war("m2k"))
            bk.read("mk", s1)
            s2 = p.emit("vector", lambda v: v.tensor_scalar(need[:, :], cnt[:, :].rearrange("a b -> a b") if False else cnt[0:1, 0:1], -1.0, None, ALU.mult), deps=[g4] + bk.war("need")) if False else None
            s3 = p.emit("vector", lambda v: v.tensor_scalar(cge[:, :], cnt[:, :], -1.0, None, ALU.mult), deps=[g4] + bk.war("cge"))
            s4 = p.emit("vector", lambda v: v.tensor_scalar(cge[:, :], cge[:, :], float(CCAP), None, ALU.add), deps=[s3])
            s5 = bcast11(cge, need, [s4])
            bk.read("cge", s5)
            g4b, g1b = count_gt(lo, sm, [s1] + btok)
            s6 = p.emit("vector", lambda v: v.tensor_sub(mk[:, :], mk[:, :], m2k[:, :]), deps=[g1b, g4b, s1])
            s7 = cumsum_excl(mk, rankt, [s6] + bk.war("rankt"))
            s8 = p.emit("vector", lambda v: v.tensor_scalar(rankt[:, :], rankt[:, :], need[:, 0:1], None, ALU.is_lt),
                        deps=[s7, s5])
            bk.read("need", s8)
            s9 = p.emit("vector", lambda v: v.tensor_mul(mk[:, :], mk[:, :], rankt[:, :]), deps=[s8, s6])
            s10 = p.emit("vector", lambda v: v.tensor_add(m2k[:, :], m2k[:, :], mk[:, :]), deps=[s9, s1])
            bk.read("mk", s10)
            s11 = cumsum_excl(m2k, rankt, [s10] + bk.war("rankt"))
            s12 = p.emit("vector", lambda v: v.tensor_mul(rankt[:, :], rankt[:, :], m2k[:, :]), deps=[s11, s10])
            s13 = p.emit("vector", lambda v: v.tensor_add(s_acc[:, :], s_acc[:, :], rankt[:, :]), deps=[s12, z3])
            bk.read("rankt", s13)
            if e > 0:
                s14 = p.emit("vector", lambda v, e=e: v.tensor_scalar(csA[:, :], m2k[:, :], float(e), None, ALU.mult),
                             deps=[s10] + bk.war("csA"))
                s15 = p.emit("vector", lambda v: v.tensor_add(e_acc[:, :], e_acc[:, :], csA[:, :]), deps=[s14, z2])
                bk.read("csA", s15)
            else:
                s15 = z2
            s16 = p.emit("vector", lambda v: v.tensor_add(tk[:, :], tk[:, :], m2k[:, :]), deps=[s10, z1])
            bk.read("m2k", s16)
            bk.read("sm", s16)
            cur = [s13, s15, s16]
        dr = p.emit("vector", lambda v: v.drain(), deps=cur)
        st1 = p.emit("gpsimd", lambda g: g.dma_start(out=oe_h[:, :], in_=e_acc[:, :]), deps=[dr], dma=True, ch="st")
        st2 = p.emit("gpsimd", lambda g: g.dma_start(out=os_h[:, :], in_=s_acc[:, :]), deps=[dr], dma=True, ch="st")
        p.emit("gpsimd", lambda g: g.engine_nop(), deps=[st1, st2])
        with nc.Block() as block:
            p.realize(block)
    return nc


def run_base_assign(scores):
    nc = _get("base_assign", build_base_assign)
    sc8 = np.stack([np.ascontiguousarray(scores[:, e].reshape(128, 128).T) for e in range(8)])
    triu_s = np.triu(np.ones((128, 128), np.float32), 1)
    res = run_bass_kernel_spmd(nc, [{"scores8": sc8, "triu_s": triu_s}] * NCORE, core_ids=list(range(NCORE)))
    e_t = res.results[0]["o_e"].T.ravel()
    slot = res.results[0]["o_slot"].T.ravel()
    return e_t.astype(np.int64), slot.astype(np.int64)


def build_base_mlp():
    SEG = TPC
    TS = 512
    NTS = SEG // TS
    FF = 4096
    NKC = D // 128
    NFC = FF // 128
    GRP = 4

    nc = bass.Bass("TRN2", target_bir_lowering=False)
    n_h = nc.dram_tensor("n", [D, SEG], F32, kind="ExternalInput").ap()
    w1_h = nc.dram_tensor("w1", [D, FF], F32, kind="ExternalInput").ap()
    b1_h = nc.dram_tensor("b1", [NFC, 128, 1], F32, kind="ExternalInput").ap()
    w2_h = nc.dram_tensor("w2", [FF, D], F32, kind="ExternalInput").ap()
    b2_h = nc.dram_tensor("b2", [NKC, 128, 1], F32, kind="ExternalInput").ap()
    y_h = nc.dram_tensor("y", [D, SEG], F32, kind="ExternalOutput").ap()

    with ExitStack() as st:
        p = P(nc, st)
        bk = Banks()
        sb = lambda name, shape, dt=F32: st.enter_context(nc.sbuf_tensor(name, shape, dt))
        pt = lambda name, shape, dt=F32: st.enter_context(nc.psum_tensor(name, shape, dt))
        nb = [sb(f"nb{k}", [128, SEG], BF16) for k in range(NKC)]
        acc = [sb(f"acc{k}", [128, SEG], F32) for k in range(NKC)]
        w1b = [sb(f"w1b{k}", [128, FF], BF16) for k in range(NKC)]
        w2g = [sb(f"w2g{j}", [128, D], BF16) for j in range(2 * GRP)]
        stage = sb("stage", [128, 2048], F32)
        hgrp = [sb(f"h{g}", [128, SEG], BF16) for g in range(GRP)]
        b1t = sb("b1t", [128, NFC], F32)
        b2t = sb("b2t", [128, NKC], F32)
        ps_h = [pt(f"ps_h{i}", [128, TS]) for i in range(2)]
        ps_y = [pt(f"ps_y{i}", [128, TS]) for i in range(2)]

        ctoks = []
        for f in range(NFC):
            ctoks.append(p.emit("sync", lambda s, f=f: s.dma_start(out=b1t[:, f:f + 1], in_=b1_h[f]), dma=True, ch="cb1"))
        for k in range(NKC):
            ctoks.append(p.emit("sync", lambda s, k=k: s.dma_start(out=b2t[:, k:k + 1], in_=b2_h[k]), dma=True, ch="cb2"))
        ntoks = []
        for k in range(NKC):
            ntoks.append(p.emit("gpsimd", lambda g, k=k: g.dma_start(
                out=nb[k][:, :], in_=n_h[k * 128:(k + 1) * 128, :]),
                deps=ctoks[-1:], dma=True, ch=f"nst{k}"))
        wtoks = []
        for k in range(NKC):
            wtoks.append(p.emit("gpsimd", lambda g, k=k: g.dma_start(
                out=w1b[k][:, :], in_=w1_h[k * 128:(k + 1) * 128, :]),
                dma=True, ch=f"w1st{k}"))
        zacc = [p.emit("vector", lambda v, k=k: v.memset(acc[k][:, :], 0.0)) for k in range(NKC)]
        w2cv_last = wtoks[-1]
        new_tail = []
        for fg in range(NFC // GRP):
            w2toks = []
            for j in range(GRP):
                f = fg * GRP + j
                slot = (fg % 2) * GRP + j
                w2cv_last = p.emit("gpsimd", lambda g, f=f, slot=slot: g.dma_start(
                    out=w2g[slot][:, :], in_=w2_h[f * 128:(f + 1) * 128, :]),
                    deps=bk.war(("w2", slot)), dma=True, ch=f"w2st{slot}")
                w2toks.append(w2cv_last)
            htoks = {}
            for j in range(GRP):
                f = fg * GRP + j
                for ts_ in range(NTS):
                    tsl = slice(ts_ * TS, (ts_ + 1) * TS)
                    hb = ps_h[ts_ % 2]
                    hkey = ("h", ts_ % 2)
                    mmt = None
                    for k in range(NKC):
                        mmt = p.emit("tensor", lambda t, k=k, f=f, tsl=tsl, hb=hb: t.matmul(
                            hb[:, :], w1b[k][:, f * 128:(f + 1) * 128], nb[k][:, tsl],
                            start=(k == 0), stop=(k == NKC - 1)),
                            deps=[wtoks[k], ntoks[k], mmt] + (bk.war(hkey) if k == 0 else []))
                    ge = p.emit("scalar", lambda s, j=j, f=f, tsl=tsl, hb=hb: s.activation(
                        out=hgrp[j][:, tsl], in_=hb[:, :], func=AF.Gelu, bias=b1t[:, f:f + 1]),
                        deps=[mmt, ctoks[NFC - 1]] + bk.war(("hg", j, ts_)))
                    bk.read(hkey, ge)
                    htoks[(j, ts_)] = ge
            for c in range(NKC):
                for ts_ in range(NTS):
                    tsl = slice(ts_ * TS, (ts_ + 1) * TS)
                    yb = ps_y[ts_ % 2]
                    ykey = ("y", ts_ % 2)
                    mmt = None
                    for j in range(GRP):
                        slot = (fg % 2) * GRP + j
                        mmt = p.emit("tensor", lambda t, j=j, c=c, tsl=tsl, yb=yb, slot=slot: t.matmul(
                            yb[:, :], w2g[slot][:, c * 128:(c + 1) * 128], hgrp[j][:, tsl],
                            start=(j == 0), stop=(j == GRP - 1)),
                            deps=[w2toks[j], htoks[(j, ts_)], mmt] + (bk.war(ykey) if j == 0 else []))
                    fl = p.emit("vector", lambda v, c=c, tsl=tsl, yb=yb: v.tensor_add(
                        acc[c][:, tsl], acc[c][:, tsl], yb[:, :]), deps=[mmt, zacc[c]])
                    bk.read(ykey, fl)
                    for j in range(GRP):
                        bk.read(("w2", (fg % 2) * GRP + j), mmt)
                        bk.read(("hg", j, ts_), mmt)
                    new_tail.append(fl)
        stoks = []
        for c in range(NKC):
            e1 = p.emit("vector", lambda v, c=c: v.tensor_scalar(
                acc[c][:, :], acc[c][:, :], b2t[:, c:c + 1], None, ALU.add),
                deps=new_tail + [ctoks[-1]])
            dr = p.emit("vector", lambda v: v.drain(), deps=[e1])
            stoks.append(p.emit("gpsimd", lambda g, c=c: g.dma_start(
                out=y_h[c * 128:(c + 1) * 128, :], in_=acc[c][:, :]), deps=[dr], dma=True, ch="st"))
        p.emit("gpsimd", lambda g: g.engine_nop(), deps=stoks)
        with nc.Block() as block:
            p.realize(block)
    return nc


def run_base_mlp(n_gathered, I):
    nc = _get("base_mlp", build_base_mlp)
    in_maps = []
    for j in range(NCORE):
        in_maps.append({
            "n": np.ascontiguousarray(n_gathered[:, j * 2048:(j + 1) * 2048]),
            "w1": np.ascontiguousarray(np.asarray(I["L1_ffn_w1"][j])),
            "b1": np.ascontiguousarray(np.asarray(I["L1_ffn_b1"][j]).reshape(32, 128, 1)),
            "w2": np.ascontiguousarray(np.asarray(I["L1_ffn_w2"][j])),
            "b2": np.ascontiguousarray(np.asarray(I["L1_ffn_b2"][j]).reshape(8, 128, 1)),
        })
    res = run_bass_kernel_spmd(nc, in_maps, core_ids=list(range(NCORE)))
    return np.concatenate([r["y"] for r in res.results], axis=1)


def build_base_post():
    TS = 512
    NTS4 = TPC // TS
    NKC = D // 128
    nc = bass.Bass("TRN2", target_bir_lowering=False)
    x_h = nc.dram_tensor("x", [D, TPC], F32, kind="ExternalInput").ap()
    z_h = nc.dram_tensor("z", [D, TPC], F32, kind="ExternalInput").ap()
    y_h = nc.dram_tensor("y", [D, TPC], F32, kind="ExternalInput").ap()
    o_h = nc.dram_tensor("xo", [D, TPC], F32, kind="ExternalOutput").ap()
    with ExitStack() as st:
        p = P(nc, st)
        bk = Banks()
        sb = lambda name, shape, dt=F32: st.enter_context(nc.sbuf_tensor(name, shape, dt))
        a = sb("a", [128, TS], F32)
        b = sb("b", [128, TS], F32)
        c = sb("c", [128, TS], F32)
        stoks = []
        for k in range(NKC):
            for ts in range(NTS4):
                gsl = slice(ts * TS, (ts + 1) * TS)
                l1 = p.emit("sync", lambda s, k=k, gsl=gsl: s.dma_start(
                    out=a[:, :], in_=z_h[k * 128:(k + 1) * 128, gsl]), deps=bk.war("a"), dma=True, ch="l1")
                l2 = p.emit("sync", lambda s, k=k, gsl=gsl: s.dma_start(
                    out=b[:, :], in_=y_h[k * 128:(k + 1) * 128, gsl]), deps=bk.war("b"), dma=True, ch="l2")
                l3 = p.emit("sync", lambda s, k=k, gsl=gsl: s.dma_start(
                    out=c[:, :], in_=x_h[k * 128:(k + 1) * 128, gsl]), deps=bk.war("c"), dma=True, ch="l3")
                v1 = p.emit("vector", lambda v: v.tensor_add(a[:, :], a[:, :], b[:, :]), deps=[l1, l2])
                bk.read("b", v1)
                v2 = p.emit("vector", lambda v: v.scalar_tensor_tensor(
                    a[:, :], a[:, :], 0.5, c[:, :], ALU.mult, ALU.add), deps=[v1, l3])
                bk.read("c", v2)
                dr = p.emit("vector", lambda v: v.drain(), deps=[v2])
                so = p.emit("gpsimd", lambda g, k=k, gsl=gsl: g.dma_start(
                    out=o_h[k * 128:(k + 1) * 128, gsl], in_=a[:, :]), deps=[dr], dma=True, ch="st")
                bk.read("a", so)
                stoks.append(so)
        p.emit("gpsimd", lambda g: g.engine_nop(), deps=stoks)
        with nc.Block() as block:
            p.realize(block)
    return nc


def run_base_post(xT, zT, yT):
    nc = _get("base_post", build_base_post)
    res = run_bass_kernel_spmd(nc, [{
        "x": np.ascontiguousarray(xT[:, c * TPC:(c + 1) * TPC]),
        "z": np.ascontiguousarray(zT[:, c * TPC:(c + 1) * TPC]),
        "y": np.ascontiguousarray(yT[:, c * TPC:(c + 1) * TPC]),
    } for c in range(NCORE)], core_ids=list(range(NCORE)))
    return np.concatenate([r["xo"] for r in res.results], axis=1)


def run_l1_ffn(xT, I):
    zT, nT, scores = run_base_pre(xT, I)
    e_t, slot = run_base_assign(scores)
    pos = e_t * 2048 + slot
    token_at_pos = np.zeros(TOK, np.int64)
    token_at_pos[pos] = np.arange(TOK)
    n_g = np.ascontiguousarray(nT[:, token_at_pos])
    Y = run_base_mlp(n_g, I)
    yT = np.zeros_like(Y)
    yT[:, token_at_pos] = Y
    return run_base_post(xT, zT, yT)
